# version 09
# baseline (speedup 1.0000x reference)
import contextlib
import numpy as np
import concourse.bass as bass
import concourse.mybir as mybir
from concourse.bass_utils import run_bass_kernel_spmd

F32 = mybir.dt.float32
BF16 = mybir.dt.bfloat16
AF = mybir.ActivationFunctionType
ALU = mybir.AluOpType
AX = mybir.AxisListType

NT = 4096
D = 1024
HC = 8
NH = 16
HD = 64
NLM = 64
SEG = 64
TG = 512
NTG = NT // TG
NE = 32
EI = 512
HALF = 2048
SCALE_MOE = 2.5

ENGINES = ("tensor", "vector", "scalar", "gpsimd", "sync")


class Op:
    __slots__ = ("engine", "build", "deps", "is_dma", "tag", "needs_inc", "inc_val", "waits")

    def __init__(self, engine, build, deps, is_dma, tag):
        self.engine = engine
        self.build = build
        self.deps = deps
        self.is_dma = is_dma
        self.tag = tag
        self.needs_inc = False
        self.inc_val = None
        self.waits = []


class Res:
    __slots__ = ("w", "readers", "name")

    def __init__(self, name=""):
        self.w = None
        self.readers = []
        self.name = name


class Plan:
    def __init__(self):
        self.ops = []
        self.psum_mutex = Res("psum_sv_mutex")

    def barrier(self):
        last = {}
        for o in self.ops:
            last[o.engine] = o
        self._bar_deps = list(last.values())
        self._bar_seen = set()

    def emit(self, engine, build, reads=(), writes=(), extra=(), is_dma=False, tag="",
             psum=False):
        if getattr(self, "_bar_deps", None) is not None and engine not in self._bar_seen:
            extra = list(extra) + self._bar_deps
            self._bar_seen.add(engine)
            if len(self._bar_seen) == len(ENGINES):
                self._bar_deps = None
        if psum:
            writes = list(writes) + [self.psum_mutex]
        deps = []
        for r in reads:
            if r.w is not None:
                deps.append(r.w)
        for wr in writes:
            deps.extend(wr.readers)
            if wr.w is not None:
                deps.append(wr.w)
        for e in extra:
            if e is not None:
                deps.append(e)
        o = Op(engine, build, deps, is_dma, tag)
        self.ops.append(o)
        for r in reads:
            r.readers.append(o)
        for wr in writes:
            wr.w = o
            wr.readers = []
        return o

    def finalize(self, nc, stack):
        pos = {id(o): i for i, o in enumerate(self.ops)}
        for o in self.ops:
            best = {}
            for d in o.deps:
                key = (d.engine, d.is_dma)
                if key not in best or pos[id(best[key])] < pos[id(d)]:
                    best[key] = d
            o.deps = list(best.values())
        for o in self.ops:
            if o.is_dma:
                o.needs_inc = True
            for d in o.deps:
                if d.engine != o.engine or d.is_dma:
                    d.needs_inc = True
        counters = {}
        for o in self.ops:
            key = (o.engine, "dma" if o.is_dma else "c")
            if o.needs_inc:
                step = 16 if o.is_dma else 1
                counters[key] = counters.get(key, 0) + step
                o.inc_val = counters[key]
            else:
                o.inc_val = None
        for o in self.ops:
            wl = {}
            for d in o.deps:
                if d.engine == o.engine and not d.is_dma:
                    continue
                key = (d.engine, "dma" if d.is_dma else "c")
                assert d.inc_val is not None
                if wl.get(key, 0) < d.inc_val:
                    wl[key] = d.inc_val
            o.waits = sorted(wl.items())
        sems = {}
        for key in counters:
            sems[key] = stack.enter_context(nc.semaphore(f"sem_{key[0]}_{key[1]}"))
        per_eng = {e: [] for e in ENGINES}
        for o in self.ops:
            per_eng[o.engine].append(o)

        with nc.Block() as block:
            def mk_section(eng_name):
                def body(eng):
                    waited = {}
                    for o in per_eng[eng_name]:
                        for key, val in o.waits:
                            if waited.get(key, 0) < val:
                                eng.wait_ge(sems[key], val)
                                waited[key] = val
                        instr = o.build(eng)
                        if o.inc_val is not None:
                            key = (o.engine, "dma" if o.is_dma else "c")
                            step = 16 if o.is_dma else 1
                            instr.then_inc(sems[key], step)
                    if eng_name == "sync":
                        for key, final in sorted(counters.items()):
                            if waited.get(key, 0) < final:
                                eng.wait_ge(sems[key], final)
                    return None
                return body

            for ename in ENGINES:
                getattr(block, ename)(mk_section(ename))
        return sems, counters


def build_nc(dbg=False, trunc=None):
    nc = bass.Bass("TRN2", target_bir_lowering=False, detect_race_conditions=False)
    stack = contextlib.ExitStack()
    P = Plan()

    xT = nc.dram_tensor("xT", [D, NT], F32, kind="ExternalInput").ap()
    wqkv = nc.dram_tensor("wqkv", [D, 3 * D], F32, kind="ExternalInput").ap()
    wproj = nc.dram_tensor("wproj", [D, D], F32, kind="ExternalInput").ap()
    gwT = nc.dram_tensor("gwT", [D, NE], F32, kind="ExternalInput").ap()
    gbias = nc.dram_tensor("gbias", [1, NE], F32, kind="ExternalInput").ap()
    wgT = nc.dram_tensor("wgT", [NE, D, EI], F32, kind="ExternalInput").ap()
    wuT = nc.dram_tensor("wuT", [NE, D, EI], F32, kind="ExternalInput").ap()
    wdT = nc.dram_tensor("wdT", [NE, EI, D], F32, kind="ExternalInput").ap()
    shgT = nc.dram_tensor("shgT", [D, D], F32, kind="ExternalInput").ap()
    shuT = nc.dram_tensor("shuT", [D, D], F32, kind="ExternalInput").ap()
    shdT = nc.dram_tensor("shdT", [D, D], F32, kind="ExternalInput").ap()
    ident_bf = nc.dram_tensor("ident_bf", [128, 128], BF16, kind="ExternalInput").ap()
    ident_f32 = nc.dram_tensor("ident_f32", [128, 128], F32, kind="ExternalInput").ap()
    outT = nc.dram_tensor("outT", [D, NT], F32, kind="ExternalOutput").ap()

    def scratch(name, shape, dt):
        kind = "ExternalOutput" if dbg else "Internal"
        return nc.dram_tensor(name, shape, dt, kind=kind).ap()

    qk_sp = scratch("qk_sp", [16, 128, NT], BF16)
    v_sp = scratch("v_sp", [32, 128, D], BF16)
    z1_sp = scratch("z1_sp", [HC, 128, NT], F32)
    y1b_sp = scratch("y1b_sp", [HC, 128, NT], BF16)
    y1f_sp = scratch("y1f_sp", [HC, 128, NT], F32)
    if dbg:
        qlm_dbg = scratch("qlm_dbg", [128, 16 * NLM], F32)
        sv_dbg = scratch("sv_dbg", [HC, 128, NT], BF16)
        comb_dbg = scratch("comb_dbg", [2, NE, HALF], F32)
        acc_dbg = scratch("acc_dbg", [2, HC, 128, HALF], F32)
        e1_dbg = scratch("e1_dbg", [128, NT], BF16)
        e3_dbg = scratch("e3_dbg", [128, NT], BF16)
        k3vn_dbg = scratch("k3vn_dbg", [128, 128], BF16)
        K_dbg = scratch("K_dbg", [128, 128], BF16)
        M_dbg = scratch("M_dbg", [128, 128], BF16)
        V_dbg = scratch("V_dbg", [128, 128], BF16)
        rB_dbg = scratch("rB_dbg", [128, NT], F32)
        rt_dbg = scratch("rt_dbg", [8, 128, 512], F32)

    sb = lambda name, shape, dt: stack.enter_context(nc.sbuf_tensor(name, shape, dt))
    stack_bc = contextlib.ExitStack()
    sbBC = lambda name, shape, dt: stack_bc.enter_context(nc.sbuf_tensor(name, shape, dt, side="right"))
    qlm_all = sbBC("qlm_all", [128, 16 * NLM], F32)
    ones2w = sbBC("ones2w", [128, 128], BF16)
    blockmask = sbBC("blockmask", [128, 128], F32)
    i7 = sbBC("i7", [128, 128], F32)
    i15 = sbBC("i15", [128, 128], F32)
    i325 = sbBC("i325", [128, 128], F32)
    onesc_bf = sb("onesc_bf", [128, 1], BF16)
    onesc_f = sb("onesc_f", [128, 1], F32)
    ones1_f = sb("ones1_f", [1, 128], F32)
    ones1_bf = sb("ones1_bf", [1, 128], BF16)
    id_bf = sb("id_bf", [128, 128], BF16)
    id_f32 = sb("id_f32", [128, 128], F32)
    gb_b = sb("gb_b", [128, NE], F32)

    r_const = Res("const")

    def init_consts():
        def b(eng):
            return eng.memset(ones2w[:, :], 0.0)
        P.emit("vector", b, writes=[r_const], tag="c0")
        P.emit("vector", lambda e: e.memset(ones2w[0:64, 0:64], 1.0), writes=[r_const])
        P.emit("vector", lambda e: e.memset(ones2w[64:128, 64:128], 1.0), writes=[r_const])
        P.emit("vector", lambda e: e.memset(onesc_bf[:, :], 1.0), writes=[r_const])
        P.emit("vector", lambda e: e.memset(onesc_f[:, :], 1.0), writes=[r_const])
        P.emit("vector", lambda e: e.memset(ones1_f[:, :], 1.0), writes=[r_const])
        P.emit("vector", lambda e: e.memset(ones1_bf[:, :], 1.0), writes=[r_const])
        P.emit("vector", lambda e: e.memset(blockmask[:, :], 0.0), writes=[r_const])
        P.emit("vector", lambda e: e.memset(blockmask[0:64, 0:64], 1.0), writes=[r_const])
        P.emit("vector", lambda e: e.memset(blockmask[64:128, 64:128], 1.0), writes=[r_const])
        P.emit("gpsimd", lambda e: e.dma_start(id_bf[:, :], ident_bf), writes=[r_const], is_dma=True)
        P.emit("gpsimd", lambda e: e.dma_start(id_f32[:, :], ident_f32), writes=[r_const], is_dma=True)
        P.emit("vector", lambda e: e.tensor_scalar_mul(i7[:, :], id_f32[:, :], 7.0), reads=[r_const], writes=[r_const])
        P.emit("vector", lambda e: e.tensor_scalar_mul(i15[:, :], id_f32[:, :], 15.0), reads=[r_const], writes=[r_const])
        P.emit("vector", lambda e: e.tensor_scalar_mul(i325[:, :], id_f32[:, :], 3.25), reads=[r_const], writes=[r_const])
        P.emit("gpsimd", lambda e: e.dma_start(gb_row[:, :], gbias), writes=[r_const], is_dma=True)

    gb_row = sb("gb_row", [1, NE], F32)

    with contextlib.ExitStack() as pha:
        x_bf = pha.enter_context(nc.sbuf_tensor("x_bf", [128, HC * NT], BF16))
        wq_bf = pha.enter_context(nc.sbuf_tensor("wq_bf", [128, HC * 3 * D], BF16))
        stage_qk = [pha.enter_context(nc.sbuf_tensor(f"stage_qk{i}", [128, NT], BF16)) for i in range(2)]
        stage_v = [pha.enter_context(nc.sbuf_tensor(f"stage_v{i}", [128, D], BF16)) for i in range(2)]
        qk_banks = [pha.enter_context(nc.psum_tensor(f"qkb{i}", [128, TG], F32)) for i in range(4)]
        v_banks = [pha.enter_context(nc.psum_tensor(f"vb{i}", [128, D], F32)) for i in range(2)]

        r_xbf = Res("xbf")
        r_wqbf = Res("wqbf")
        r_stage_qk = [Res(f"stqk{i}") for i in range(2)]
        r_stage_v = [Res(f"stv{i}") for i in range(2)]
        r_qkbank = [Res(f"qkb{i}") for i in range(4)]
        r_vbank = [Res(f"vb{i}") for i in range(2)]
        r_qlm = Res("qlm")
        r_qk_sp = [Res(f"qksp{m}") for m in range(16)]
        r_v_sp = [Res(f"vsp{t}") for t in range(32)]

        init_consts()

        P.emit("gpsimd",
               lambda e: e.dma_start(x_bf[:, :].rearrange("p (c t) -> p c t", c=HC),
                                     xT.rearrange("(c p) t -> p c t", p=128)),
               writes=[r_xbf], is_dma=True, tag="dma_x")
        P.emit("gpsimd",
               lambda e: e.dma_start(wq_bf[:, :].rearrange("p (c o) -> p c o", c=HC),
                                     wqkv.rearrange("(c p) o -> p c o", p=128)),
               writes=[r_wqbf], is_dma=True, tag="dma_wqkv")

        for m in range(16):
            st = r_stage_qk[m % 2]
            stt = stage_qk[m % 2]
            for tg in range(NTG):
                bk = r_qkbank[(m * NTG + tg) % 4]
                bkt = qk_banks[(m * NTG + tg) % 4]
                for h in range(HC):
                    def mm(e, h=h, m=m, tg=tg, bkt=bkt):
                        return e.matmul(bkt[:, :],
                                        wq_bf[:, h * 3 * D + m * 128:h * 3 * D + (m + 1) * 128],
                                        x_bf[:, h * NT + tg * TG:h * NT + (tg + 1) * TG],
                                        start=(h == 0), stop=(h == HC - 1))
                    P.emit("tensor", mm,
                           reads=[r_xbf, r_wqbf],
                           writes=[bk] if h in (0, HC - 1) else [], tag=f"qk_mm{m}_{tg}_{h}")
                sc = 0.125 if m < 8 else 1.0
                def ev(e, bkt=bkt, stt=stt, tg=tg, sc=sc):
                    return e.activation(stt[:, tg * TG:(tg + 1) * TG], bkt[:, :], AF.Copy, scale=sc)
                P.emit("scalar", ev, reads=[bk], writes=[st], tag=f"qk_ev{m}_{tg}", psum=True)
                def lm(e, bkt=bkt, m=m, tg=tg):
                    return e.reduce_sum(qlm_all[:, m * NLM + tg * 8:m * NLM + tg * 8 + 8],
                                        bkt[:, :].rearrange("p (s j) -> p s j", j=SEG), axis=AX.X)
                P.emit("vector", lm, reads=[bk], writes=[r_qlm], tag=f"qk_lm{m}_{tg}", psum=True)
            def sp(e, stt=stt, m=m):
                return e.dma_start(qk_sp[m, :, :], stt[:, :])
            P.emit("sync", sp, reads=[st], writes=[r_qk_sp[m]], is_dma=True, tag=f"qk_sp{m}")

        for tc in range(32):
            bk = r_vbank[tc % 2]
            bkt = v_banks[tc % 2]
            for h in range(HC):
                for half in range(2):
                    def mmv(e, h=h, tc=tc, half=half, bkt=bkt):
                        return e.matmul(bkt[:, half * TG:(half + 1) * TG],
                                        x_bf[:, h * NT + tc * 128:h * NT + (tc + 1) * 128],
                                        wq_bf[:, h * 3 * D + 2 * D + half * TG:h * 3 * D + 2 * D + (half + 1) * TG],
                                        start=(h == 0), stop=(h == HC - 1))
                    P.emit("tensor", mmv,
                           reads=[r_xbf, r_wqbf],
                           writes=[bk] if ((h == 0 and half == 0) or (h == HC - 1 and half == 1)) else [], tag=f"v_mm{tc}_{h}_{half}")
            def evv(e, bkt=bkt, tc=tc):
                return e.activation(stage_v[tc % 2][:, :], bkt[:, :], AF.Copy)
            P.emit("scalar", evv, reads=[bk], writes=[r_stage_v[tc % 2]], tag=f"v_ev{tc}", psum=True)
            def spv(e, tc=tc):
                return e.dma_start(v_sp[tc, :, :], stage_v[tc % 2][:, :])
            P.emit("sync", spv, reads=[r_stage_v[tc % 2]], writes=[r_v_sp[tc]], is_dma=True, tag=f"v_sp{tc}")

        if dbg:
            P.emit("sync", lambda e: e.dma_start(qlm_dbg, qlm_all[:, :]), reads=[r_qlm], is_dma=True)

    P.barrier()
    sv_bf = stack_bc.enter_context(nc.sbuf_tensor("sv_bf", [128, HC * NT], BF16, side="right"))
    r_sv = [Res(f"sv{p}") for p in range(HC)]
    with contextlib.ExitStack() as phb:
        sbb = lambda name, shape, dt: phb.enter_context(nc.sbuf_tensor(name, shape, dt))
        q_pair = [sbb(f"q_pair{i}", [128, NT], BF16) for i in range(2)]
        k_pair = [sbb("k_pair0", [128, NT], BF16)]
        v_pair = sbb("v_pair", [128, NT], BF16)
        expk3 = sbb("expk3", [128, NT], BF16)
        expk1 = [sbb(f"expk1_{i}", [128, NT], BF16) for i in range(4)]
        rb_tmp = [sbb(f"rb_tmp{i}", [128, TG], F32) for i in range(2)]
        rb_rec = [sbb(f"rb_rec{i}", [128, TG], F32) for i in range(2)]
        recipB = [sbb(f"recipB{i}", [128, NT], BF16) for i in range(4)]
        qlm_bf = [sbb(f"qlm_bf{i}", [128, NLM], BF16) for i in range(2)]
        klm_bf = [sbb(f"klm_bf{i}", [128, NLM], BF16) for i in range(2)]
        qlm_bd4 = [sbb(f"qlm_bd4_{g}", [128, 512], BF16) for g in range(2)]
        klm_bd4 = [sbb(f"klm_bd4_{g}", [128, 512], BF16) for g in range(2)]
        k3vn4 = [sbb(f"k3vn4_{g}", [128, 512], BF16) for g in range(2)]
        recip3 = [sbb(f"recip3_{i}", [128, 1], F32) for i in range(2)]
        kexp4 = sbb("kexp4", [128, 512], F32)
        K4 = sbb("K4", [128, 512], BF16)
        KT4 = sbb("KT4", [128, 512], BF16)
        KV4 = sbb("KV4", [128, 512], BF16)
        KVT4 = sbb("KVT4", [128, 512], BF16)
        C14 = sbb("C14", [128, 512], BF16)
        C1T4 = sbb("C1T4", [128, 512], BF16)
        C24 = sbb("C24", [128, 512], BF16)
        C2T4 = sbb("C2T4", [128, 512], BF16)
        C34 = sbb("C34", [128, 512], BF16)
        C3T4 = sbb("C3T4", [128, 512], BF16)
        Vb4 = [sbb(f"Vb4_{i}", [128, 512], BF16) for i in range(2)]
        VTb4 = [sbb(f"VTb4_{i}", [128, 512], BF16) for i in range(2)]
        M4 = sbb("M4", [128, 512], BF16)
        s2sum4 = sbb("s2sum4", [128, 4], F32)
        s2rec4 = sbb("s2rec4", [128, 4], F32)
        dvec4 = sbb("dvec4", [128, 4], F32)
        drec4 = sbb("drec4", [128, 4], F32)
        vjunk = sbb("vjunk", [128, 4], F32)

        pb_big = [phb.enter_context(nc.psum_tensor(f"pbig{i}", [128, TG], F32)) for i in range(2)]
        pb_s3 = [phb.enter_context(nc.psum_tensor(f"ps3{i}", [128, TG], F32)) for i in range(2)]
        pb_k3v = phb.enter_context(nc.psum_tensor("pk3v", [128, 128], F32))
        pb_ssum = phb.enter_context(nc.psum_tensor("pssum", [128, 1], F32))
        pb_inv4 = phb.enter_context(nc.psum_tensor("pinv4", [128, 512], F32))
        pb_tr4 = phb.enter_context(nc.psum_tensor("ptr4", [128, 1024], BF16))

        r_qp = [Res(f"qp{i}") for i in range(2)]
        r_kp = [Res("kp0")]
        r_vp = Res("vp")
        r_e3 = Res("e3")
        r_e1 = [Res(f"e1_{i}") for i in range(4)]
        r_rbt = [Res(f"rbt{i}") for i in range(2)]
        r_rbr = [Res(f"rbr{i}") for i in range(2)]
        r_rB = [Res(f"rB{i}") for i in range(4)]
        r_lmb = [Res(f"lmb{i}") for i in range(2)]
        r_bd = [Res(f"bd{g}") for g in range(2)]
        r_k3vn = [Res(f"k3vn{g}") for g in range(2)]
        r_r3 = [Res(f"r3_{i}") for i in range(2)]
        r_big = [Res(f"big{i}") for i in range(2)]
        r_s3 = [Res(f"s3{i}") for i in range(2)]
        r_k3vb = Res("k3vb"); r_ssum = Res("ssum"); r_inv4 = Res("inv4"); r_tr4 = Res("tr4")
        r_kexp = Res("kexp"); r_K = Res("K"); r_KT = Res("KT")
        r_KV = Res("KV"); r_KVT = Res("KVT")
        r_C = [Res(f"C{i}") for i in range(6)]
        r_V = [Res("V0"), Res("V1")]; r_VT = [Res("VT0"), Res("VT1")]
        r_M = Res("M")
        r_srow = [Res(f"srow{i}") for i in range(4)]

        for g in range(2):
            P.emit("vector", lambda e, g=g: e.memset(qlm_bd4[g][:, :], 0.0), writes=[r_bd[g]])
            P.emit("vector", lambda e, g=g: e.memset(klm_bd4[g][:, :], 0.0), writes=[r_bd[g]])
        P.emit("vector", lambda e: e.memset(K4[:, :], 0.0), writes=[r_K])

        def front_thunks(p):
            a = p % 2
            s = p % 4
            g = p // 4
            jo = 128 * s
            T = []
            T.append(lambda: P.emit("sync", lambda e: e.dma_start(q_pair[a][:, :], qk_sp[p, :, :]),
                                    reads=[r_qk_sp[p]], writes=[r_qp[a]], is_dma=True))
            T.append(lambda: P.emit("sync", lambda e: e.dma_start(k_pair[0][:, :], qk_sp[8 + p, :, :]),
                                    reads=[r_qk_sp[8 + p]], writes=[r_kp[0]], is_dma=True))
            T.append(lambda: P.emit("sync", lambda e: e.dma_start(
                v_pair[:, :].rearrange("p (c j) -> p c j", c=32),
                v_sp[:, :, p * 128:(p + 1) * 128].rearrange("c p j -> p c j")),
                                    reads=r_v_sp, writes=[r_vp], is_dma=True))

            def lm_thunk():
                P.emit("vector", lambda e: e.tensor_scalar_mul(qlm_bf[a][:, :], qlm_all[:, p * NLM:(p + 1) * NLM], 1.0 / 512.0),
                       reads=[r_qlm], writes=[r_lmb[a]])
                P.emit("vector", lambda e: e.tensor_scalar_mul(klm_bf[a][:, :], qlm_all[:, (8 + p) * NLM:(9 + p) * NLM], 1.0 / 64.0),
                       reads=[r_qlm], writes=[r_lmb[a]])
                for half in range(2):
                    sl = slice(64 * half, 64 * half + 64)
                    P.emit("vector", lambda e, sl=sl: e.tensor_copy(qlm_bd4[g][sl, jo + sl.start:jo + sl.start + 64], qlm_bf[a][sl, :]),
                           reads=[r_lmb[a]], writes=[r_bd[g]])
                    P.emit("vector", lambda e, sl=sl: e.tensor_copy(klm_bd4[g][sl, jo + sl.start:jo + sl.start + 64], klm_bf[a][sl, :]),
                           reads=[r_lmb[a]], writes=[r_bd[g]])
            T.append(lm_thunk)

            def s3_thunk(tcg):
                bk, rbk = pb_s3[tcg % 2], r_s3[tcg % 2]
                for j in range(4):
                    tc = tcg * 4 + j
                    P.emit("tensor", lambda e, tc=tc, j=j, bk=bk: e.matmul(
                        bk[:, j * 128:(j + 1) * 128], k_pair[0][:, tc * 128:(tc + 1) * 128],
                        qlm_bd4[g][:, jo:jo + 128], start=True, stop=True),
                           reads=[r_kp[0], r_bd[g]],
                           writes=[rbk] if j in (0, 3) else [], tag=f"s3_{p}_{tc}")
                P.emit("scalar", lambda e, tcg=tcg, bk=bk: e.activation(
                    expk3[:, tcg * TG:(tcg + 1) * TG], bk[:, :], AF.Exp),
                       reads=[rbk], writes=[r_e3], psum=True, tag=f"s3e{p}_{tcg}")
            for tcg in range(8):
                T.append(lambda tcg=tcg: s3_thunk(tcg))

            def k3v_thunk(gg):
                for tc in range(gg * 8, gg * 8 + 8):
                    P.emit("tensor", lambda e, tc=tc: e.matmul(
                        pb_k3v[:, :], expk3[:, tc * 128:(tc + 1) * 128], v_pair[:, tc * 128:(tc + 1) * 128],
                        start=(tc == 0), stop=(tc == 31)),
                           reads=[r_e3, r_vp],
                           writes=[r_k3vb] if tc in (0, 31) else [], tag=f"k3v{p}_{tc}")
                    P.emit("tensor", lambda e, tc=tc: e.matmul(
                        pb_ssum[:, :], expk3[:, tc * 128:(tc + 1) * 128], onesc_bf[:, :],
                        start=(tc == 0), stop=(tc == 31)),
                           reads=[r_e3, r_const],
                           writes=[r_ssum] if tc in (0, 31) else [], tag=f"s3s{p}_{tc}")
            for gg in range(4):
                T.append(lambda gg=gg: k3v_thunk(gg))

            def k3vn_thunk():
                P.emit("vector", lambda e: e.reciprocal(recip3[a][:, :], pb_ssum[:, :]),
                       reads=[r_ssum], writes=[r_r3[a]], psum=True)
                P.emit("vector", lambda e: e.tensor_copy(vjunk[:, 0:1], recip3[a][:, :]), reads=[r_r3[a]])
                P.emit("vector", lambda e: e.tensor_copy(vjunk[:, 0:1], recip3[a][:, :]), reads=[r_r3[a]])
                P.emit("vector", lambda e: e.tensor_tensor(k3vn4[g][:, jo:jo + 128], pb_k3v[:, :],
                                                           bass.AP(recip3[a], 0, [[1, 128], [0, 128]]), ALU.mult),
                       reads=[r_k3vb, r_r3[a]], writes=[r_k3vn[g]], psum=True)
            T.append(k3vn_thunk)

            def s1_thunk(tg):
                bk, rbk = pb_big[tg % 2], r_big[tg % 2]
                P.emit("tensor", lambda e, tg=tg, bk=bk: e.matmul(
                    bk[:, :], klm_bd4[g][:, jo:jo + 128], q_pair[a][:, tg * TG:(tg + 1) * TG], start=True, stop=True),
                       reads=[r_bd[g], r_qp[a], rbk], writes=[rbk], tag=f"s1_{p}_{tg}")
                P.emit("scalar", lambda e, tg=tg, bk=bk: e.activation(
                    expk1[s][:, tg * TG:(tg + 1) * TG], bk[:, :], AF.Exp),
                       reads=[rbk], writes=[r_e1[s]], psum=True)
            for tg in range(NTG):
                T.append(lambda tg=tg: s1_thunk(tg))

            def rs_thunk(tg):
                bk, rbk = pb_big[tg % 2], r_big[tg % 2]
                P.emit("tensor", lambda e, tg=tg, bk=bk: e.matmul(
                    bk[:, :], ones2w[:, :], expk1[s][:, tg * TG:(tg + 1) * TG], start=True, stop=True),
                       reads=[r_const, r_e1[s], rbk], writes=[rbk], tag=f"rs{p}_{tg}")
                P.emit("scalar", lambda e, tg=tg, bk=bk: e.activation(
                    rb_tmp[tg % 2][:, :], bk[:, :], AF.Copy),
                       reads=[rbk], writes=[r_rbt[tg % 2]], psum=True)
                P.emit("vector", lambda e, tg=tg: e.reciprocal(rb_rec[tg % 2][:, :], rb_tmp[tg % 2][:, :]),
                       reads=[r_rbt[tg % 2]], writes=[r_rbr[tg % 2]])
                if tg > 0:
                    round_rb(tg - 1)
                if tg == NTG - 1:
                    round_rb(tg)

            def round_rb(tg):
                def b(e, tg=tg):
                    with nc.allow_low_precision(reason="recipB is a bf16 matmul-side scale"):
                        return e.tensor_copy(recipB[s][:, tg * TG:(tg + 1) * TG], rb_rec[tg % 2][:, :])
                P.emit("vector", b, reads=[r_rbr[tg % 2]], writes=[r_rB[s]])
            for tg in range(NTG):
                T.append(lambda tg=tg: rs_thunk(tg))
            return T

        def mid_thunks(g):
            T = []

            def s2_thunk():
                for j in range(4):
                    P.emit("tensor", lambda e, j=j: e.matmul(
                        pb_inv4[:, j * 128:(j + 1) * 128], qlm_bd4[g][:, j * 128:(j + 1) * 128],
                        klm_bd4[g][:, j * 128:(j + 1) * 128], start=True, stop=True),
                           reads=[r_bd[g]],
                           writes=[r_inv4] if j in (0, 3) else [], tag=f"s2mm{g}_{j}")
                P.emit("scalar", lambda e: e.activation(kexp4[:, :], pb_inv4[:, :], AF.Exp),
                       reads=[r_inv4], writes=[r_kexp], psum=True)
            T.append(s2_thunk)

            def dview(t_, half, inner=64):
                return bass.AP(t_, 64 * half * (t_.shape[1] if hasattr(t_, "shape") else 512) // 1, [])

            def ksm_thunk():
                for half in range(2):
                    P.emit("vector", lambda e, half=half: e.reduce_sum(
                        s2sum4[64 * half:64 * half + 64, :],
                        bass.AP(kexp4, 64 * half * 512 // 128 * 128 + 64 * half, [[512, 64], [128, 4], [1, 64]]),
                        axis=AX.X),
                           reads=[r_kexp], writes=[r_srow[0]])
                P.emit("vector", lambda e: e.reciprocal(s2rec4[:, :], s2sum4[:, :]),
                       reads=[r_srow[0]], writes=[r_srow[1]])
                P.emit("vector", lambda e: e.tensor_copy(vjunk[:, :], s2rec4[:, :]), reads=[r_srow[1]])
                P.emit("vector", lambda e: e.tensor_copy(vjunk[:, :], s2rec4[:, :]), reads=[r_srow[1]])
                for half in range(2):
                    P.emit("vector", lambda e, half=half: e.tensor_tensor(
                        bass.AP(K4, 64 * half * 512 + 64 * half, [[512, 64], [128, 4], [1, 64]]),
                        bass.AP(kexp4, 64 * half * 512 + 64 * half, [[512, 64], [128, 4], [1, 64]]),
                        bass.AP(s2rec4, 64 * half * 4, [[4, 64], [1, 4], [0, 64]]), ALU.mult),
                           reads=[r_kexp, r_srow[1]], writes=[r_K])
            T.append(ksm_thunk)

            def kt_thunk():
                for j in range(4):
                    P.emit("tensor", lambda e, j=j: e.transpose(
                        pb_tr4[:, j * 128:(j + 1) * 128], K4[:, j * 128:(j + 1) * 128], id_bf[:, :]),
                           reads=[r_K, r_const],
                           writes=[r_tr4] if j in (0, 3) else [], tag=f"ktr{g}_{j}")
                P.emit("vector", lambda e: e.tensor_copy(KT4[:, :], pb_tr4[:, 0:512]),
                       reads=[r_tr4], writes=[r_KT], psum=True)
            T.append(kt_thunk)

            def den_thunk():
                for j in range(4):
                    P.emit("tensor", lambda e, j=j: e.matmul(
                        pb_inv4[:, j * 128:(j + 1) * 128], ones2w[:, :], K4[:, j * 128:(j + 1) * 128],
                        start=True, stop=True),
                           reads=[r_K, r_const],
                           writes=[r_inv4] if j in (0, 3) else [], tag=f"den{g}_{j}")
                for half in range(2):
                    P.emit("vector", lambda e, half=half: e.reduce_max(
                        dvec4[64 * half:64 * half + 64, :],
                        bass.AP(pb_inv4, 64 * half * 512 + 64 * half, [[512, 64], [128, 4], [1, 64]]),
                        axis=AX.X),
                           reads=[r_inv4], writes=[r_srow[2]], psum=True)
                P.emit("vector", lambda e: e.reciprocal(drec4[:, :], dvec4[:, :]),
                       reads=[r_srow[2]], writes=[r_srow[3]])
                P.emit("vector", lambda e: e.tensor_copy(vjunk[:, :], drec4[:, :]), reads=[r_srow[3]])
                P.emit("vector", lambda e: e.tensor_copy(vjunk[:, :], drec4[:, :]), reads=[r_srow[3]])
                P.emit("vector", lambda e: e.tensor_tensor(
                    Vb4[0][:, :].rearrange("p (a b) -> p a b", a=4),
                    KT4[:, :].rearrange("p (a b) -> p a b", a=4),
                    bass.AP(drec4, 0, [[4, 128], [1, 4], [0, 128]]), ALU.mult),
                       reads=[r_KT, r_srow[3]], writes=[r_V[0]])
                P.emit("vector", lambda e: e.tensor_tensor(
                    VTb4[0][:, :].rearrange("p (a b) -> p a b", a=4),
                    K4[:, :].rearrange("p (a b) -> p a b", a=4),
                    bass.AP(drec4, 0, [[4, 128], [1, 4], [0, 128]]), ALU.mult),
                       reads=[r_K, r_srow[3]], writes=[r_VT[0]])
            T.append(den_thunk)

            def mm4_evac(lhs4, rhs4, lhs_r, rhs_r, out_sb, out_r, kind, sc=None, iN=None, tagn=""):
                for j in range(4):
                    P.emit("tensor", lambda e, j=j, lhs4=lhs4, rhs4=rhs4: e.matmul(
                        pb_inv4[:, j * 128:(j + 1) * 128], lhs4[:, j * 128:(j + 1) * 128],
                        rhs4[:, j * 128:(j + 1) * 128], start=True, stop=True),
                           reads=[lhs_r, rhs_r],
                           writes=[r_inv4] if j in (0, 3) else [], tag=f"{tagn}_{j}")
                if kind == "copy":
                    P.emit("vector", lambda e, out_sb=out_sb: e.tensor_copy(out_sb[:, :], pb_inv4[:, :]),
                           reads=[r_inv4], writes=[out_r], psum=True)
                else:
                    P.emit("vector", lambda e, out_sb=out_sb, sc=sc, iN=iN: e.scalar_tensor_tensor(
                        out_sb[:, :].rearrange("p (a b) -> p a b", a=4),
                        pb_inv4[:, :].rearrange("p (a b) -> p a b", a=4), sc,
                        bass.AP(iN, 0, [[128, 128], [0, 4], [1, 128]]), ALU.mult, ALU.add),
                           reads=[r_inv4, r_const], writes=[out_r], psum=True)

            for it in range(6):
                cur, nxt = it % 2, (it + 1) % 2
                def iter_thunkA(cur=cur, it=it):
                    mm4_evac(KT4, Vb4[cur], r_KT, r_V[cur], KV4, r_KV, "copy", tagn=f"kv{g}_{it}")
                    mm4_evac(Vb4[cur], KT4, r_V[cur], r_KT, KVT4, r_KVT, "copy", tagn=f"kvt{g}_{it}")
                    P.emit("vector", lambda e: e.scalar_tensor_tensor(
                        C14[:, :].rearrange("p (a b) -> p a b", a=4),
                        KV4[:, :].rearrange("p (a b) -> p a b", a=4), -1.0,
                        bass.AP(i7, 0, [[128, 128], [0, 4], [1, 128]]), ALU.mult, ALU.add),
                           reads=[r_KV, r_const], writes=[r_C[0]])
                    P.emit("vector", lambda e: e.scalar_tensor_tensor(
                        C1T4[:, :].rearrange("p (a b) -> p a b", a=4),
                        KVT4[:, :].rearrange("p (a b) -> p a b", a=4), -1.0,
                        bass.AP(i7, 0, [[128, 128], [0, 4], [1, 128]]), ALU.mult, ALU.add),
                           reads=[r_KVT, r_const], writes=[r_C[1]])
                def iter_thunkB(it=it):
                    mm4_evac(KVT4, C14, r_KVT, r_C[0], C24, r_C[2], "stt", sc=-1.0, iN=i15, tagn=f"b{g}_{it}")
                    mm4_evac(C14, KVT4, r_C[0], r_KVT, C2T4, r_C[3], "stt", sc=-1.0, iN=i15, tagn=f"bt{g}_{it}")
                def iter_thunkC(it=it):
                    mm4_evac(KVT4, C24, r_KVT, r_C[2], C34, r_C[4], "stt", sc=-0.25, iN=i325, tagn=f"d{g}_{it}")
                    mm4_evac(C2T4, KVT4, r_C[3], r_KVT, C3T4, r_C[5], "stt", sc=-0.25, iN=i325, tagn=f"dt{g}_{it}")
                def iter_thunkD(cur=cur, nxt=nxt, it=it):
                    mm4_evac(VTb4[cur], C34, r_VT[cur], r_C[4], Vb4[nxt], r_V[nxt], "copy", tagn=f"vn{g}_{it}")
                    mm4_evac(C34, VTb4[cur], r_C[4], r_VT[cur], VTb4[nxt], r_VT[nxt], "copy", tagn=f"vnt{g}_{it}")
                T.extend([iter_thunkA, iter_thunkB, iter_thunkC, iter_thunkD])
            fin = 6 % 2

            def m_thunk():
                for j in range(4):
                    P.emit("tensor", lambda e, j=j, fin=fin: e.matmul(
                        pb_inv4[:, j * 128:(j + 1) * 128], VTb4[fin][:, j * 128:(j + 1) * 128],
                        k3vn4[g][:, j * 128:(j + 1) * 128], start=True, stop=True),
                           reads=[r_VT[fin], r_k3vn[g]],
                           writes=[r_inv4] if j in (0, 3) else [], tag=f"Mmm{g}_{j}")
                P.emit("vector", lambda e: e.tensor_tensor(
                    M4[:, :].rearrange("p (a b) -> p a b", a=4),
                    pb_inv4[:, :].rearrange("p (a b) -> p a b", a=4),
                    bass.AP(blockmask, 0, [[128, 128], [0, 4], [1, 128]]), ALU.mult),
                       reads=[r_inv4, r_const], writes=[r_M], psum=True)
            T.append(m_thunk)
            return T

        def back_thunks(p):
            s = p % 4
            jo = 128 * s
            T = []
            def svu_thunk(tg):
                bk, rbk = pb_big[tg % 2], r_big[tg % 2]
                P.emit("tensor", lambda e, tg=tg, bk=bk: e.matmul(
                    bk[:, :], M4[:, jo:jo + 128], expk1[s][:, tg * TG:(tg + 1) * TG], start=True, stop=True),
                       reads=[r_M, r_e1[s], rbk], writes=[rbk], tag=f"svu{p}_{tg}")
                P.emit("vector", lambda e, tg=tg, bk=bk: e.tensor_tensor(
                    sv_bf[:, p * NT + tg * TG:p * NT + (tg + 1) * TG], bk[:, :],
                    recipB[s][:, tg * TG:(tg + 1) * TG], ALU.mult),
                       reads=[rbk, r_rB[s]], writes=[r_sv[p]], psum=True)
            for tg in range(NTG):
                T.append(lambda tg=tg: svu_thunk(tg))
            if dbg and p == 0:
                def dbg_thunk():
                    P.emit("sync", lambda e: e.dma_start(e1_dbg, expk1[0][:, :]), reads=[r_e1[0]], is_dma=True)
                    P.emit("sync", lambda e: e.dma_start(k3vn_dbg, k3vn4[0][:, 0:128]), reads=[r_k3vn[0]], is_dma=True)
                    P.emit("sync", lambda e: e.dma_start(K_dbg, K4[:, 0:128]), reads=[r_K], is_dma=True)
                    P.emit("sync", lambda e: e.dma_start(M_dbg, M4[:, 0:128]), reads=[r_M], is_dma=True)

                T.append(dbg_thunk)
            return T

        def zip_emit(main, filler):
            fi = 0
            for mi, mth in enumerate(main):
                mth()
                while fi * len(main) < (mi + 1) * len(filler):
                    filler[fi]()
                    fi += 1
            while fi < len(filler):
                filler[fi]()
                fi += 1

        for p in range(4):
            for th in front_thunks(p):
                th()
        fl = []
        for p in range(4, 8):
            fl.extend(front_thunks(p))
        ml = mid_thunks(0)
        for p in range(4):
            ml.extend(back_thunks(p))
        zip_emit(ml, fl)
        zip_emit(mid_thunks(1), [])
        for p in range(4, 8):
            for th in back_thunks(p):
                th()

        if dbg:
            for p in range(HC):
                P.emit("sync", lambda e, p=p: e.dma_start(sv_dbg[p, :, :], sv_bf[:, p * NT:(p + 1) * NT]),
                       reads=[r_sv[p]], is_dma=True)

    P.barrier()
    with contextlib.ExitStack() as phc:
        sbc = lambda name, shape, dt: phc.enter_context(nc.sbuf_tensor(name, shape, dt))
        wp_bf = sbc("wp_bf", [128, HC * D], BF16)
        x_res = sbc("x_res", [128, NT], F32)
        z1_buf = sbc("z1_buf", [128, NT], F32)
        z1b_buf = sbc("z1b_buf", [128, TG], BF16)
        sq_buf = sbc("sq_buf", [128, TG], BF16)
        y1f_buf = sbc("y1f_buf", [128, HALF], F32)
        y1b_buf = sbc("y1b_buf", [128, HALF], BF16)
        mrow = sbc("mrow", [1, NT], F32)
        srow = sbc("srow", [1, NT], F32)
        arow = sbc("arow", [1, NT], F32)

        pb_proj = [phc.enter_context(nc.psum_tensor(f"pproj{i}", [128, TG], F32)) for i in range(2)]
        pb_mean = phc.enter_context(nc.psum_tensor("pmean", [1, TG], F32))
        pb_sq = phc.enter_context(nc.psum_tensor("psq", [1, TG], F32))
        pb_a = phc.enter_context(nc.psum_tensor("pa", [128, TG], F32))
        pb_b = phc.enter_context(nc.psum_tensor("pb", [128, TG], F32))

        r_wp = Res("wp"); r_xres = Res("xres"); r_z1 = Res("z1"); r_z1b = Res("z1b")
        r_sq = Res("sq"); r_y1f = Res("y1f"); r_y1b = Res("y1b")
        r_proj = [Res(f"proj{i}") for i in range(2)]
        r_pmean = Res("pmean"); r_psq = Res("psq"); r_pa = Res("pa"); r_pb = Res("pb")
        r_rows = Res("rows")
        r_z1_sp = [Res(f"z1sp{o}") for o in range(HC)]
        r_y1f_sp = [Res(f"y1fsp{o}") for o in range(HC)]
        r_y1b_sp = [Res(f"y1bsp{o}") for o in range(HC)]

        P.emit("gpsimd", lambda e: e.dma_start(wp_bf[:, :].rearrange("p (c o) -> p c o", c=HC),
                                               wproj.rearrange("(c p) o -> p c o", p=128)),
               writes=[r_wp], is_dma=True)
        P.emit("vector", lambda e: e.memset(mrow[:, :], 0.0), writes=[r_rows])
        P.emit("vector", lambda e: e.memset(srow[:, :], 0.0), writes=[r_rows])

        for o in range(HC):
            P.emit("sync", lambda e, o=o: e.dma_start(x_res[:, :], xT[o * 128:(o + 1) * 128, :]),
                   writes=[r_xres], is_dma=True, tag=f"xr{o}")
            for tg in range(NTG):
                bk, rbk = pb_proj[tg % 2], r_proj[tg % 2]
                for c in range(HC):
                    P.emit("tensor", lambda e, o=o, tg=tg, c=c, bk=bk: e.matmul(
                        bk[:, :], wp_bf[:, c * D + o * 128:c * D + (o + 1) * 128],
                        sv_bf[:, c * NT + tg * TG:c * NT + (tg + 1) * TG],
                        start=(c == 0), stop=(c == HC - 1)),
                           reads=[r_wp, r_sv[c]],
                           writes=[rbk] if c in (0, HC - 1) else [], tag=f"pj{o}_{tg}_{c}")
                P.emit("vector", lambda e, tg=tg, bk=bk: e.scalar_tensor_tensor(
                    z1_buf[:, tg * TG:(tg + 1) * TG], bk[:, :], 1.0,
                    x_res[:, tg * TG:(tg + 1) * TG], ALU.mult, ALU.add),
                       reads=[rbk, r_xres], writes=[r_z1], psum=True, tag=f"z1_{o}_{tg}")
                P.emit("scalar", lambda e, tg=tg: e.activation(
                    sq_buf[:, :], z1_buf[:, tg * TG:(tg + 1) * TG], AF.Square),
                       reads=[r_z1], writes=[r_sq])
                P.emit("tensor", lambda e, tg=tg: e.matmul(
                    pb_mean[:, :], onesc_f[:, :], z1_buf[:, tg * TG:(tg + 1) * TG], start=True, stop=True),
                       reads=[r_const, r_z1, r_pmean], writes=[r_pmean], tag=f"mn{o}_{tg}")
                P.emit("vector", lambda e, tg=tg: e.tensor_tensor(
                    mrow[:, tg * TG:(tg + 1) * TG], pb_mean[0:1, :], mrow[:, tg * TG:(tg + 1) * TG], ALU.add),
                       reads=[r_pmean, r_rows], writes=[r_rows], psum=True)
                P.emit("tensor", lambda e, tg=tg: e.matmul(
                    pb_sq[:, :], onesc_bf[:, :], sq_buf[:, :], start=True, stop=True),
                       reads=[r_const, r_sq, r_psq], writes=[r_psq], tag=f"sq{o}_{tg}")
                P.emit("vector", lambda e, tg=tg: e.tensor_tensor(
                    srow[:, tg * TG:(tg + 1) * TG], pb_sq[0:1, :], srow[:, tg * TG:(tg + 1) * TG], ALU.add),
                       reads=[r_psq, r_rows], writes=[r_rows], psum=True)
            P.emit("sync", lambda e, o=o: e.dma_start(z1_sp[o, :, :], z1_buf[:, :]),
                   reads=[r_z1], writes=[r_z1_sp[o]], is_dma=True, tag=f"z1sp{o}")

        P.emit("vector", lambda e: e.tensor_scalar_mul(mrow[:, :], mrow[:, :], 1.0 / D), reads=[r_rows], writes=[r_rows])
        P.emit("vector", lambda e: e.tensor_mul(arow[:, :], mrow[:, :], mrow[:, :]), reads=[r_rows], writes=[r_rows])
        P.emit("vector", lambda e: e.scalar_tensor_tensor(srow[:, :], srow[:, :], 1.0 / D, arow[:, :], ALU.mult, ALU.subtract),
               reads=[r_rows], writes=[r_rows])
        P.emit("vector", lambda e: e.tensor_scalar_add(srow[:, :], srow[:, :], 1e-5), reads=[r_rows], writes=[r_rows])
        P.emit("scalar", lambda e: e.activation(arow[:, :], srow[:, :], AF.Ln),
               reads=[r_rows], writes=[r_rows])
        P.emit("scalar", lambda e: e.activation(arow[:, :], arow[:, :], AF.Exp, scale=-0.5),
               reads=[r_rows], writes=[r_rows])
        P.emit("vector", lambda e: e.scalar_tensor_tensor(srow[:, :], mrow[:, :], -1.0, arow[:, :], ALU.mult, ALU.mult),
               reads=[r_rows], writes=[r_rows])

        a_bc = phc.enter_context(nc.sbuf_tensor("a_bc", [128, HALF], F32))
        b_bc = phc.enter_context(nc.sbuf_tensor("b_bc", [128, HALF], F32))
        r_abc = Res("abc")
        for th in range(2):
            for tgl in range(4):
                tg = th * 4 + tgl
                P.emit("tensor", lambda e, tg=tg: e.matmul(
                    pb_a[:, :], ones1_f[:, :], arow[:, tg * TG:(tg + 1) * TG], start=True, stop=True),
                       reads=[r_const, r_rows, r_pa], writes=[r_pa], tag=f"ba{tg}")
                P.emit("tensor", lambda e, tg=tg: e.matmul(
                    pb_b[:, :], ones1_f[:, :], srow[:, tg * TG:(tg + 1) * TG], start=True, stop=True),
                       reads=[r_const, r_rows, r_pb], writes=[r_pb], tag=f"bb{tg}")
                P.emit("scalar", lambda e, tgl=tgl: e.activation(a_bc[:, tgl * TG:(tgl + 1) * TG], pb_a[:, :], AF.Copy),
                       reads=[r_pa], writes=[r_abc], psum=True)
                P.emit("scalar", lambda e, tgl=tgl: e.activation(b_bc[:, tgl * TG:(tgl + 1) * TG], pb_b[:, :], AF.Copy),
                       reads=[r_pb], writes=[r_abc], psum=True)
            for o in range(HC):
                P.emit("sync", lambda e, o=o, th=th: e.dma_start(
                    z1_buf[:, 0:HALF], z1_sp[o, :, th * HALF:(th + 1) * HALF]),
                       reads=[r_z1_sp[o]], writes=[r_z1], extra=[r_z1b.w, r_sq.w], is_dma=True, tag=f"z1rl{o}_{th}")
                for tgl in range(4):
                    P.emit("vector", lambda e, tgl=tgl: e.tensor_tensor(
                        y1f_buf[:, tgl * TG:(tgl + 1) * TG], z1_buf[:, tgl * TG:(tgl + 1) * TG],
                        a_bc[:, tgl * TG:(tgl + 1) * TG], ALU.mult),
                           reads=[r_z1, r_abc], writes=[r_y1f])
                    P.emit("vector", lambda e, tgl=tgl: e.tensor_tensor(
                        y1f_buf[:, tgl * TG:(tgl + 1) * TG], y1f_buf[:, tgl * TG:(tgl + 1) * TG],
                        b_bc[:, tgl * TG:(tgl + 1) * TG], ALU.add),
                           reads=[r_abc], writes=[r_y1f])
                    P.emit("vector", lambda e, tgl=tgl: e.tensor_copy(
                        y1b_buf[:, tgl * TG:(tgl + 1) * TG], y1f_buf[:, tgl * TG:(tgl + 1) * TG]),
                           reads=[r_y1f], writes=[r_y1b])
                P.emit("sync", lambda e, o=o, th=th: e.dma_start(
                    y1f_sp[o, :, th * HALF:(th + 1) * HALF], y1f_buf[:, :]),
                       reads=[r_y1f], writes=[r_y1f_sp[o]], is_dma=True)
                P.emit("sync", lambda e, o=o, th=th: e.dma_start(
                    y1b_sp[o, :, th * HALF:(th + 1) * HALF], y1b_buf[:, :]),
                       reads=[r_y1b], writes=[r_y1b_sp[o]], is_dma=True)

    stack_bc.close()
    P.barrier()
    gb_ready = Res("gb_ready")

    def emit_half(hf):
        P.barrier()
        with contextlib.ExitStack() as phd:
            sbd = lambda name, shape, dt: phd.enter_context(nc.sbuf_tensor(f"{name}_h{hf}", shape, dt))
            y1h = sbd("y1h", [128, HC * HALF], BF16)
            accum = sbd("accum", [128, HC * HALF], F32)
            wg_bf = [sbd(f"wg_bf{i}", [128, HC * 512], BF16) for i in range(2)]
            wu_bf = [sbd(f"wu_bf{i}", [128, HC * 512], BF16) for i in range(2)]
            wd_bf = [sbd(f"wd_bf{i}", [128, 4 * D], BF16) for i in range(2)]
            c_sb = [sbd(f"c_sb{i}", [128, TG], F32) for i in range(2)]

            pb_gu = [phd.enter_context(nc.psum_tensor(f"pgu{i}_h{hf}", [128, TG], F32)) for i in range(4)]
            pb_dn = [phd.enter_context(nc.psum_tensor(f"pdn{i}_h{hf}", [128, TG], F32)) for i in range(3)]
            pb_mc = phd.enter_context(nc.psum_tensor(f"pmc_h{hf}", [128, TG], F32))
            scopeRE = phd.enter_context(contextlib.ExitStack())
            sbr = lambda name, shape, dt: scopeRE.enter_context(nc.sbuf_tensor(f"{name}_h{hf}", shape, dt))
            combine_fm = sbr("combine_fm", [NE, HALF], BF16)
            c_row = [sbr(f"c_row{i}", [1, HALF], BF16) for i in range(2)]
            guc = [sbr(f"guc{i}", [128, 4 * HALF], BF16) for i in range(2)]
            sg_t = [sbr("sg0", [128, TG], F32)]
            uc_t = [sbr("uc0", [128, TG], F32)]
            NSLOT = 1
            scopeR = contextlib.ExitStack()
            sbr2 = lambda name, shape, dt: scopeR.enter_context(nc.sbuf_tensor(f"{name}_h{hf}", shape, dt))
            gw_bf = sbr2("gw_bf", [128, HC * NE], BF16)
            logits_f = sbr2("logits_f", [32, TG], F32)
            scores_all = sbr2("scores_all", [128, 16 * NE], F32)
            scb_all = sbr2("scb_all", [128, 16 * NE], F32)
            gsc_all = sbr2("gsc_all", [128, 128], F32)
            gtmp = [sbr2(f"gtmp{i}", [128, 128], F32) for i in range(3)]
            m8g = gtmp[0]
            m8m = gtmp[1]
            masked_all = scb_all
            sel_all = scb_all
            w_all = scores_all
            comb_all = w_all
            thr8 = sbr2("thr8", [128, 16], F32)
            ws_r = sbr2("ws_r", [128, 16], F32)
            wr_r = sbr2("wr_r", [128, 16], F32)

            r_y1h = Res("y1h"); r_acc = Res("acc")
            r_guc = [Res(f"guc{i}") for i in range(2)]
            r_wg = [Res(f"wg{i}") for i in range(2)]; r_wu = [Res(f"wu{i}") for i in range(2)]
            r_wd = [Res(f"wd{i}") for i in range(2)]
            r_csb = Res("csb"); r_sg = [Res("sg0")]; r_uct = [Res("uct0")]
            r_gw = Res("gw"); r_cfm = Res("cfm"); r_rout = Res("rout")
            r_pgu = [Res(f"pgu{i}") for i in range(4)]; r_pdn = [Res(f"pdn{i}") for i in range(3)]
            r_pmc = Res("pmc")
            r_y1ft = Res("y1ft"); r_z2b = Res("z2b"); r_sq2 = Res("sq2"); r_outt = Res("outt")
            r_rows2 = Res("rows2")

            ts = slice(hf * HALF, (hf + 1) * HALF)
            for o in range(HC):
                P.emit("sync", lambda e, o=o: e.dma_start(
                    y1h[:, o * HALF:(o + 1) * HALF], y1b_sp[o, :, ts]),
                       reads=[r_y1b_sp[o]], writes=[r_y1h], is_dma=True, tag=f"y1h{o}")
            P.emit("gpsimd", lambda e: e.dma_start(gw_bf[:, :].rearrange("p (c n) -> p c n", c=HC),
                                                   gwT.rearrange("(c p) n -> p c n", p=128)),
                   writes=[r_gw], is_dma=True)
            P.emit("vector", lambda e: e.memset(accum[:, :], 0.0), writes=[r_acc])
            if hf == 0:
                P.emit("tensor", lambda e: e.matmul(pb_mc[:, 0:NE], ones1_f[:, :], gb_row[:, :], start=True, stop=True),
                       reads=[r_const, r_pmc], writes=[r_pmc], tag="gbb")
                P.emit("vector", lambda e: e.tensor_copy(gb_b[:, :], pb_mc[:, 0:NE]),
                       reads=[r_pmc], writes=[gb_ready], psum=True)
                P.emit("vector", lambda e: e.tensor_copy(gsc_all[:, 0:NE], gb_b[:, :]), reads=[gb_ready])
                P.emit("vector", lambda e: e.tensor_copy(gsc_all[:, 0:NE], gb_b[:, :]), reads=[gb_ready])

            r_lg = Res("lg")
            for tg in range(4):
                for h in range(HC):
                    P.emit("tensor", lambda e, tg=tg, h=h: e.matmul(
                        pb_mc[0:32, :], gw_bf[:, h * NE:(h + 1) * NE],
                        y1h[:, h * HALF + tg * TG:h * HALF + (tg + 1) * TG],
                        start=(h == 0), stop=(h == HC - 1)),
                           reads=[r_gw, r_y1h],
                           writes=[r_pmc] if h in (0, HC - 1) else [], tag=f"lg{tg}_{h}")
                P.emit("scalar", lambda e: e.activation(
                    logits_f[:, :], pb_mc[0:32, :], AF.Copy),
                       reads=[r_pmc], writes=[r_lg], psum=True)
                for j in range(4):
                    tc = tg * 4 + j
                    P.emit("tensor", lambda e, tc=tc, j=j: e.transpose(
                        pb_dn[2][:, tc * 32:(tc + 1) * 32], logits_f[0:32, j * 128:(j + 1) * 128], id_f32[0:32, 0:32]),
                           reads=[r_lg],
                           writes=[r_pdn[2]] if tc in (0, 15) else [], tag=f"ltr{tc}")
            P.emit("scalar", lambda e: e.activation(scores_all[:, :], pb_dn[2][:, 0:512], AF.Sigmoid),
                   reads=[r_pdn[2]], writes=[r_rout], psum=True)
            if dbg and hf == 0:
                P.emit("sync", lambda e: e.dma_start(rt_dbg[0], scores_all[:, :]), reads=[r_rout], is_dma=True)
            v16e = lambda t_: t_[:, :].rearrange("p (a b) -> p a b", a=16)
            v16g4 = lambda t_: t_[:, :].rearrange("p (a g j) -> p a g j", a=16, j=4)
            v16x8 = lambda t_: t_[:, :].rearrange("p (a g) -> p a g", a=16)
            P.emit("vector", lambda e: e.tensor_tensor(
                v16e(scb_all), v16e(scores_all),
                bass.AP(gb_b, 0, [[NE, 128], [0, 16], [1, NE]]), ALU.add),
                   reads=[r_rout, gb_ready], writes=[r_rout])
            if dbg and hf == 0:
                P.emit("sync", lambda e: e.dma_start(rt_dbg[1], scb_all[:, :]), reads=[r_rout], is_dma=True)
            def Xj(j):
                return bass.AP(scb_all, j, [[512, 128], [NE, 16], [4, 8]])
            g3 = lambda t_: t_[:, :].rearrange("p (a g) -> p a g", a=16)
            P.emit("vector", lambda e: e.tensor_tensor(g3(gtmp[1]), Xj(0), Xj(1), ALU.min), reads=[r_rout], writes=[r_rout])
            P.emit("vector", lambda e: e.tensor_tensor(g3(gtmp[2]), Xj(2), Xj(3), ALU.min), reads=[r_rout], writes=[r_rout])
            P.emit("vector", lambda e: e.tensor_tensor(gtmp[1][:, :], gtmp[1][:, :], gtmp[2][:, :], ALU.max), reads=[r_rout], writes=[r_rout])
            P.emit("vector", lambda e: e.tensor_tensor(g3(gtmp[2]), Xj(0), Xj(1), ALU.max), reads=[r_rout], writes=[r_rout])
            P.emit("vector", lambda e: e.tensor_tensor(g3(gtmp[0]), Xj(2), Xj(3), ALU.max), reads=[r_rout], writes=[r_rout])
            P.emit("vector", lambda e: e.tensor_tensor(gsc_all[:, :], gtmp[2][:, :], gtmp[0][:, :], ALU.max), reads=[r_rout], writes=[r_rout])
            P.emit("vector", lambda e: e.tensor_tensor(gtmp[2][:, :], gtmp[2][:, :], gtmp[0][:, :], ALU.min), reads=[r_rout], writes=[r_rout])
            P.emit("vector", lambda e: e.tensor_tensor(gtmp[2][:, :], gtmp[2][:, :], gtmp[1][:, :], ALU.max), reads=[r_rout], writes=[r_rout])
            P.emit("vector", lambda e: e.tensor_tensor(gsc_all[:, :], gsc_all[:, :], gtmp[2][:, :], ALU.add), reads=[r_rout], writes=[r_rout])
            for tc in range(16):
                P.emit("vector", lambda e, tc=tc: e.max(m8g[:, tc * 8:(tc + 1) * 8], gsc_all[:, tc * 8:(tc + 1) * 8]),
                       reads=[r_rout], writes=[r_rout])
            if dbg and hf == 0:
                P.emit("sync", lambda e: e.dma_start(rt_dbg[2][:, 0:128], gsc_all[:, :]), reads=[r_rout], is_dma=True)
                P.emit("sync", lambda e: e.dma_start(rt_dbg[2][:, 128:256], m8g[:, :]), reads=[r_rout], is_dma=True)
            P.emit("vector", lambda e: e.tensor_copy(gtmp[2][:, 0:1], m8g[:, 0:1]), reads=[r_rout], writes=[r_rout])
            P.emit("vector", lambda e: e.tensor_copy(gtmp[2][:, 0:1], m8g[:, 0:1]), reads=[r_rout], writes=[r_rout])
            P.emit("vector", lambda e: e.tensor_tensor(
                v16x8(gsc_all), v16x8(gsc_all), bass.AP(m8g, 3, [[128, 128], [8, 16], [0, 8]]), ALU.is_ge),
                   reads=[r_rout], writes=[r_rout])
            P.emit("vector", lambda e: e.tensor_tensor(
                v16g4(masked_all), v16g4(scb_all),
                bass.AP(gsc_all, 0, [[128, 128], [8, 16], [1, 8], [0, 4]]), ALU.mult),
                   reads=[r_rout], writes=[r_rout])
            if dbg and hf == 0:
                P.emit("sync", lambda e: e.dma_start(rt_dbg[3], masked_all[:, :]), reads=[r_rout], is_dma=True)
                P.emit("sync", lambda e: e.dma_start(rt_dbg[4][:, 0:128], gsc_all[:, :]), reads=[r_rout], is_dma=True)
            for tc in range(16):
                P.emit("vector", lambda e, tc=tc: e.max(m8m[:, tc * 8:(tc + 1) * 8], masked_all[:, tc * NE:(tc + 1) * NE]),
                       reads=[r_rout], writes=[r_rout])
            P.emit("vector", lambda e: e.tensor_copy(gtmp[2][:, 0:8], m8m[:, 120:128]), reads=[r_rout], writes=[r_rout])
            P.emit("vector", lambda e: e.tensor_copy(gtmp[2][:, 0:8], m8m[:, 120:128]), reads=[r_rout], writes=[r_rout])
            P.emit("vector", lambda e: e.tensor_scalar_max(thr8[:, :], bass.AP(m8m, 7, [[128, 128], [8, 16]]), 1e-30),
                   reads=[r_rout], writes=[r_rout])
            P.emit("vector", lambda e: e.tensor_copy(gtmp[2][:, 0:16], thr8[:, :]), reads=[r_rout], writes=[r_rout])
            P.emit("vector", lambda e: e.tensor_copy(gtmp[2][:, 0:16], thr8[:, :]), reads=[r_rout], writes=[r_rout])
            P.emit("vector", lambda e: e.tensor_tensor(
                v16e(sel_all), v16e(masked_all), bass.AP(thr8, 0, [[16, 128], [1, 16], [0, NE]]), ALU.is_ge),
                   reads=[r_rout], writes=[r_rout])
            if dbg and hf == 0:
                P.emit("sync", lambda e: e.dma_start(rt_dbg[6], sel_all[:, :]), reads=[r_rout], is_dma=True)
                P.emit("sync", lambda e: e.dma_start(rt_dbg[4][:, 128:144], thr8[:, :]), reads=[r_rout], is_dma=True)
            P.emit("vector", lambda e: e.tensor_mul(w_all[:, :], sel_all[:, :], scores_all[:, :]),
                   reads=[r_rout], writes=[r_rout])
            if dbg and hf == 0:
                P.emit("sync", lambda e: e.dma_start(rt_dbg[7], w_all[:, :]), reads=[r_rout], is_dma=True)
            P.emit("vector", lambda e: e.reduce_sum(
                ws_r[:, :], bass.AP(w_all, 0, [[512, 128], [NE, 16], [1, NE]]), axis=AX.X),
                   reads=[r_rout], writes=[r_rout])
            if dbg and hf == 0:
                P.emit("sync", lambda e: e.dma_start(rt_dbg[4][:, 176:192], ws_r[:, :]), reads=[r_rout], is_dma=True)
            P.emit("vector", lambda e: e.tensor_scalar_add(ws_r[:, :], ws_r[:, :], 1e-20), reads=[r_rout], writes=[r_rout])
            if dbg and hf == 0:
                P.emit("sync", lambda e: e.dma_start(rt_dbg[4][:, 192:208], ws_r[:, :]), reads=[r_rout], is_dma=True)
            P.emit("vector", lambda e: e.reciprocal(wr_r[:, :], ws_r[:, :]), reads=[r_rout], writes=[r_rout])
            if dbg and hf == 0:
                P.emit("sync", lambda e: e.dma_start(rt_dbg[4][:, 208:224], wr_r[:, :]), reads=[r_rout], is_dma=True)
            P.emit("vector", lambda e: e.tensor_copy(gtmp[2][:, 0:16], ws_r[:, :]), reads=[r_rout], writes=[r_rout])
            P.emit("vector", lambda e: e.tensor_copy(gtmp[2][:, 0:16], ws_r[:, :]), reads=[r_rout], writes=[r_rout])
            P.emit("vector", lambda e: e.tensor_scalar_mul(wr_r[:, :], wr_r[:, :], SCALE_MOE), reads=[r_rout], writes=[r_rout])
            if dbg and hf == 0:
                P.emit("sync", lambda e: e.dma_start(rt_dbg[4][:, 144:160], ws_r[:, :]), reads=[r_rout], is_dma=True)
                P.emit("sync", lambda e: e.dma_start(rt_dbg[4][:, 160:176], wr_r[:, :]), reads=[r_rout], is_dma=True)
            P.emit("vector", lambda e: e.tensor_copy(gtmp[1][:, 0:16], wr_r[:, :]), reads=[r_rout], writes=[r_rout])
            P.emit("vector", lambda e: e.tensor_copy(gtmp[1][:, 0:16], wr_r[:, :]), reads=[r_rout], writes=[r_rout])
            P.emit("vector", lambda e: e.tensor_tensor(
                v16e(comb_all), v16e(w_all), bass.AP(wr_r, 0, [[16, 128], [1, 16], [0, NE]]), ALU.mult),
                   reads=[r_rout], writes=[r_rout])
            if dbg and hf == 0:
                P.emit("sync", lambda e: e.dma_start(rt_dbg[5], comb_all[:, :]), reads=[r_rout], is_dma=True)
            for tcg in range(4):
                for j in range(4):
                    tc = tcg * 4 + j
                    P.emit("tensor", lambda e, tc=tc, j=j: e.transpose(
                        pb_mc[0:32, j * 128:(j + 1) * 128], comb_all[:, tc * NE:(tc + 1) * NE], id_f32[:, :]),
                           reads=[r_rout],
                           writes=[r_pmc] if j in (0, 3) else [], tag=f"ctr{tc}")
                P.emit("vector", lambda e, tcg=tcg: e.tensor_copy(
                    combine_fm[:, tcg * TG:(tcg + 1) * TG], pb_mc[0:32, :]),
                       reads=[r_pmc], writes=[r_cfm], psum=True)
            if dbg:
                P.emit("gpsimd", lambda e, hf=hf: e.dma_start(comb_dbg[hf, :, :], combine_fm[:, :]),
                       reads=[r_cfm], is_dma=True)
            scopeR.close()
            P.barrier()

            NEXP = NE + 2

            r_crow = [Res(f"crow{i}") for i in range(2)]
            r_cring = [Res(f"cring{i}") for i in range(2)]

            def emit_crow_dma(e_idx):
                if 0 <= e_idx < NE:
                    P.emit("sync", lambda e, e_idx=e_idx: e.dma_start(
                        c_row[e_idx % 2][:, :], combine_fm[e_idx:e_idx + 1, :]),
                           reads=[r_cfm], writes=[r_crow[e_idx % 2]], is_dma=True, tag=f"crow{e_idx}")

            def emit_cb_tg(e_idx, tg):
                P.emit("tensor", lambda e, e_idx=e_idx, tg=tg: e.matmul(
                    pb_mc[:, :], ones1_bf[:, :], c_row[e_idx % 2][0:1, tg * TG:(tg + 1) * TG],
                    start=True, stop=True),
                       reads=[r_const, r_crow[e_idx % 2], r_pmc], writes=[r_pmc], tag=f"cb{e_idx}_{tg}")
                P.emit("scalar", lambda e, tg=tg: e.activation(
                    c_sb[tg % 2][:, :], pb_mc[:, :], AF.Copy),
                       reads=[r_pmc], writes=[r_cring[tg % 2]], psum=True)

            def emit_wgu_dma(e_idx):
                b = e_idx % 2
                if e_idx < NE:
                    srcs = [(wgT[e_idx], wg_bf[b], r_wg[b]), (wuT[e_idx], wu_bf[b], r_wu[b])]
                else:
                    k = e_idx - NE
                    srcs = [(shgT[:, k * 512:(k + 1) * 512], wg_bf[b], r_wg[b]),
                            (shuT[:, k * 512:(k + 1) * 512], wu_bf[b], r_wu[b])]
                for src, dst, rr in srcs:
                    P.emit("gpsimd", lambda e, src=src, dst=dst: e.dma_start(
                        dst[:, :].rearrange("p (c i) -> p c i", c=HC),
                        src.rearrange("(c p) i -> p c i", p=128)),
                           writes=[rr], is_dma=True, tag=f"wgu{e_idx}")

            def emit_wd_dma(e_idx):
                if e_idx < NE:
                    wdsrc = wdT[e_idx]
                else:
                    k = e_idx - NE
                    wdsrc = shdT[k * 512:(k + 1) * 512, :]
                b = e_idx % 2
                P.emit("gpsimd", lambda e, wdsrc=wdsrc, b=b: e.dma_start(
                    wd_bf[b][:, :].rearrange("p (c o) -> p c o", c=4),
                    wdsrc.rearrange("(c p) o -> p c o", p=128)),
                       writes=[r_wd[b]], is_dma=True, tag=f"wd{e_idx}")

            def emit_gu(e_idx):
                b = e_idx % 2
                emit_crow_dma(e_idx + 1)
                shared = e_idx >= NE
                for tg in range(4):
                    if not shared:
                        emit_cb_tg(e_idx, tg)
                    for i in range(4):
                        pslot = (tg * 4 + i) % 2
                        slot = 0
                        bg, bu = pb_gu[2 * pslot], pb_gu[2 * pslot + 1]
                        rg, ru = r_pgu[2 * pslot], r_pgu[2 * pslot + 1]
                        for h in range(HC):
                            P.emit("tensor", lambda e, b=b, tg=tg, i=i, h=h, bg=bg: e.matmul(
                                bg[:, :], wg_bf[b][:, h * 512 + i * 128:h * 512 + (i + 1) * 128],
                                y1h[:, h * HALF + tg * TG:h * HALF + (tg + 1) * TG],
                                start=(h == 0), stop=(h == HC - 1)),
                                   reads=[r_wg[b], r_y1h],
                                   writes=[rg] if h in (0, HC - 1) else [], tag=f"g{e_idx}_{tg}_{i}")
                        for h in range(HC):
                            P.emit("tensor", lambda e, b=b, tg=tg, i=i, h=h, bu=bu: e.matmul(
                                bu[:, :], wu_bf[b][:, h * 512 + i * 128:h * 512 + (i + 1) * 128],
                                y1h[:, h * HALF + tg * TG:h * HALF + (tg + 1) * TG],
                                start=(h == 0), stop=(h == HC - 1)),
                                   reads=[r_wu[b], r_y1h],
                                   writes=[ru] if h in (0, HC - 1) else [], tag=f"u{e_idx}_{tg}_{i}")
                        P.emit("scalar", lambda e, slot=slot, bg=bg: e.activation(
                            sg_t[slot][:, :], bg[:, :], AF.Silu),
                               reads=[rg], writes=[r_sg[slot]], psum=True, tag=f"sg{e_idx}_{tg}_{i}")
                        if shared:
                            P.emit("vector", lambda e, slot=slot, bu=bu: e.tensor_copy(uc_t[slot][:, :], bu[:, :]),
                                   reads=[ru], writes=[r_uct[slot]], psum=True)
                        else:
                            P.emit("vector", lambda e, slot=slot, bu=bu, tg=tg: e.tensor_tensor(
                                uc_t[slot][:, :], bu[:, :], c_sb[tg % 2][:, :], ALU.mult),
                                   reads=[ru, r_cring[tg % 2]], writes=[r_uct[slot]], psum=True)
                        P.emit("vector", lambda e, slot=slot, b=b, i=i, tg=tg: e.tensor_mul(
                            guc[b][:, i * HALF + tg * TG:i * HALF + (tg + 1) * TG],
                            sg_t[slot][:, :], uc_t[slot][:, :]),
                               reads=[r_sg[slot], r_uct[slot]], writes=[r_guc[b]], tag=f"guc{e_idx}_{tg}_{i}")

            def emit_down(e_idx):
                b = e_idx % 2
                for ho in range(HC):
                    for tg in range(4):
                        slot = (ho * 4 + tg) % 3
                        bd, rd = pb_dn[slot], r_pdn[slot]
                        for i in range(4):
                            P.emit("tensor", lambda e, ho=ho, tg=tg, i=i, bd=bd, b=b: e.matmul(
                                bd[:, :], wd_bf[b][:, i * D + ho * 128:i * D + (ho + 1) * 128],
                                guc[b][:, i * HALF + tg * TG:i * HALF + (tg + 1) * TG],
                                start=(i == 0), stop=(i == 3)),
                                   reads=[r_wd[b], r_guc[b]],
                                   writes=[rd] if i in (0, 3) else [], tag=f"d{e_idx}_{ho}_{tg}_{i}")
                        P.emit("vector", lambda e, ho=ho, tg=tg, bd=bd: e.tensor_tensor(
                            accum[:, ho * HALF + tg * TG:ho * HALF + (tg + 1) * TG], bd[:, :],
                            accum[:, ho * HALF + tg * TG:ho * HALF + (tg + 1) * TG], ALU.add),
                               reads=[rd], writes=[r_acc], psum=True, tag=f"da{e_idx}_{ho}_{tg}")

            emit_wgu_dma(0)
            emit_wd_dma(0)
            emit_wgu_dma(1)
            emit_wd_dma(1)
            emit_crow_dma(0)
            emit_gu(0)
            for e_idx in range(1, NEXP):
                if e_idx + 1 < NEXP:
                    emit_wgu_dma(e_idx + 1)
                emit_gu(e_idx)
                emit_down(e_idx - 1)
                if e_idx + 1 < NEXP:
                    emit_wd_dma(e_idx + 1)
            emit_down(NEXP - 1)
            if dbg:
                for o in range(HC):
                    P.emit("sync", lambda e, hf=hf, o=o: e.dma_start(
                        acc_dbg[hf, o, :, :], accum[:, o * HALF:(o + 1) * HALF]),
                           reads=[r_acc], is_dma=True)

            scopeRE.close()
            P.barrier()
            y1f_t = sbd("y1f_t", [128, HALF], F32)
            z2b_t = sbd("z2b_t", [128, TG], BF16)
            sq2_t = sbd("sq2_t", [128, TG], BF16)
            out_t = sbd("out_t", [128, TG], F32)
            m2row = sbd("m2row", [1, HALF], F32)
            s2row = sbd("s2row", [1, HALF], F32)
            a2row = sbd("a2row", [1, HALF], F32)

            zb_t = [z2b_t, sbd("z2b_t2", [128, TG], BF16)]
            sqb_t = [sq2_t, sbd("sq2_t2", [128, TG], BF16)]
            r_zb = [r_z2b, Res("z2b2")]
            r_sqb = [r_sq2, Res("sq22")]
            for o in range(HC):
                P.emit("sync", lambda e, o=o: e.dma_start(y1f_t[:, :], y1f_sp[o, :, ts]),
                       reads=[r_y1f_sp[o]], writes=[r_y1ft], is_dma=True, tag=f"y1f{o}")
                P.emit("vector", lambda e, o=o: e.tensor_tensor(
                    accum[:, o * HALF:(o + 1) * HALF], accum[:, o * HALF:(o + 1) * HALF], y1f_t[:, :], ALU.add),
                       reads=[r_y1ft], writes=[r_acc])
                for tg in range(4):
                    sl = (o * 4 + tg) % 2
                    P.emit("vector", lambda e, o=o, tg=tg, sl=sl: e.tensor_copy(
                        zb_t[sl][:, :], accum[:, o * HALF + tg * TG:o * HALF + (tg + 1) * TG]),
                           reads=[r_acc], writes=[r_zb[sl]])
                    P.emit("scalar", lambda e, o=o, tg=tg, sl=sl: e.activation(
                        sqb_t[sl][:, :], accum[:, o * HALF + tg * TG:o * HALF + (tg + 1) * TG], AF.Square),
                           reads=[r_acc], writes=[r_sqb[sl]])
                    P.emit("tensor", lambda e, tg=tg, o=o, sl=sl: e.matmul(
                        pb_gu[tg][0:1, :], onesc_bf[:, :], zb_t[sl][:, :], start=(o == 0), stop=(o == HC - 1)),
                           reads=[r_const, r_zb[sl]],
                           writes=[r_pgu[tg]] if o in (0, HC - 1) else [], tag=f"m2_{o}_{tg}")
                    bsq = pb_dn[tg] if tg < 3 else pb_mc
                    rsq = r_pdn[tg] if tg < 3 else r_pmc
                    P.emit("tensor", lambda e, tg=tg, o=o, sl=sl, bsq=bsq: e.matmul(
                        bsq[0:1, :], onesc_bf[:, :], sqb_t[sl][:, :], start=(o == 0), stop=(o == HC - 1)),
                           reads=[r_const, r_sqb[sl]],
                           writes=[rsq] if o in (0, HC - 1) else [], tag=f"s2_{o}_{tg}")
            for tg in range(4):
                P.emit("vector", lambda e, tg=tg: e.tensor_copy(m2row[:, tg * TG:(tg + 1) * TG], pb_gu[tg][0:1, :]),
                       reads=[r_pgu[tg]], writes=[r_rows2], psum=True)
                bsq = pb_dn[tg] if tg < 3 else pb_mc
                rsq = r_pdn[tg] if tg < 3 else r_pmc
                P.emit("vector", lambda e, tg=tg, bsq=bsq: e.tensor_copy(s2row[:, tg * TG:(tg + 1) * TG], bsq[0:1, :]),
                       reads=[rsq], writes=[r_rows2], psum=True)
            P.emit("vector", lambda e: e.tensor_scalar_mul(m2row[:, :], m2row[:, :], 1.0 / D), reads=[r_rows2], writes=[r_rows2])
            P.emit("vector", lambda e: e.tensor_mul(a2row[:, :], m2row[:, :], m2row[:, :]), reads=[r_rows2], writes=[r_rows2])
            P.emit("vector", lambda e: e.scalar_tensor_tensor(s2row[:, :], s2row[:, :], 1.0 / D, a2row[:, :], ALU.mult, ALU.subtract),
                   reads=[r_rows2], writes=[r_rows2])
            P.emit("vector", lambda e: e.tensor_scalar_add(s2row[:, :], s2row[:, :], 1e-5), reads=[r_rows2], writes=[r_rows2])
            P.emit("scalar", lambda e: e.activation(a2row[:, :], s2row[:, :], AF.Ln),
                   reads=[r_rows2], writes=[r_rows2])
            P.emit("scalar", lambda e: e.activation(a2row[:, :], a2row[:, :], AF.Exp, scale=-0.5),
                   reads=[r_rows2], writes=[r_rows2])
            P.emit("vector", lambda e: e.scalar_tensor_tensor(s2row[:, :], m2row[:, :], -1.0, a2row[:, :], ALU.mult, ALU.mult),
                   reads=[r_rows2], writes=[r_rows2])
            a2_bc = sbd("a2_bc", [128, HALF], F32)
            b2_bc = sbd("b2_bc", [128, HALF], F32)
            r_abc2 = Res("abc2")
            out_ts = [out_t, sbd("out_t2", [128, TG], F32)]
            r_outs = [r_outt, Res("outt2")]
            for tg in range(4):
                P.emit("tensor", lambda e, tg=tg: e.matmul(
                    pb_gu[0][:, :], ones1_f[:, :], a2row[:, tg * TG:(tg + 1) * TG], start=True, stop=True),
                       reads=[r_const, r_rows2, r_pgu[0]], writes=[r_pgu[0]], tag=f"ba2{tg}")
                P.emit("tensor", lambda e, tg=tg: e.matmul(
                    pb_gu[1][:, :], ones1_f[:, :], s2row[:, tg * TG:(tg + 1) * TG], start=True, stop=True),
                       reads=[r_const, r_rows2, r_pgu[1]], writes=[r_pgu[1]], tag=f"bb2{tg}")
                P.emit("scalar", lambda e, tg=tg: e.activation(a2_bc[:, tg * TG:(tg + 1) * TG], pb_gu[0][:, :], AF.Copy),
                       reads=[r_pgu[0]], writes=[r_abc2], psum=True)
                P.emit("scalar", lambda e, tg=tg: e.activation(b2_bc[:, tg * TG:(tg + 1) * TG], pb_gu[1][:, :], AF.Copy),
                       reads=[r_pgu[1]], writes=[r_abc2], psum=True)
            for o in range(HC):
                for tg in range(4):
                    sl = (o * 4 + tg) % 2
                    P.emit("vector", lambda e, o=o, tg=tg, sl=sl: e.tensor_tensor(
                        out_ts[sl][:, :], accum[:, o * HALF + tg * TG:o * HALF + (tg + 1) * TG],
                        a2_bc[:, tg * TG:(tg + 1) * TG], ALU.mult),
                           reads=[r_acc, r_abc2], writes=[r_outs[sl]])
                    P.emit("vector", lambda e, tg=tg, sl=sl: e.tensor_tensor(
                        out_ts[sl][:, :], out_ts[sl][:, :], b2_bc[:, tg * TG:(tg + 1) * TG], ALU.add),
                           reads=[r_abc2], writes=[r_outs[sl]])
                    P.emit("sync", lambda e, o=o, tg=tg, hf=hf, sl=sl: e.dma_start(
                        outT[o * 128:(o + 1) * 128, hf * HALF + tg * TG:hf * HALF + (tg + 1) * TG], out_ts[sl][:, :]),
                           reads=[r_outs[sl]], is_dma=True, tag=f"out{o}_{tg}")

    for hf in range(2):
        emit_half(hf)

    if trunc is not None:
        P.ops = P.ops[:trunc]
    P.finalize(nc, stack)
    nc._stack = stack
    return nc


def kernel(L0_attn_x, L0_attn_w_qkv, L0_attn_w_proj, L0_ffn_gate_w, L0_ffn_gate_bias,
           L0_ffn_w_gate, L0_ffn_w_up, L0_ffn_w_down, L0_ffn_sh_gate, L0_ffn_sh_up,
           L0_ffn_sh_down):
    import ml_dtypes
    x = np.asarray(L0_attn_x, dtype=np.float32)
    common = {
        "wqkv": np.ascontiguousarray(np.asarray(L0_attn_w_qkv, dtype=np.float32)),
        "wproj": np.ascontiguousarray(np.asarray(L0_attn_w_proj, dtype=np.float32)),
        "gwT": np.ascontiguousarray(np.asarray(L0_ffn_gate_w, dtype=np.float32).T),
        "gbias": np.ascontiguousarray(np.asarray(L0_ffn_gate_bias, dtype=np.float32).reshape(1, NE)),
        "wgT": np.ascontiguousarray(np.asarray(L0_ffn_w_gate, dtype=np.float32).transpose(0, 2, 1)),
        "wuT": np.ascontiguousarray(np.asarray(L0_ffn_w_up, dtype=np.float32).transpose(0, 2, 1)),
        "wdT": np.ascontiguousarray(np.asarray(L0_ffn_w_down, dtype=np.float32).transpose(0, 2, 1)),
        "shgT": np.ascontiguousarray(np.asarray(L0_ffn_sh_gate, dtype=np.float32).T),
        "shuT": np.ascontiguousarray(np.asarray(L0_ffn_sh_up, dtype=np.float32).T),
        "shdT": np.ascontiguousarray(np.asarray(L0_ffn_sh_down, dtype=np.float32).T),
        "ident_bf": np.eye(128, dtype=ml_dtypes.bfloat16),
        "ident_f32": np.eye(128, dtype=np.float32),
    }
    in_maps = [dict(common, xT=np.ascontiguousarray(x[c].T)) for c in range(8)]
    nc = build_nc()
    res = run_bass_kernel_spmd(nc, in_maps, core_ids=list(range(8)))
    out = np.stack([np.asarray(r["outT"]).T for r in res.results])
    return np.ascontiguousarray(out.astype(np.float32))


# version 10
# speedup vs baseline: 1.0651x; 1.0651x over previous
import contextlib
import numpy as np
import concourse.bass as bass
import concourse.mybir as mybir
from concourse.bass_utils import run_bass_kernel_spmd

F32 = mybir.dt.float32
BF16 = mybir.dt.bfloat16
AF = mybir.ActivationFunctionType
ALU = mybir.AluOpType
AX = mybir.AxisListType

NT = 4096
D = 1024
HC = 8
NH = 16
HD = 64
NLM = 64
SEG = 64
TG = 512
NTG = NT // TG
NE = 32
EI = 512
HALF = 2048
SCALE_MOE = 2.5

ENGINES = ("tensor", "vector", "scalar", "gpsimd", "sync")


class Op:
    __slots__ = ("engine", "build", "deps", "is_dma", "tag", "needs_inc", "inc_val", "waits")

    def __init__(self, engine, build, deps, is_dma, tag):
        self.engine = engine
        self.build = build
        self.deps = deps
        self.is_dma = is_dma
        self.tag = tag
        self.needs_inc = False
        self.inc_val = None
        self.waits = []


class Res:
    __slots__ = ("w", "readers", "name")

    def __init__(self, name=""):
        self.w = None
        self.readers = []
        self.name = name


class Plan:
    def __init__(self):
        self.ops = []
        self.psum_mutex = Res("psum_sv_mutex")

    def barrier(self):
        last = {}
        for o in self.ops:
            last[o.engine] = o
        self._bar_deps = list(last.values())
        self._bar_seen = set()

    def emit(self, engine, build, reads=(), writes=(), extra=(), is_dma=False, tag="",
             psum=False):
        if getattr(self, "_bar_deps", None) is not None and engine not in self._bar_seen:
            extra = list(extra) + self._bar_deps
            self._bar_seen.add(engine)
            if len(self._bar_seen) == len(ENGINES):
                self._bar_deps = None
        if psum:
            writes = list(writes) + [self.psum_mutex]
        deps = []
        for r in reads:
            if r.w is not None:
                deps.append(r.w)
        for wr in writes:
            deps.extend(wr.readers)
            if wr.w is not None:
                deps.append(wr.w)
        for e in extra:
            if e is not None:
                deps.append(e)
        o = Op(engine, build, deps, is_dma, tag)
        self.ops.append(o)
        for r in reads:
            r.readers.append(o)
        for wr in writes:
            wr.w = o
            wr.readers = []
        return o

    def finalize(self, nc, stack):
        pos = {id(o): i for i, o in enumerate(self.ops)}
        for o in self.ops:
            best = {}
            for d in o.deps:
                key = (d.engine, d.is_dma)
                if key not in best or pos[id(best[key])] < pos[id(d)]:
                    best[key] = d
            o.deps = list(best.values())
        for o in self.ops:
            if o.is_dma:
                o.needs_inc = True
            for d in o.deps:
                if d.engine != o.engine or d.is_dma:
                    d.needs_inc = True
        counters = {}
        for o in self.ops:
            key = (o.engine, "dma" if o.is_dma else "c")
            if o.needs_inc:
                step = 16 if o.is_dma else 1
                counters[key] = counters.get(key, 0) + step
                o.inc_val = counters[key]
            else:
                o.inc_val = None
        for o in self.ops:
            wl = {}
            for d in o.deps:
                if d.engine == o.engine and not d.is_dma:
                    continue
                key = (d.engine, "dma" if d.is_dma else "c")
                assert d.inc_val is not None
                if wl.get(key, 0) < d.inc_val:
                    wl[key] = d.inc_val
            o.waits = sorted(wl.items())
        sems = {}
        for key in counters:
            sems[key] = stack.enter_context(nc.semaphore(f"sem_{key[0]}_{key[1]}"))
        per_eng = {e: [] for e in ENGINES}
        for o in self.ops:
            per_eng[o.engine].append(o)

        with nc.Block() as block:
            def mk_section(eng_name):
                def body(eng):
                    waited = {}
                    for o in per_eng[eng_name]:
                        for key, val in o.waits:
                            if waited.get(key, 0) < val:
                                eng.wait_ge(sems[key], val)
                                waited[key] = val
                        instr = o.build(eng)
                        if o.inc_val is not None:
                            key = (o.engine, "dma" if o.is_dma else "c")
                            step = 16 if o.is_dma else 1
                            instr.then_inc(sems[key], step)
                    if eng_name == "sync":
                        for key, final in sorted(counters.items()):
                            if waited.get(key, 0) < final:
                                eng.wait_ge(sems[key], final)
                    return None
                return body

            for ename in ENGINES:
                getattr(block, ename)(mk_section(ename))
        return sems, counters


def build_nc(dbg=False, trunc=None):
    nc = bass.Bass("TRN2", target_bir_lowering=False, detect_race_conditions=False)
    stack = contextlib.ExitStack()
    P = Plan()

    xT = nc.dram_tensor("xT", [D, NT], F32, kind="ExternalInput").ap()
    wqkv = nc.dram_tensor("wqkv", [D, 3 * D], F32, kind="ExternalInput").ap()
    wproj = nc.dram_tensor("wproj", [D, D], F32, kind="ExternalInput").ap()
    gwT = nc.dram_tensor("gwT", [D, NE], F32, kind="ExternalInput").ap()
    gbias = nc.dram_tensor("gbias", [1, NE], F32, kind="ExternalInput").ap()
    wgT = nc.dram_tensor("wgT", [NE, D, EI], F32, kind="ExternalInput").ap()
    wuT = nc.dram_tensor("wuT", [NE, D, EI], F32, kind="ExternalInput").ap()
    wdT = nc.dram_tensor("wdT", [NE, EI, D], F32, kind="ExternalInput").ap()
    shgT = nc.dram_tensor("shgT", [D, D], F32, kind="ExternalInput").ap()
    shuT = nc.dram_tensor("shuT", [D, D], F32, kind="ExternalInput").ap()
    shdT = nc.dram_tensor("shdT", [D, D], F32, kind="ExternalInput").ap()
    ident_bf = nc.dram_tensor("ident_bf", [128, 128], BF16, kind="ExternalInput").ap()
    ident_f32 = nc.dram_tensor("ident_f32", [128, 128], F32, kind="ExternalInput").ap()
    outT = nc.dram_tensor("outT", [D, NT], F32, kind="ExternalOutput").ap()

    def scratch(name, shape, dt):
        kind = "ExternalOutput" if dbg else "Internal"
        return nc.dram_tensor(name, shape, dt, kind=kind).ap()

    qk_sp = scratch("qk_sp", [16, 128, NT], BF16)
    v_sp = scratch("v_sp", [32, 128, D], BF16)
    z1_sp = scratch("z1_sp", [HC, 128, NT], F32)
    y1b_sp = scratch("y1b_sp", [HC, 128, NT], BF16)
    y1f_sp = scratch("y1f_sp", [HC, 128, NT], F32)
    if dbg:
        qlm_dbg = scratch("qlm_dbg", [128, 16 * NLM], F32)
        sv_dbg = scratch("sv_dbg", [HC, 128, NT], BF16)
        comb_dbg = scratch("comb_dbg", [2, NE, HALF], F32)
        acc_dbg = scratch("acc_dbg", [2, HC, 128, HALF], F32)
        e1_dbg = scratch("e1_dbg", [128, NT], BF16)
        e3_dbg = scratch("e3_dbg", [128, NT], BF16)
        k3vn_dbg = scratch("k3vn_dbg", [128, 128], BF16)
        K_dbg = scratch("K_dbg", [128, 128], BF16)
        M_dbg = scratch("M_dbg", [128, 128], BF16)
        V_dbg = scratch("V_dbg", [128, 128], BF16)
        rB_dbg = scratch("rB_dbg", [128, NT], F32)
        rt_dbg = scratch("rt_dbg", [8, 128, 512], F32)

    sb = lambda name, shape, dt: stack.enter_context(nc.sbuf_tensor(name, shape, dt))
    stack_bc = contextlib.ExitStack()
    sbBC = lambda name, shape, dt: stack_bc.enter_context(nc.sbuf_tensor(name, shape, dt, side="right"))
    qlm_all = sbBC("qlm_all", [128, 16 * NLM], F32)
    ones2w = sbBC("ones2w", [128, 128], BF16)
    blockmask = sbBC("blockmask", [128, 128], F32)
    i7 = sbBC("i7", [128, 128], F32)
    i15 = sbBC("i15", [128, 128], F32)
    i325 = sbBC("i325", [128, 128], F32)
    onesc_bf = sb("onesc_bf", [128, 1], BF16)
    onesc_f = sb("onesc_f", [128, 1], F32)
    ones1_f = sb("ones1_f", [1, 128], F32)
    ones1_bf = sb("ones1_bf", [1, 128], BF16)
    id_bf = sb("id_bf", [128, 128], BF16)
    id_f32 = sb("id_f32", [128, 128], F32)
    gb_b = sb("gb_b", [128, NE], F32)

    r_const = Res("const")

    def init_consts():
        def b(eng):
            return eng.memset(ones2w[:, :], 0.0)
        P.emit("vector", b, writes=[r_const], tag="c0")
        P.emit("vector", lambda e: e.memset(ones2w[0:64, 0:64], 1.0), writes=[r_const])
        P.emit("vector", lambda e: e.memset(ones2w[64:128, 64:128], 1.0), writes=[r_const])
        P.emit("vector", lambda e: e.memset(onesc_bf[:, :], 1.0), writes=[r_const])
        P.emit("vector", lambda e: e.memset(onesc_f[:, :], 1.0), writes=[r_const])
        P.emit("vector", lambda e: e.memset(ones1_f[:, :], 1.0), writes=[r_const])
        P.emit("vector", lambda e: e.memset(ones1_bf[:, :], 1.0), writes=[r_const])
        P.emit("vector", lambda e: e.memset(blockmask[:, :], 0.0), writes=[r_const])
        P.emit("vector", lambda e: e.memset(blockmask[0:64, 0:64], 1.0), writes=[r_const])
        P.emit("vector", lambda e: e.memset(blockmask[64:128, 64:128], 1.0), writes=[r_const])
        P.emit("gpsimd", lambda e: e.dma_start(id_bf[:, :], ident_bf), writes=[r_const], is_dma=True)
        P.emit("gpsimd", lambda e: e.dma_start(id_f32[:, :], ident_f32), writes=[r_const], is_dma=True)
        P.emit("vector", lambda e: e.tensor_scalar_mul(i7[:, :], id_f32[:, :], 7.0), reads=[r_const], writes=[r_const])
        P.emit("vector", lambda e: e.tensor_scalar_mul(i15[:, :], id_f32[:, :], 15.0), reads=[r_const], writes=[r_const])
        P.emit("vector", lambda e: e.tensor_scalar_mul(i325[:, :], id_f32[:, :], 3.25), reads=[r_const], writes=[r_const])
        P.emit("gpsimd", lambda e: e.dma_start(gb_row[:, :], gbias), writes=[r_const], is_dma=True)

    gb_row = sb("gb_row", [1, NE], F32)

    with contextlib.ExitStack() as pha:
        x_bf = pha.enter_context(nc.sbuf_tensor("x_bf", [128, HC * NT], BF16))
        wq_bf = pha.enter_context(nc.sbuf_tensor("wq_bf", [128, HC * 3 * D], BF16))
        stage_qk = [pha.enter_context(nc.sbuf_tensor(f"stage_qk{i}", [128, NT], BF16)) for i in range(2)]
        stage_v = [pha.enter_context(nc.sbuf_tensor(f"stage_v{i}", [128, D], BF16)) for i in range(2)]
        qk_banks = [pha.enter_context(nc.psum_tensor(f"qkb{i}", [128, TG], F32)) for i in range(4)]
        v_banks = [pha.enter_context(nc.psum_tensor(f"vb{i}", [128, D], F32)) for i in range(2)]

        r_xbf = Res("xbf")
        r_wqbf = Res("wqbf")
        r_stage_qk = [Res(f"stqk{i}") for i in range(2)]
        r_stage_v = [Res(f"stv{i}") for i in range(2)]
        r_qkbank = [Res(f"qkb{i}") for i in range(4)]
        r_vbank = [Res(f"vb{i}") for i in range(2)]
        r_qlm = Res("qlm")
        r_qk_sp = [Res(f"qksp{m}") for m in range(16)]
        r_v_sp = [Res(f"vsp{t}") for t in range(32)]

        init_consts()

        P.emit("gpsimd",
               lambda e: e.dma_start(x_bf[:, :].rearrange("p (c t) -> p c t", c=HC),
                                     xT.rearrange("(c p) t -> p c t", p=128)),
               writes=[r_xbf], is_dma=True, tag="dma_x")
        P.emit("gpsimd",
               lambda e: e.dma_start(wq_bf[:, :].rearrange("p (c o) -> p c o", c=HC),
                                     wqkv.rearrange("(c p) o -> p c o", p=128)),
               writes=[r_wqbf], is_dma=True, tag="dma_wqkv")

        for m in range(16):
            st = r_stage_qk[m % 2]
            stt = stage_qk[m % 2]
            for tg in range(NTG):
                bk = r_qkbank[(m * NTG + tg) % 4]
                bkt = qk_banks[(m * NTG + tg) % 4]
                for h in range(HC):
                    def mm(e, h=h, m=m, tg=tg, bkt=bkt):
                        return e.matmul(bkt[:, :],
                                        wq_bf[:, h * 3 * D + m * 128:h * 3 * D + (m + 1) * 128],
                                        x_bf[:, h * NT + tg * TG:h * NT + (tg + 1) * TG],
                                        start=(h == 0), stop=(h == HC - 1))
                    P.emit("tensor", mm,
                           reads=[r_xbf, r_wqbf],
                           writes=[bk] if h in (0, HC - 1) else [], tag=f"qk_mm{m}_{tg}_{h}")
                sc = 0.125 if m < 8 else 1.0
                def ev(e, bkt=bkt, stt=stt, tg=tg, sc=sc):
                    return e.activation(stt[:, tg * TG:(tg + 1) * TG], bkt[:, :], AF.Copy, scale=sc)
                P.emit("scalar", ev, reads=[bk], writes=[st], tag=f"qk_ev{m}_{tg}", psum=True)
                def lm(e, bkt=bkt, m=m, tg=tg):
                    return e.reduce_sum(qlm_all[:, m * NLM + tg * 8:m * NLM + tg * 8 + 8],
                                        bkt[:, :].rearrange("p (s j) -> p s j", j=SEG), axis=AX.X)
                P.emit("vector", lm, reads=[bk], writes=[r_qlm], tag=f"qk_lm{m}_{tg}", psum=True)
            def sp(e, stt=stt, m=m):
                return e.dma_start(qk_sp[m, :, :], stt[:, :])
            P.emit("sync", sp, reads=[st], writes=[r_qk_sp[m]], is_dma=True, tag=f"qk_sp{m}")

        for tc in range(32):
            bk = r_vbank[tc % 2]
            bkt = v_banks[tc % 2]
            for h in range(HC):
                for half in range(2):
                    def mmv(e, h=h, tc=tc, half=half, bkt=bkt):
                        return e.matmul(bkt[:, half * TG:(half + 1) * TG],
                                        x_bf[:, h * NT + tc * 128:h * NT + (tc + 1) * 128],
                                        wq_bf[:, h * 3 * D + 2 * D + half * TG:h * 3 * D + 2 * D + (half + 1) * TG],
                                        start=(h == 0), stop=(h == HC - 1))
                    P.emit("tensor", mmv,
                           reads=[r_xbf, r_wqbf],
                           writes=[bk] if ((h == 0 and half == 0) or (h == HC - 1 and half == 1)) else [], tag=f"v_mm{tc}_{h}_{half}")
            def evv(e, bkt=bkt, tc=tc):
                return e.activation(stage_v[tc % 2][:, :], bkt[:, :], AF.Copy)
            P.emit("scalar", evv, reads=[bk], writes=[r_stage_v[tc % 2]], tag=f"v_ev{tc}", psum=True)
            def spv(e, tc=tc):
                return e.dma_start(v_sp[tc, :, :], stage_v[tc % 2][:, :])
            P.emit("sync", spv, reads=[r_stage_v[tc % 2]], writes=[r_v_sp[tc]], is_dma=True, tag=f"v_sp{tc}")

        if dbg:
            P.emit("sync", lambda e: e.dma_start(qlm_dbg, qlm_all[:, :]), reads=[r_qlm], is_dma=True)

    P.barrier()
    sv_bf = stack_bc.enter_context(nc.sbuf_tensor("sv_bf", [128, HC * NT], BF16, side="right"))
    r_sv = [Res(f"sv{p}") for p in range(HC)]
    with contextlib.ExitStack() as phb:
        sbb = lambda name, shape, dt: phb.enter_context(nc.sbuf_tensor(name, shape, dt))
        q_pair = [sbb(f"q_pair{i}", [128, NT], BF16) for i in range(2)]
        k_pair = [sbb("k_pair0", [128, NT], BF16)]
        v_pair = sbb("v_pair", [128, NT], BF16)
        expk3 = sbb("expk3", [128, NT], BF16)
        expk1 = [sbb(f"expk1_{i}", [128, NT], BF16) for i in range(4)]
        rb_tmp = [sbb(f"rb_tmp{i}", [128, TG], F32) for i in range(2)]
        rb_rec = [sbb(f"rb_rec{i}", [128, TG], F32) for i in range(2)]
        recipB = [sbb(f"recipB{i}", [128, NT], BF16) for i in range(4)]
        qlm_bf = [sbb(f"qlm_bf{i}", [128, NLM], BF16) for i in range(2)]
        klm_bf = [sbb(f"klm_bf{i}", [128, NLM], BF16) for i in range(2)]
        qlm_bd4 = [sbb(f"qlm_bd4_{g}", [128, 512], BF16) for g in range(2)]
        klm_bd4 = [sbb(f"klm_bd4_{g}", [128, 512], BF16) for g in range(2)]
        k3vn4 = [sbb(f"k3vn4_{g}", [128, 512], BF16) for g in range(2)]
        recip3 = [sbb(f"recip3_{i}", [128, 1], F32) for i in range(2)]
        kexp4 = sbb("kexp4", [128, 512], F32)
        K4 = sbb("K4", [128, 512], BF16)
        KT4 = sbb("KT4", [128, 512], BF16)
        KV4 = sbb("KV4", [128, 512], BF16)
        KVT4 = sbb("KVT4", [128, 512], BF16)
        C14 = sbb("C14", [128, 512], BF16)
        C1T4 = sbb("C1T4", [128, 512], BF16)
        C24 = sbb("C24", [128, 512], BF16)
        C2T4 = sbb("C2T4", [128, 512], BF16)
        C34 = sbb("C34", [128, 512], BF16)
        C3T4 = sbb("C3T4", [128, 512], BF16)
        Vb4 = [sbb(f"Vb4_{i}", [128, 512], BF16) for i in range(2)]
        VTb4 = [sbb(f"VTb4_{i}", [128, 512], BF16) for i in range(2)]
        M4 = sbb("M4", [128, 512], BF16)
        s2sum4 = sbb("s2sum4", [128, 4], F32)
        s2rec4 = sbb("s2rec4", [128, 4], F32)
        dvec4 = sbb("dvec4", [128, 4], F32)
        drec4 = sbb("drec4", [128, 4], F32)
        vjunk = sbb("vjunk", [128, 4], F32)

        pb_big = [phb.enter_context(nc.psum_tensor(f"pbig{i}", [128, TG], F32)) for i in range(2)]
        pb_s3 = [phb.enter_context(nc.psum_tensor(f"ps3{i}", [128, TG], F32)) for i in range(2)]
        pb_k3v = phb.enter_context(nc.psum_tensor("pk3v", [128, 128], F32))
        pb_ssum = phb.enter_context(nc.psum_tensor("pssum", [128, 1], F32))
        pb_inv4 = phb.enter_context(nc.psum_tensor("pinv4", [128, 512], F32))
        pb_tr4 = phb.enter_context(nc.psum_tensor("ptr4", [128, 1024], BF16))

        r_qp = [Res(f"qp{i}") for i in range(2)]
        r_kp = [Res("kp0")]
        r_vp = Res("vp")
        r_e3 = Res("e3")
        r_e1 = [Res(f"e1_{i}") for i in range(4)]
        r_rbt = [Res(f"rbt{i}") for i in range(2)]
        r_rbr = [Res(f"rbr{i}") for i in range(2)]
        r_rB = [Res(f"rB{i}") for i in range(4)]
        r_lmb = [Res(f"lmb{i}") for i in range(2)]
        r_bd = [Res(f"bd{g}") for g in range(2)]
        r_k3vn = [Res(f"k3vn{g}") for g in range(2)]
        r_r3 = [Res(f"r3_{i}") for i in range(2)]
        r_big = [Res(f"big{i}") for i in range(2)]
        r_s3 = [Res(f"s3{i}") for i in range(2)]
        r_k3vb = Res("k3vb"); r_ssum = Res("ssum"); r_inv4 = Res("inv4"); r_tr4 = Res("tr4")
        r_kexp = Res("kexp"); r_K = Res("K"); r_KT = Res("KT")
        r_KV = Res("KV"); r_KVT = Res("KVT")
        r_C = [Res(f"C{i}") for i in range(6)]
        r_V = [Res("V0"), Res("V1")]; r_VT = [Res("VT0"), Res("VT1")]
        r_M = Res("M")
        r_srow = [Res(f"srow{i}") for i in range(4)]

        for g in range(2):
            P.emit("vector", lambda e, g=g: e.memset(qlm_bd4[g][:, :], 0.0), writes=[r_bd[g]])
            P.emit("vector", lambda e, g=g: e.memset(klm_bd4[g][:, :], 0.0), writes=[r_bd[g]])
        P.emit("vector", lambda e: e.memset(K4[:, :], 0.0), writes=[r_K])

        def front_thunks(p):
            a = p % 2
            s = p % 4
            g = p // 4
            jo = 128 * s
            T = []
            T.append(lambda: P.emit("sync", lambda e: e.dma_start(q_pair[a][:, :], qk_sp[p, :, :]),
                                    reads=[r_qk_sp[p]], writes=[r_qp[a]], is_dma=True))
            T.append(lambda: P.emit("sync", lambda e: e.dma_start(k_pair[0][:, :], qk_sp[8 + p, :, :]),
                                    reads=[r_qk_sp[8 + p]], writes=[r_kp[0]], is_dma=True))
            T.append(lambda: P.emit("sync", lambda e: e.dma_start(
                v_pair[:, :].rearrange("p (c j) -> p c j", c=32),
                v_sp[:, :, p * 128:(p + 1) * 128].rearrange("c p j -> p c j")),
                                    reads=r_v_sp, writes=[r_vp], is_dma=True))

            def lm_thunk():
                P.emit("vector", lambda e: e.tensor_scalar_mul(qlm_bf[a][:, :], qlm_all[:, p * NLM:(p + 1) * NLM], 1.0 / 512.0),
                       reads=[r_qlm], writes=[r_lmb[a]])
                P.emit("vector", lambda e: e.tensor_scalar_mul(klm_bf[a][:, :], qlm_all[:, (8 + p) * NLM:(9 + p) * NLM], 1.0 / 64.0),
                       reads=[r_qlm], writes=[r_lmb[a]])
                for half in range(2):
                    sl = slice(64 * half, 64 * half + 64)
                    P.emit("vector", lambda e, sl=sl: e.tensor_copy(qlm_bd4[g][sl, jo + sl.start:jo + sl.start + 64], qlm_bf[a][sl, :]),
                           reads=[r_lmb[a]], writes=[r_bd[g]])
                    P.emit("vector", lambda e, sl=sl: e.tensor_copy(klm_bd4[g][sl, jo + sl.start:jo + sl.start + 64], klm_bf[a][sl, :]),
                           reads=[r_lmb[a]], writes=[r_bd[g]])
            T.append(lm_thunk)

            def s3_thunk(tcg):
                bk, rbk = pb_s3[tcg % 2], r_s3[tcg % 2]
                for j in range(4):
                    tc = tcg * 4 + j
                    P.emit("tensor", lambda e, tc=tc, j=j, bk=bk: e.matmul(
                        bk[:, j * 128:(j + 1) * 128], k_pair[0][:, tc * 128:(tc + 1) * 128],
                        qlm_bd4[g][:, jo:jo + 128], start=True, stop=True),
                           reads=[r_kp[0], r_bd[g]],
                           writes=[rbk] if j in (0, 3) else [], tag=f"s3_{p}_{tc}")
                P.emit("scalar", lambda e, tcg=tcg, bk=bk: e.activation(
                    expk3[:, tcg * TG:(tcg + 1) * TG], bk[:, :], AF.Exp),
                       reads=[rbk], writes=[r_e3], psum=True, tag=f"s3e{p}_{tcg}")
            for tcg in range(8):
                T.append(lambda tcg=tcg: s3_thunk(tcg))

            def k3v_thunk(gg):
                for tc in range(gg * 8, gg * 8 + 8):
                    P.emit("tensor", lambda e, tc=tc: e.matmul(
                        pb_k3v[:, :], expk3[:, tc * 128:(tc + 1) * 128], v_pair[:, tc * 128:(tc + 1) * 128],
                        start=(tc == 0), stop=(tc == 31)),
                           reads=[r_e3, r_vp],
                           writes=[r_k3vb] if tc in (0, 31) else [], tag=f"k3v{p}_{tc}")
                    P.emit("tensor", lambda e, tc=tc: e.matmul(
                        pb_ssum[:, :], expk3[:, tc * 128:(tc + 1) * 128], onesc_bf[:, :],
                        start=(tc == 0), stop=(tc == 31)),
                           reads=[r_e3, r_const],
                           writes=[r_ssum] if tc in (0, 31) else [], tag=f"s3s{p}_{tc}")
            for gg in range(4):
                T.append(lambda gg=gg: k3v_thunk(gg))

            def k3vn_thunk():
                P.emit("vector", lambda e: e.reciprocal(recip3[a][:, :], pb_ssum[:, :]),
                       reads=[r_ssum], writes=[r_r3[a]], psum=True)
                P.emit("vector", lambda e: e.tensor_copy(vjunk[:, 0:1], recip3[a][:, :]), reads=[r_r3[a]])
                P.emit("vector", lambda e: e.tensor_copy(vjunk[:, 0:1], recip3[a][:, :]), reads=[r_r3[a]])
                P.emit("vector", lambda e: e.tensor_tensor(k3vn4[g][:, jo:jo + 128], pb_k3v[:, :],
                                                           bass.AP(recip3[a], 0, [[1, 128], [0, 128]]), ALU.mult),
                       reads=[r_k3vb, r_r3[a]], writes=[r_k3vn[g]], psum=True)
            T.append(k3vn_thunk)

            def s1_thunk(tg):
                bk, rbk = pb_big[tg % 2], r_big[tg % 2]
                P.emit("tensor", lambda e, tg=tg, bk=bk: e.matmul(
                    bk[:, :], klm_bd4[g][:, jo:jo + 128], q_pair[a][:, tg * TG:(tg + 1) * TG], start=True, stop=True),
                       reads=[r_bd[g], r_qp[a], rbk], writes=[rbk], tag=f"s1_{p}_{tg}")
                P.emit("scalar", lambda e, tg=tg, bk=bk: e.activation(
                    expk1[s][:, tg * TG:(tg + 1) * TG], bk[:, :], AF.Exp),
                       reads=[rbk], writes=[r_e1[s]], psum=True)
            for tg in range(NTG):
                T.append(lambda tg=tg: s1_thunk(tg))

            def rs_thunk(tg):
                bk, rbk = pb_big[tg % 2], r_big[tg % 2]
                P.emit("tensor", lambda e, tg=tg, bk=bk: e.matmul(
                    bk[:, :], ones2w[:, :], expk1[s][:, tg * TG:(tg + 1) * TG], start=True, stop=True),
                       reads=[r_const, r_e1[s], rbk], writes=[rbk], tag=f"rs{p}_{tg}")
                P.emit("scalar", lambda e, tg=tg, bk=bk: e.activation(
                    rb_tmp[tg % 2][:, :], bk[:, :], AF.Copy),
                       reads=[rbk], writes=[r_rbt[tg % 2]], psum=True)
                P.emit("vector", lambda e, tg=tg: e.reciprocal(rb_rec[tg % 2][:, :], rb_tmp[tg % 2][:, :]),
                       reads=[r_rbt[tg % 2]], writes=[r_rbr[tg % 2]])
                if tg > 0:
                    round_rb(tg - 1)
                if tg == NTG - 1:
                    round_rb(tg)

            def round_rb(tg):
                def b(e, tg=tg):
                    with nc.allow_low_precision(reason="recipB is a bf16 matmul-side scale"):
                        return e.tensor_copy(recipB[s][:, tg * TG:(tg + 1) * TG], rb_rec[tg % 2][:, :])
                P.emit("vector", b, reads=[r_rbr[tg % 2]], writes=[r_rB[s]])
            for tg in range(NTG):
                T.append(lambda tg=tg: rs_thunk(tg))
            return T

        def mid_thunks(g):
            T = []

            def s2_thunk():
                for j in range(4):
                    P.emit("tensor", lambda e, j=j: e.matmul(
                        pb_inv4[:, j * 128:(j + 1) * 128], qlm_bd4[g][:, j * 128:(j + 1) * 128],
                        klm_bd4[g][:, j * 128:(j + 1) * 128], start=True, stop=True),
                           reads=[r_bd[g]],
                           writes=[r_inv4] if j in (0, 3) else [], tag=f"s2mm{g}_{j}")
                P.emit("scalar", lambda e: e.activation(kexp4[:, :], pb_inv4[:, :], AF.Exp),
                       reads=[r_inv4], writes=[r_kexp], psum=True)
            T.append(s2_thunk)

            def dview(t_, half, inner=64):
                return bass.AP(t_, 64 * half * (t_.shape[1] if hasattr(t_, "shape") else 512) // 1, [])

            def ksm_thunk():
                for half in range(2):
                    P.emit("vector", lambda e, half=half: e.reduce_sum(
                        s2sum4[64 * half:64 * half + 64, :],
                        bass.AP(kexp4, 64 * half * 512 // 128 * 128 + 64 * half, [[512, 64], [128, 4], [1, 64]]),
                        axis=AX.X),
                           reads=[r_kexp], writes=[r_srow[0]])
                P.emit("vector", lambda e: e.reciprocal(s2rec4[:, :], s2sum4[:, :]),
                       reads=[r_srow[0]], writes=[r_srow[1]])
                P.emit("vector", lambda e: e.tensor_copy(vjunk[:, :], s2rec4[:, :]), reads=[r_srow[1]])
                P.emit("vector", lambda e: e.tensor_copy(vjunk[:, :], s2rec4[:, :]), reads=[r_srow[1]])
                for half in range(2):
                    P.emit("vector", lambda e, half=half: e.tensor_tensor(
                        bass.AP(K4, 64 * half * 512 + 64 * half, [[512, 64], [128, 4], [1, 64]]),
                        bass.AP(kexp4, 64 * half * 512 + 64 * half, [[512, 64], [128, 4], [1, 64]]),
                        bass.AP(s2rec4, 64 * half * 4, [[4, 64], [1, 4], [0, 64]]), ALU.mult),
                           reads=[r_kexp, r_srow[1]], writes=[r_K])
            T.append(ksm_thunk)

            def kt_thunk():
                for j in range(4):
                    P.emit("tensor", lambda e, j=j: e.transpose(
                        pb_tr4[:, j * 128:(j + 1) * 128], K4[:, j * 128:(j + 1) * 128], id_bf[:, :]),
                           reads=[r_K, r_const],
                           writes=[r_tr4] if j in (0, 3) else [], tag=f"ktr{g}_{j}")
                P.emit("vector", lambda e: e.tensor_copy(KT4[:, :], pb_tr4[:, 0:512]),
                       reads=[r_tr4], writes=[r_KT], psum=True)
            T.append(kt_thunk)

            def den_thunk():
                for j in range(4):
                    P.emit("tensor", lambda e, j=j: e.matmul(
                        pb_inv4[:, j * 128:(j + 1) * 128], ones2w[:, :], K4[:, j * 128:(j + 1) * 128],
                        start=True, stop=True),
                           reads=[r_K, r_const],
                           writes=[r_inv4] if j in (0, 3) else [], tag=f"den{g}_{j}")
                for half in range(2):
                    P.emit("vector", lambda e, half=half: e.reduce_max(
                        dvec4[64 * half:64 * half + 64, :],
                        bass.AP(pb_inv4, 64 * half * 512 + 64 * half, [[512, 64], [128, 4], [1, 64]]),
                        axis=AX.X),
                           reads=[r_inv4], writes=[r_srow[2]], psum=True)
                P.emit("vector", lambda e: e.reciprocal(drec4[:, :], dvec4[:, :]),
                       reads=[r_srow[2]], writes=[r_srow[3]])
                P.emit("vector", lambda e: e.tensor_copy(vjunk[:, :], drec4[:, :]), reads=[r_srow[3]])
                P.emit("vector", lambda e: e.tensor_copy(vjunk[:, :], drec4[:, :]), reads=[r_srow[3]])
                P.emit("vector", lambda e: e.tensor_tensor(
                    Vb4[0][:, :].rearrange("p (a b) -> p a b", a=4),
                    KT4[:, :].rearrange("p (a b) -> p a b", a=4),
                    bass.AP(drec4, 0, [[4, 128], [1, 4], [0, 128]]), ALU.mult),
                       reads=[r_KT, r_srow[3]], writes=[r_V[0]])
                P.emit("vector", lambda e: e.tensor_tensor(
                    VTb4[0][:, :].rearrange("p (a b) -> p a b", a=4),
                    K4[:, :].rearrange("p (a b) -> p a b", a=4),
                    bass.AP(drec4, 0, [[4, 128], [1, 4], [0, 128]]), ALU.mult),
                       reads=[r_K, r_srow[3]], writes=[r_VT[0]])
            T.append(den_thunk)

            def mm4_evac(lhs4, rhs4, lhs_r, rhs_r, out_sb, out_r, kind, sc=None, iN=None, tagn=""):
                for j in range(4):
                    P.emit("tensor", lambda e, j=j, lhs4=lhs4, rhs4=rhs4: e.matmul(
                        pb_inv4[:, j * 128:(j + 1) * 128], lhs4[:, j * 128:(j + 1) * 128],
                        rhs4[:, j * 128:(j + 1) * 128], start=True, stop=True),
                           reads=[lhs_r, rhs_r],
                           writes=[r_inv4] if j in (0, 3) else [], tag=f"{tagn}_{j}")
                if kind == "copy":
                    P.emit("vector", lambda e, out_sb=out_sb: e.tensor_copy(out_sb[:, :], pb_inv4[:, :]),
                           reads=[r_inv4], writes=[out_r], psum=True)
                else:
                    P.emit("vector", lambda e, out_sb=out_sb, sc=sc, iN=iN: e.scalar_tensor_tensor(
                        out_sb[:, :].rearrange("p (a b) -> p a b", a=4),
                        pb_inv4[:, :].rearrange("p (a b) -> p a b", a=4), sc,
                        bass.AP(iN, 0, [[128, 128], [0, 4], [1, 128]]), ALU.mult, ALU.add),
                           reads=[r_inv4, r_const], writes=[out_r], psum=True)

            for it in range(6):
                cur, nxt = it % 2, (it + 1) % 2
                def iter_thunkA(cur=cur, it=it):
                    mm4_evac(KT4, Vb4[cur], r_KT, r_V[cur], KV4, r_KV, "copy", tagn=f"kv{g}_{it}")
                    mm4_evac(Vb4[cur], KT4, r_V[cur], r_KT, KVT4, r_KVT, "copy", tagn=f"kvt{g}_{it}")
                    P.emit("vector", lambda e: e.scalar_tensor_tensor(
                        C14[:, :].rearrange("p (a b) -> p a b", a=4),
                        KV4[:, :].rearrange("p (a b) -> p a b", a=4), -1.0,
                        bass.AP(i7, 0, [[128, 128], [0, 4], [1, 128]]), ALU.mult, ALU.add),
                           reads=[r_KV, r_const], writes=[r_C[0]])
                    P.emit("vector", lambda e: e.scalar_tensor_tensor(
                        C1T4[:, :].rearrange("p (a b) -> p a b", a=4),
                        KVT4[:, :].rearrange("p (a b) -> p a b", a=4), -1.0,
                        bass.AP(i7, 0, [[128, 128], [0, 4], [1, 128]]), ALU.mult, ALU.add),
                           reads=[r_KVT, r_const], writes=[r_C[1]])
                def iter_thunkB(it=it):
                    mm4_evac(KVT4, C14, r_KVT, r_C[0], C24, r_C[2], "stt", sc=-1.0, iN=i15, tagn=f"b{g}_{it}")
                    mm4_evac(C14, KVT4, r_C[0], r_KVT, C2T4, r_C[3], "stt", sc=-1.0, iN=i15, tagn=f"bt{g}_{it}")
                def iter_thunkC(it=it):
                    mm4_evac(KVT4, C24, r_KVT, r_C[2], C34, r_C[4], "stt", sc=-0.25, iN=i325, tagn=f"d{g}_{it}")
                    mm4_evac(C2T4, KVT4, r_C[3], r_KVT, C3T4, r_C[5], "stt", sc=-0.25, iN=i325, tagn=f"dt{g}_{it}")
                def iter_thunkD(cur=cur, nxt=nxt, it=it):
                    mm4_evac(VTb4[cur], C34, r_VT[cur], r_C[4], Vb4[nxt], r_V[nxt], "copy", tagn=f"vn{g}_{it}")
                    mm4_evac(C34, VTb4[cur], r_C[4], r_VT[cur], VTb4[nxt], r_VT[nxt], "copy", tagn=f"vnt{g}_{it}")
                T.extend([iter_thunkA, iter_thunkB, iter_thunkC, iter_thunkD])
            fin = 6 % 2

            def m_thunk():
                for j in range(4):
                    P.emit("tensor", lambda e, j=j, fin=fin: e.matmul(
                        pb_inv4[:, j * 128:(j + 1) * 128], VTb4[fin][:, j * 128:(j + 1) * 128],
                        k3vn4[g][:, j * 128:(j + 1) * 128], start=True, stop=True),
                           reads=[r_VT[fin], r_k3vn[g]],
                           writes=[r_inv4] if j in (0, 3) else [], tag=f"Mmm{g}_{j}")
                P.emit("vector", lambda e: e.tensor_tensor(
                    M4[:, :].rearrange("p (a b) -> p a b", a=4),
                    pb_inv4[:, :].rearrange("p (a b) -> p a b", a=4),
                    bass.AP(blockmask, 0, [[128, 128], [0, 4], [1, 128]]), ALU.mult),
                       reads=[r_inv4, r_const], writes=[r_M], psum=True)
            T.append(m_thunk)
            return T

        def back_thunks(p):
            s = p % 4
            jo = 128 * s
            T = []
            def svu_thunk(tg):
                bk, rbk = pb_big[tg % 2], r_big[tg % 2]
                P.emit("tensor", lambda e, tg=tg, bk=bk: e.matmul(
                    bk[:, :], M4[:, jo:jo + 128], expk1[s][:, tg * TG:(tg + 1) * TG], start=True, stop=True),
                       reads=[r_M, r_e1[s], rbk], writes=[rbk], tag=f"svu{p}_{tg}")
                P.emit("vector", lambda e, tg=tg, bk=bk: e.tensor_tensor(
                    sv_bf[:, p * NT + tg * TG:p * NT + (tg + 1) * TG], bk[:, :],
                    recipB[s][:, tg * TG:(tg + 1) * TG], ALU.mult),
                       reads=[rbk, r_rB[s]], writes=[r_sv[p]], psum=True)
            for tg in range(NTG):
                T.append(lambda tg=tg: svu_thunk(tg))
            if dbg and p == 0:
                def dbg_thunk():
                    P.emit("sync", lambda e: e.dma_start(e1_dbg, expk1[0][:, :]), reads=[r_e1[0]], is_dma=True)
                    P.emit("sync", lambda e: e.dma_start(k3vn_dbg, k3vn4[0][:, 0:128]), reads=[r_k3vn[0]], is_dma=True)
                    P.emit("sync", lambda e: e.dma_start(K_dbg, K4[:, 0:128]), reads=[r_K], is_dma=True)
                    P.emit("sync", lambda e: e.dma_start(M_dbg, M4[:, 0:128]), reads=[r_M], is_dma=True)

                T.append(dbg_thunk)
            return T

        def zip_emit(main, filler):
            fi = 0
            for mi, mth in enumerate(main):
                mth()
                while fi * len(main) < (mi + 1) * len(filler):
                    filler[fi]()
                    fi += 1
            while fi < len(filler):
                filler[fi]()
                fi += 1

        for p in range(4):
            for th in front_thunks(p):
                th()
        fl = []
        for p in range(4, 8):
            fl.extend(front_thunks(p))
        zip_emit(mid_thunks(0), [])
        bl = []
        for p in range(4):
            bl.extend(back_thunks(p))
        zip_emit(bl, fl)
        zip_emit(mid_thunks(1), [])
        for p in range(4, 8):
            for th in back_thunks(p):
                th()

        if dbg:
            for p in range(HC):
                P.emit("sync", lambda e, p=p: e.dma_start(sv_dbg[p, :, :], sv_bf[:, p * NT:(p + 1) * NT]),
                       reads=[r_sv[p]], is_dma=True)

    P.barrier()
    with contextlib.ExitStack() as phc:
        sbc = lambda name, shape, dt: phc.enter_context(nc.sbuf_tensor(name, shape, dt))
        wp_bf = sbc("wp_bf", [128, HC * D], BF16)
        x_res = sbc("x_res", [128, NT], F32)
        z1_buf = sbc("z1_buf", [128, NT], F32)
        z1b_buf = sbc("z1b_buf", [128, TG], BF16)
        sq_buf = sbc("sq_buf", [128, TG], BF16)
        y1f_buf = sbc("y1f_buf", [128, HALF], F32)
        y1b_buf = sbc("y1b_buf", [128, HALF], BF16)
        mrow = sbc("mrow", [1, NT], F32)
        srow = sbc("srow", [1, NT], F32)
        arow = sbc("arow", [1, NT], F32)

        pb_proj = [phc.enter_context(nc.psum_tensor(f"pproj{i}", [128, TG], F32)) for i in range(2)]
        pb_mean = phc.enter_context(nc.psum_tensor("pmean", [1, TG], F32))
        pb_sq = phc.enter_context(nc.psum_tensor("psq", [1, TG], F32))
        pb_a = phc.enter_context(nc.psum_tensor("pa", [128, TG], F32))
        pb_b = phc.enter_context(nc.psum_tensor("pb", [128, TG], F32))

        r_wp = Res("wp"); r_xres = Res("xres"); r_z1 = Res("z1"); r_z1b = Res("z1b")
        r_sq = Res("sq"); r_y1f = Res("y1f"); r_y1b = Res("y1b")
        r_proj = [Res(f"proj{i}") for i in range(2)]
        r_pmean = Res("pmean"); r_psq = Res("psq"); r_pa = Res("pa"); r_pb = Res("pb")
        r_rows = Res("rows")
        r_z1_sp = [Res(f"z1sp{o}") for o in range(HC)]
        r_y1f_sp = [Res(f"y1fsp{o}") for o in range(HC)]
        r_y1b_sp = [Res(f"y1bsp{o}") for o in range(HC)]

        P.emit("gpsimd", lambda e: e.dma_start(wp_bf[:, :].rearrange("p (c o) -> p c o", c=HC),
                                               wproj.rearrange("(c p) o -> p c o", p=128)),
               writes=[r_wp], is_dma=True)
        P.emit("vector", lambda e: e.memset(mrow[:, :], 0.0), writes=[r_rows])
        P.emit("vector", lambda e: e.memset(srow[:, :], 0.0), writes=[r_rows])

        for o in range(HC):
            P.emit("sync", lambda e, o=o: e.dma_start(x_res[:, :], xT[o * 128:(o + 1) * 128, :]),
                   writes=[r_xres], is_dma=True, tag=f"xr{o}")
            for tg in range(NTG):
                bk, rbk = pb_proj[tg % 2], r_proj[tg % 2]
                for c in range(HC):
                    P.emit("tensor", lambda e, o=o, tg=tg, c=c, bk=bk: e.matmul(
                        bk[:, :], wp_bf[:, c * D + o * 128:c * D + (o + 1) * 128],
                        sv_bf[:, c * NT + tg * TG:c * NT + (tg + 1) * TG],
                        start=(c == 0), stop=(c == HC - 1)),
                           reads=[r_wp, r_sv[c]],
                           writes=[rbk] if c in (0, HC - 1) else [], tag=f"pj{o}_{tg}_{c}")
                P.emit("vector", lambda e, tg=tg, bk=bk: e.scalar_tensor_tensor(
                    z1_buf[:, tg * TG:(tg + 1) * TG], bk[:, :], 1.0,
                    x_res[:, tg * TG:(tg + 1) * TG], ALU.mult, ALU.add),
                       reads=[rbk, r_xres], writes=[r_z1], psum=True, tag=f"z1_{o}_{tg}")
            P.emit("sync", lambda e, o=o: e.dma_start(z1_sp[o, :, :], z1_buf[:, :]),
                   reads=[r_z1], writes=[r_z1_sp[o]], is_dma=True, tag=f"z1sp{o}")

        pb_sx = [phc.enter_context(nc.psum_tensor(f"psx{i}", [1, TG], F32)) for i in range(2)]
        r_psx = [Res(f"psx{i}") for i in range(2)]
        pb_mn4 = [pb_proj[0], pb_proj[1], pb_a, pb_b]
        r_mn4 = [r_proj[0], r_proj[1], r_pa, r_pb]
        pb_sq4 = [pb_mean, pb_sq, pb_sx[0], pb_sx[1]]
        r_sq4 = [r_pmean, r_psq, r_psx[0], r_psx[1]]
        sqb = [z1b_buf, sq_buf]
        r_sqb = [r_z1b, r_sq]
        for th in range(2):
            for o in range(HC):
                P.emit("sync", lambda e, o=o, th=th: e.dma_start(
                    z1_buf[:, 0:HALF], z1_sp[o, :, th * HALF:(th + 1) * HALF]),
                       reads=[r_z1_sp[o]], writes=[r_z1], is_dma=True, tag=f"z1st{o}_{th}")
                for tgl in range(4):
                    P.emit("scalar", lambda e, tgl=tgl: e.activation(
                        sqb[tgl % 2][:, :], z1_buf[:, tgl * TG:(tgl + 1) * TG], AF.Square),
                           reads=[r_z1], writes=[r_sqb[tgl % 2]])
                    P.emit("tensor", lambda e, tgl=tgl, o=o: e.matmul(
                        pb_mn4[tgl][0:1, :], onesc_f[:, :], z1_buf[:, tgl * TG:(tgl + 1) * TG],
                        start=(o == 0), stop=(o == HC - 1)),
                           reads=[r_const, r_z1],
                           writes=[r_mn4[tgl]] if o in (0, HC - 1) else [], tag=f"mn{th}_{o}_{tgl}")
                    P.emit("tensor", lambda e, tgl=tgl, o=o: e.matmul(
                        pb_sq4[tgl][0:1, :], onesc_bf[:, :], sqb[tgl % 2][:, :],
                        start=(o == 0), stop=(o == HC - 1)),
                           reads=[r_const, r_sqb[tgl % 2]],
                           writes=[r_sq4[tgl]] if o in (0, HC - 1) else [], tag=f"sq{th}_{o}_{tgl}")
            for tgl in range(4):
                P.emit("vector", lambda e, tgl=tgl, th=th: e.tensor_copy(
                    mrow[:, th * HALF + tgl * TG:th * HALF + (tgl + 1) * TG], pb_mn4[tgl][0:1, :]),
                       reads=[r_mn4[tgl]], writes=[r_rows], psum=True)
                P.emit("vector", lambda e, tgl=tgl, th=th: e.tensor_copy(
                    srow[:, th * HALF + tgl * TG:th * HALF + (tgl + 1) * TG], pb_sq4[tgl][0:1, :]),
                       reads=[r_sq4[tgl]], writes=[r_rows], psum=True)

        P.emit("vector", lambda e: e.tensor_scalar_mul(mrow[:, :], mrow[:, :], 1.0 / D), reads=[r_rows], writes=[r_rows])
        P.emit("vector", lambda e: e.tensor_mul(arow[:, :], mrow[:, :], mrow[:, :]), reads=[r_rows], writes=[r_rows])
        P.emit("vector", lambda e: e.scalar_tensor_tensor(srow[:, :], srow[:, :], 1.0 / D, arow[:, :], ALU.mult, ALU.subtract),
               reads=[r_rows], writes=[r_rows])
        P.emit("vector", lambda e: e.tensor_scalar_add(srow[:, :], srow[:, :], 1e-5), reads=[r_rows], writes=[r_rows])
        P.emit("scalar", lambda e: e.activation(arow[:, :], srow[:, :], AF.Ln),
               reads=[r_rows], writes=[r_rows])
        P.emit("scalar", lambda e: e.activation(arow[:, :], arow[:, :], AF.Exp, scale=-0.5),
               reads=[r_rows], writes=[r_rows])
        P.emit("vector", lambda e: e.scalar_tensor_tensor(srow[:, :], mrow[:, :], -1.0, arow[:, :], ALU.mult, ALU.mult),
               reads=[r_rows], writes=[r_rows])

        a_bc = phc.enter_context(nc.sbuf_tensor("a_bc", [128, HALF], F32))
        b_bc = phc.enter_context(nc.sbuf_tensor("b_bc", [128, HALF], F32))
        r_abc = Res("abc")
        for th in range(2):
            for tgl in range(4):
                tg = th * 4 + tgl
                P.emit("tensor", lambda e, tg=tg: e.matmul(
                    pb_a[:, :], ones1_f[:, :], arow[:, tg * TG:(tg + 1) * TG], start=True, stop=True),
                       reads=[r_const, r_rows, r_pa], writes=[r_pa], tag=f"ba{tg}")
                P.emit("tensor", lambda e, tg=tg: e.matmul(
                    pb_b[:, :], ones1_f[:, :], srow[:, tg * TG:(tg + 1) * TG], start=True, stop=True),
                       reads=[r_const, r_rows, r_pb], writes=[r_pb], tag=f"bb{tg}")
                P.emit("scalar", lambda e, tgl=tgl: e.activation(a_bc[:, tgl * TG:(tgl + 1) * TG], pb_a[:, :], AF.Copy),
                       reads=[r_pa], writes=[r_abc], psum=True)
                P.emit("scalar", lambda e, tgl=tgl: e.activation(b_bc[:, tgl * TG:(tgl + 1) * TG], pb_b[:, :], AF.Copy),
                       reads=[r_pb], writes=[r_abc], psum=True)
            for o in range(HC):
                P.emit("sync", lambda e, o=o, th=th: e.dma_start(
                    z1_buf[:, 0:HALF], z1_sp[o, :, th * HALF:(th + 1) * HALF]),
                       reads=[r_z1_sp[o]], writes=[r_z1], extra=[r_z1b.w, r_sq.w], is_dma=True, tag=f"z1rl{o}_{th}")
                for tgl in range(4):
                    P.emit("vector", lambda e, tgl=tgl: e.tensor_tensor(
                        y1f_buf[:, tgl * TG:(tgl + 1) * TG], z1_buf[:, tgl * TG:(tgl + 1) * TG],
                        a_bc[:, tgl * TG:(tgl + 1) * TG], ALU.mult),
                           reads=[r_z1, r_abc], writes=[r_y1f])
                    P.emit("vector", lambda e, tgl=tgl: e.tensor_tensor(
                        y1f_buf[:, tgl * TG:(tgl + 1) * TG], y1f_buf[:, tgl * TG:(tgl + 1) * TG],
                        b_bc[:, tgl * TG:(tgl + 1) * TG], ALU.add),
                           reads=[r_abc], writes=[r_y1f])
                    P.emit("vector", lambda e, tgl=tgl: e.tensor_copy(
                        y1b_buf[:, tgl * TG:(tgl + 1) * TG], y1f_buf[:, tgl * TG:(tgl + 1) * TG]),
                           reads=[r_y1f], writes=[r_y1b])
                P.emit("sync", lambda e, o=o, th=th: e.dma_start(
                    y1f_sp[o, :, th * HALF:(th + 1) * HALF], y1f_buf[:, :]),
                       reads=[r_y1f], writes=[r_y1f_sp[o]], is_dma=True)
                P.emit("sync", lambda e, o=o, th=th: e.dma_start(
                    y1b_sp[o, :, th * HALF:(th + 1) * HALF], y1b_buf[:, :]),
                       reads=[r_y1b], writes=[r_y1b_sp[o]], is_dma=True)

    stack_bc.close()
    P.barrier()
    gb_ready = Res("gb_ready")

    def emit_half(hf):
        P.barrier()
        with contextlib.ExitStack() as phd:
            sbd = lambda name, shape, dt: phd.enter_context(nc.sbuf_tensor(f"{name}_h{hf}", shape, dt))
            y1h = sbd("y1h", [128, HC * HALF], BF16)
            accum = sbd("accum", [128, HC * HALF], F32)
            wg_bf = [sbd(f"wg_bf{i}", [128, HC * 512], BF16) for i in range(2)]
            wu_bf = [sbd(f"wu_bf{i}", [128, HC * 512], BF16) for i in range(2)]
            wd_bf = [sbd(f"wd_bf{i}", [128, 4 * D], BF16) for i in range(2)]
            c_sb = [sbd(f"c_sb{i}", [128, TG], F32) for i in range(2)]

            pb_gu = [phd.enter_context(nc.psum_tensor(f"pgu{i}_h{hf}", [128, TG], F32)) for i in range(4)]
            pb_dn = [phd.enter_context(nc.psum_tensor(f"pdn{i}_h{hf}", [128, TG], F32)) for i in range(3)]
            pb_mc = phd.enter_context(nc.psum_tensor(f"pmc_h{hf}", [128, TG], F32))
            scopeRE = phd.enter_context(contextlib.ExitStack())
            sbr = lambda name, shape, dt: scopeRE.enter_context(nc.sbuf_tensor(f"{name}_h{hf}", shape, dt))
            combine_fm = sbr("combine_fm", [NE, HALF], BF16)
            c_row = [sbr(f"c_row{i}", [1, HALF], BF16) for i in range(2)]
            guc = [sbr(f"guc{i}", [128, 4 * HALF], BF16) for i in range(2)]
            sg_t = [sbr("sg0", [128, TG], F32)]
            uc_t = [sbr("uc0", [128, TG], F32)]
            NSLOT = 1
            scopeR = contextlib.ExitStack()
            sbr2 = lambda name, shape, dt: scopeR.enter_context(nc.sbuf_tensor(f"{name}_h{hf}", shape, dt))
            gw_bf = sbr2("gw_bf", [128, HC * NE], BF16)
            logits_f = sbr2("logits_f", [32, TG], F32)
            scores_all = sbr2("scores_all", [128, 16 * NE], F32)
            scb_all = sbr2("scb_all", [128, 16 * NE], F32)
            gsc_all = sbr2("gsc_all", [128, 128], F32)
            gtmp = [sbr2(f"gtmp{i}", [128, 128], F32) for i in range(3)]
            m8g = gtmp[0]
            m8m = gtmp[1]
            masked_all = scb_all
            sel_all = scb_all
            w_all = scores_all
            comb_all = w_all
            thr8 = sbr2("thr8", [128, 16], F32)
            ws_r = sbr2("ws_r", [128, 16], F32)
            wr_r = sbr2("wr_r", [128, 16], F32)

            r_y1h = Res("y1h"); r_acc = Res("acc")
            r_guc = [Res(f"guc{i}") for i in range(2)]
            r_wg = [Res(f"wg{i}") for i in range(2)]; r_wu = [Res(f"wu{i}") for i in range(2)]
            r_wd = [Res(f"wd{i}") for i in range(2)]
            r_csb = Res("csb"); r_sg = [Res("sg0")]; r_uct = [Res("uct0")]
            r_gw = Res("gw"); r_cfm = Res("cfm"); r_rout = Res("rout")
            r_pgu = [Res(f"pgu{i}") for i in range(4)]; r_pdn = [Res(f"pdn{i}") for i in range(3)]
            r_pmc = Res("pmc")
            r_y1ft = Res("y1ft"); r_z2b = Res("z2b"); r_sq2 = Res("sq2"); r_outt = Res("outt")
            r_rows2 = Res("rows2")

            ts = slice(hf * HALF, (hf + 1) * HALF)
            for o in range(HC):
                P.emit("sync", lambda e, o=o: e.dma_start(
                    y1h[:, o * HALF:(o + 1) * HALF], y1b_sp[o, :, ts]),
                       reads=[r_y1b_sp[o]], writes=[r_y1h], is_dma=True, tag=f"y1h{o}")
            P.emit("gpsimd", lambda e: e.dma_start(gw_bf[:, :].rearrange("p (c n) -> p c n", c=HC),
                                                   gwT.rearrange("(c p) n -> p c n", p=128)),
                   writes=[r_gw], is_dma=True)
            P.emit("vector", lambda e: e.memset(accum[:, :], 0.0), writes=[r_acc])
            if hf == 0:
                P.emit("tensor", lambda e: e.matmul(pb_mc[:, 0:NE], ones1_f[:, :], gb_row[:, :], start=True, stop=True),
                       reads=[r_const, r_pmc], writes=[r_pmc], tag="gbb")
                P.emit("vector", lambda e: e.tensor_copy(gb_b[:, :], pb_mc[:, 0:NE]),
                       reads=[r_pmc], writes=[gb_ready], psum=True)
                P.emit("vector", lambda e: e.tensor_copy(gsc_all[:, 0:NE], gb_b[:, :]), reads=[gb_ready])
                P.emit("vector", lambda e: e.tensor_copy(gsc_all[:, 0:NE], gb_b[:, :]), reads=[gb_ready])

            r_lg = Res("lg")
            for tg in range(4):
                for h in range(HC):
                    P.emit("tensor", lambda e, tg=tg, h=h: e.matmul(
                        pb_mc[0:32, :], gw_bf[:, h * NE:(h + 1) * NE],
                        y1h[:, h * HALF + tg * TG:h * HALF + (tg + 1) * TG],
                        start=(h == 0), stop=(h == HC - 1)),
                           reads=[r_gw, r_y1h],
                           writes=[r_pmc] if h in (0, HC - 1) else [], tag=f"lg{tg}_{h}")
                P.emit("scalar", lambda e: e.activation(
                    logits_f[:, :], pb_mc[0:32, :], AF.Copy),
                       reads=[r_pmc], writes=[r_lg], psum=True)
                for j in range(4):
                    tc = tg * 4 + j
                    P.emit("tensor", lambda e, tc=tc, j=j: e.transpose(
                        pb_dn[2][:, tc * 32:(tc + 1) * 32], logits_f[0:32, j * 128:(j + 1) * 128], id_f32[0:32, 0:32]),
                           reads=[r_lg],
                           writes=[r_pdn[2]] if tc in (0, 15) else [], tag=f"ltr{tc}")
            P.emit("scalar", lambda e: e.activation(scores_all[:, :], pb_dn[2][:, 0:512], AF.Sigmoid),
                   reads=[r_pdn[2]], writes=[r_rout], psum=True)
            if dbg and hf == 0:
                P.emit("sync", lambda e: e.dma_start(rt_dbg[0], scores_all[:, :]), reads=[r_rout], is_dma=True)
            v16e = lambda t_: t_[:, :].rearrange("p (a b) -> p a b", a=16)
            v16g4 = lambda t_: t_[:, :].rearrange("p (a g j) -> p a g j", a=16, j=4)
            v16x8 = lambda t_: t_[:, :].rearrange("p (a g) -> p a g", a=16)
            P.emit("vector", lambda e: e.tensor_tensor(
                v16e(scb_all), v16e(scores_all),
                bass.AP(gb_b, 0, [[NE, 128], [0, 16], [1, NE]]), ALU.add),
                   reads=[r_rout, gb_ready], writes=[r_rout])
            if dbg and hf == 0:
                P.emit("sync", lambda e: e.dma_start(rt_dbg[1], scb_all[:, :]), reads=[r_rout], is_dma=True)
            def Xj(j):
                return bass.AP(scb_all, j, [[512, 128], [NE, 16], [4, 8]])
            g3 = lambda t_: t_[:, :].rearrange("p (a g) -> p a g", a=16)
            P.emit("vector", lambda e: e.tensor_tensor(g3(gtmp[1]), Xj(0), Xj(1), ALU.min), reads=[r_rout], writes=[r_rout])
            P.emit("vector", lambda e: e.tensor_tensor(g3(gtmp[2]), Xj(2), Xj(3), ALU.min), reads=[r_rout], writes=[r_rout])
            P.emit("vector", lambda e: e.tensor_tensor(gtmp[1][:, :], gtmp[1][:, :], gtmp[2][:, :], ALU.max), reads=[r_rout], writes=[r_rout])
            P.emit("vector", lambda e: e.tensor_tensor(g3(gtmp[2]), Xj(0), Xj(1), ALU.max), reads=[r_rout], writes=[r_rout])
            P.emit("vector", lambda e: e.tensor_tensor(g3(gtmp[0]), Xj(2), Xj(3), ALU.max), reads=[r_rout], writes=[r_rout])
            P.emit("vector", lambda e: e.tensor_tensor(gsc_all[:, :], gtmp[2][:, :], gtmp[0][:, :], ALU.max), reads=[r_rout], writes=[r_rout])
            P.emit("vector", lambda e: e.tensor_tensor(gtmp[2][:, :], gtmp[2][:, :], gtmp[0][:, :], ALU.min), reads=[r_rout], writes=[r_rout])
            P.emit("vector", lambda e: e.tensor_tensor(gtmp[2][:, :], gtmp[2][:, :], gtmp[1][:, :], ALU.max), reads=[r_rout], writes=[r_rout])
            P.emit("vector", lambda e: e.tensor_tensor(gsc_all[:, :], gsc_all[:, :], gtmp[2][:, :], ALU.add), reads=[r_rout], writes=[r_rout])
            for tc in range(16):
                P.emit("vector", lambda e, tc=tc: e.max(m8g[:, tc * 8:(tc + 1) * 8], gsc_all[:, tc * 8:(tc + 1) * 8]),
                       reads=[r_rout], writes=[r_rout])
            if dbg and hf == 0:
                P.emit("sync", lambda e: e.dma_start(rt_dbg[2][:, 0:128], gsc_all[:, :]), reads=[r_rout], is_dma=True)
                P.emit("sync", lambda e: e.dma_start(rt_dbg[2][:, 128:256], m8g[:, :]), reads=[r_rout], is_dma=True)
            P.emit("vector", lambda e: e.tensor_copy(gtmp[2][:, 0:1], m8g[:, 0:1]), reads=[r_rout], writes=[r_rout])
            P.emit("vector", lambda e: e.tensor_copy(gtmp[2][:, 0:1], m8g[:, 0:1]), reads=[r_rout], writes=[r_rout])
            P.emit("vector", lambda e: e.tensor_tensor(
                v16x8(gsc_all), v16x8(gsc_all), bass.AP(m8g, 3, [[128, 128], [8, 16], [0, 8]]), ALU.is_ge),
                   reads=[r_rout], writes=[r_rout])
            P.emit("vector", lambda e: e.tensor_tensor(
                v16g4(masked_all), v16g4(scb_all),
                bass.AP(gsc_all, 0, [[128, 128], [8, 16], [1, 8], [0, 4]]), ALU.mult),
                   reads=[r_rout], writes=[r_rout])
            if dbg and hf == 0:
                P.emit("sync", lambda e: e.dma_start(rt_dbg[3], masked_all[:, :]), reads=[r_rout], is_dma=True)
                P.emit("sync", lambda e: e.dma_start(rt_dbg[4][:, 0:128], gsc_all[:, :]), reads=[r_rout], is_dma=True)
            for tc in range(16):
                P.emit("vector", lambda e, tc=tc: e.max(m8m[:, tc * 8:(tc + 1) * 8], masked_all[:, tc * NE:(tc + 1) * NE]),
                       reads=[r_rout], writes=[r_rout])
            P.emit("vector", lambda e: e.tensor_copy(gtmp[2][:, 0:8], m8m[:, 120:128]), reads=[r_rout], writes=[r_rout])
            P.emit("vector", lambda e: e.tensor_copy(gtmp[2][:, 0:8], m8m[:, 120:128]), reads=[r_rout], writes=[r_rout])
            P.emit("vector", lambda e: e.tensor_scalar_max(thr8[:, :], bass.AP(m8m, 7, [[128, 128], [8, 16]]), 1e-30),
                   reads=[r_rout], writes=[r_rout])
            P.emit("vector", lambda e: e.tensor_copy(gtmp[2][:, 0:16], thr8[:, :]), reads=[r_rout], writes=[r_rout])
            P.emit("vector", lambda e: e.tensor_copy(gtmp[2][:, 0:16], thr8[:, :]), reads=[r_rout], writes=[r_rout])
            P.emit("vector", lambda e: e.tensor_tensor(
                v16e(sel_all), v16e(masked_all), bass.AP(thr8, 0, [[16, 128], [1, 16], [0, NE]]), ALU.is_ge),
                   reads=[r_rout], writes=[r_rout])
            if dbg and hf == 0:
                P.emit("sync", lambda e: e.dma_start(rt_dbg[6], sel_all[:, :]), reads=[r_rout], is_dma=True)
                P.emit("sync", lambda e: e.dma_start(rt_dbg[4][:, 128:144], thr8[:, :]), reads=[r_rout], is_dma=True)
            P.emit("vector", lambda e: e.tensor_mul(w_all[:, :], sel_all[:, :], scores_all[:, :]),
                   reads=[r_rout], writes=[r_rout])
            if dbg and hf == 0:
                P.emit("sync", lambda e: e.dma_start(rt_dbg[7], w_all[:, :]), reads=[r_rout], is_dma=True)
            P.emit("vector", lambda e: e.reduce_sum(
                ws_r[:, :], bass.AP(w_all, 0, [[512, 128], [NE, 16], [1, NE]]), axis=AX.X),
                   reads=[r_rout], writes=[r_rout])
            if dbg and hf == 0:
                P.emit("sync", lambda e: e.dma_start(rt_dbg[4][:, 176:192], ws_r[:, :]), reads=[r_rout], is_dma=True)
            P.emit("vector", lambda e: e.tensor_scalar_add(ws_r[:, :], ws_r[:, :], 1e-20), reads=[r_rout], writes=[r_rout])
            if dbg and hf == 0:
                P.emit("sync", lambda e: e.dma_start(rt_dbg[4][:, 192:208], ws_r[:, :]), reads=[r_rout], is_dma=True)
            P.emit("vector", lambda e: e.reciprocal(wr_r[:, :], ws_r[:, :]), reads=[r_rout], writes=[r_rout])
            if dbg and hf == 0:
                P.emit("sync", lambda e: e.dma_start(rt_dbg[4][:, 208:224], wr_r[:, :]), reads=[r_rout], is_dma=True)
            P.emit("vector", lambda e: e.tensor_copy(gtmp[2][:, 0:16], ws_r[:, :]), reads=[r_rout], writes=[r_rout])
            P.emit("vector", lambda e: e.tensor_copy(gtmp[2][:, 0:16], ws_r[:, :]), reads=[r_rout], writes=[r_rout])
            P.emit("vector", lambda e: e.tensor_scalar_mul(wr_r[:, :], wr_r[:, :], SCALE_MOE), reads=[r_rout], writes=[r_rout])
            if dbg and hf == 0:
                P.emit("sync", lambda e: e.dma_start(rt_dbg[4][:, 144:160], ws_r[:, :]), reads=[r_rout], is_dma=True)
                P.emit("sync", lambda e: e.dma_start(rt_dbg[4][:, 160:176], wr_r[:, :]), reads=[r_rout], is_dma=True)
            P.emit("vector", lambda e: e.tensor_copy(gtmp[1][:, 0:16], wr_r[:, :]), reads=[r_rout], writes=[r_rout])
            P.emit("vector", lambda e: e.tensor_copy(gtmp[1][:, 0:16], wr_r[:, :]), reads=[r_rout], writes=[r_rout])
            P.emit("vector", lambda e: e.tensor_tensor(
                v16e(comb_all), v16e(w_all), bass.AP(wr_r, 0, [[16, 128], [1, 16], [0, NE]]), ALU.mult),
                   reads=[r_rout], writes=[r_rout])
            if dbg and hf == 0:
                P.emit("sync", lambda e: e.dma_start(rt_dbg[5], comb_all[:, :]), reads=[r_rout], is_dma=True)
            for tcg in range(4):
                for j in range(4):
                    tc = tcg * 4 + j
                    P.emit("tensor", lambda e, tc=tc, j=j: e.transpose(
                        pb_mc[0:32, j * 128:(j + 1) * 128], comb_all[:, tc * NE:(tc + 1) * NE], id_f32[:, :]),
                           reads=[r_rout],
                           writes=[r_pmc] if j in (0, 3) else [], tag=f"ctr{tc}")
                P.emit("vector", lambda e, tcg=tcg: e.tensor_copy(
                    combine_fm[:, tcg * TG:(tcg + 1) * TG], pb_mc[0:32, :]),
                       reads=[r_pmc], writes=[r_cfm], psum=True)
            if dbg:
                P.emit("gpsimd", lambda e, hf=hf: e.dma_start(comb_dbg[hf, :, :], combine_fm[:, :]),
                       reads=[r_cfm], is_dma=True)
            scopeR.close()
            P.barrier()

            NEXP = NE + 2

            r_crow = [Res(f"crow{i}") for i in range(2)]
            r_cring = [Res(f"cring{i}") for i in range(2)]

            def emit_crow_dma(e_idx):
                if 0 <= e_idx < NE:
                    P.emit("sync", lambda e, e_idx=e_idx: e.dma_start(
                        c_row[e_idx % 2][:, :], combine_fm[e_idx:e_idx + 1, :]),
                           reads=[r_cfm], writes=[r_crow[e_idx % 2]], is_dma=True, tag=f"crow{e_idx}")

            def emit_cb_tg(e_idx, tg):
                P.emit("tensor", lambda e, e_idx=e_idx, tg=tg: e.matmul(
                    pb_mc[:, :], ones1_bf[:, :], c_row[e_idx % 2][0:1, tg * TG:(tg + 1) * TG],
                    start=True, stop=True),
                       reads=[r_const, r_crow[e_idx % 2], r_pmc], writes=[r_pmc], tag=f"cb{e_idx}_{tg}")
                P.emit("scalar", lambda e, tg=tg: e.activation(
                    c_sb[tg % 2][:, :], pb_mc[:, :], AF.Copy),
                       reads=[r_pmc], writes=[r_cring[tg % 2]], psum=True)

            def emit_wgu_dma(e_idx):
                b = e_idx % 2
                if e_idx < NE:
                    srcs = [(wgT[e_idx], wg_bf[b], r_wg[b]), (wuT[e_idx], wu_bf[b], r_wu[b])]
                else:
                    k = e_idx - NE
                    srcs = [(shgT[:, k * 512:(k + 1) * 512], wg_bf[b], r_wg[b]),
                            (shuT[:, k * 512:(k + 1) * 512], wu_bf[b], r_wu[b])]
                for src, dst, rr in srcs:
                    P.emit("gpsimd", lambda e, src=src, dst=dst: e.dma_start(
                        dst[:, :].rearrange("p (c i) -> p c i", c=HC),
                        src.rearrange("(c p) i -> p c i", p=128)),
                           writes=[rr], is_dma=True, tag=f"wgu{e_idx}")

            def emit_wd_dma(e_idx):
                if e_idx < NE:
                    wdsrc = wdT[e_idx]
                else:
                    k = e_idx - NE
                    wdsrc = shdT[k * 512:(k + 1) * 512, :]
                b = e_idx % 2
                P.emit("gpsimd", lambda e, wdsrc=wdsrc, b=b: e.dma_start(
                    wd_bf[b][:, :].rearrange("p (c o) -> p c o", c=4),
                    wdsrc.rearrange("(c p) o -> p c o", p=128)),
                       writes=[r_wd[b]], is_dma=True, tag=f"wd{e_idx}")

            def emit_gu(e_idx):
                b = e_idx % 2
                emit_crow_dma(e_idx + 1)
                shared = e_idx >= NE
                for tg in range(4):
                    if not shared:
                        emit_cb_tg(e_idx, tg)
                    for i in range(4):
                        pslot = (tg * 4 + i) % 2
                        slot = 0
                        bg, bu = pb_gu[2 * pslot], pb_gu[2 * pslot + 1]
                        rg, ru = r_pgu[2 * pslot], r_pgu[2 * pslot + 1]
                        for h in range(HC):
                            P.emit("tensor", lambda e, b=b, tg=tg, i=i, h=h, bg=bg: e.matmul(
                                bg[:, :], wg_bf[b][:, h * 512 + i * 128:h * 512 + (i + 1) * 128],
                                y1h[:, h * HALF + tg * TG:h * HALF + (tg + 1) * TG],
                                start=(h == 0), stop=(h == HC - 1)),
                                   reads=[r_wg[b], r_y1h],
                                   writes=[rg] if h in (0, HC - 1) else [], tag=f"g{e_idx}_{tg}_{i}")
                        for h in range(HC):
                            P.emit("tensor", lambda e, b=b, tg=tg, i=i, h=h, bu=bu: e.matmul(
                                bu[:, :], wu_bf[b][:, h * 512 + i * 128:h * 512 + (i + 1) * 128],
                                y1h[:, h * HALF + tg * TG:h * HALF + (tg + 1) * TG],
                                start=(h == 0), stop=(h == HC - 1)),
                                   reads=[r_wu[b], r_y1h],
                                   writes=[ru] if h in (0, HC - 1) else [], tag=f"u{e_idx}_{tg}_{i}")
                        P.emit("scalar", lambda e, slot=slot, bg=bg: e.activation(
                            sg_t[slot][:, :], bg[:, :], AF.Silu),
                               reads=[rg], writes=[r_sg[slot]], psum=True, tag=f"sg{e_idx}_{tg}_{i}")
                        if shared:
                            P.emit("vector", lambda e, slot=slot, bu=bu: e.tensor_copy(uc_t[slot][:, :], bu[:, :]),
                                   reads=[ru], writes=[r_uct[slot]], psum=True)
                        else:
                            P.emit("vector", lambda e, slot=slot, bu=bu, tg=tg: e.tensor_tensor(
                                uc_t[slot][:, :], bu[:, :], c_sb[tg % 2][:, :], ALU.mult),
                                   reads=[ru, r_cring[tg % 2]], writes=[r_uct[slot]], psum=True)
                        P.emit("vector", lambda e, slot=slot, b=b, i=i, tg=tg: e.tensor_mul(
                            guc[b][:, i * HALF + tg * TG:i * HALF + (tg + 1) * TG],
                            sg_t[slot][:, :], uc_t[slot][:, :]),
                               reads=[r_sg[slot], r_uct[slot]], writes=[r_guc[b]], tag=f"guc{e_idx}_{tg}_{i}")

            def emit_down(e_idx):
                b = e_idx % 2
                for ho in range(HC):
                    for tg in range(4):
                        slot = (ho * 4 + tg) % 3
                        bd, rd = pb_dn[slot], r_pdn[slot]
                        for i in range(4):
                            P.emit("tensor", lambda e, ho=ho, tg=tg, i=i, bd=bd, b=b: e.matmul(
                                bd[:, :], wd_bf[b][:, i * D + ho * 128:i * D + (ho + 1) * 128],
                                guc[b][:, i * HALF + tg * TG:i * HALF + (tg + 1) * TG],
                                start=(i == 0), stop=(i == 3)),
                                   reads=[r_wd[b], r_guc[b]],
                                   writes=[rd] if i in (0, 3) else [], tag=f"d{e_idx}_{ho}_{tg}_{i}")
                        P.emit("vector", lambda e, ho=ho, tg=tg, bd=bd: e.tensor_tensor(
                            accum[:, ho * HALF + tg * TG:ho * HALF + (tg + 1) * TG], bd[:, :],
                            accum[:, ho * HALF + tg * TG:ho * HALF + (tg + 1) * TG], ALU.add),
                               reads=[rd], writes=[r_acc], psum=True, tag=f"da{e_idx}_{ho}_{tg}")

            emit_wgu_dma(0)
            emit_wd_dma(0)
            emit_wgu_dma(1)
            emit_wd_dma(1)
            emit_crow_dma(0)
            emit_gu(0)
            for e_idx in range(1, NEXP):
                if e_idx + 1 < NEXP:
                    emit_wgu_dma(e_idx + 1)
                emit_gu(e_idx)
                emit_down(e_idx - 1)
                if e_idx + 1 < NEXP:
                    emit_wd_dma(e_idx + 1)
            emit_down(NEXP - 1)
            if dbg:
                for o in range(HC):
                    P.emit("sync", lambda e, hf=hf, o=o: e.dma_start(
                        acc_dbg[hf, o, :, :], accum[:, o * HALF:(o + 1) * HALF]),
                           reads=[r_acc], is_dma=True)

            scopeRE.close()
            P.barrier()
            y1f_t = sbd("y1f_t", [128, HALF], F32)
            z2b_t = sbd("z2b_t", [128, TG], BF16)
            sq2_t = sbd("sq2_t", [128, TG], BF16)
            out_t = sbd("out_t", [128, TG], F32)
            m2row = sbd("m2row", [1, HALF], F32)
            s2row = sbd("s2row", [1, HALF], F32)
            a2row = sbd("a2row", [1, HALF], F32)

            zb_t = [z2b_t, sbd("z2b_t2", [128, TG], BF16)]
            sqb_t = [sq2_t, sbd("sq2_t2", [128, TG], BF16)]
            r_zb = [r_z2b, Res("z2b2")]
            r_sqb = [r_sq2, Res("sq22")]
            for o in range(HC):
                P.emit("sync", lambda e, o=o: e.dma_start(y1f_t[:, :], y1f_sp[o, :, ts]),
                       reads=[r_y1f_sp[o]], writes=[r_y1ft], is_dma=True, tag=f"y1f{o}")
                P.emit("vector", lambda e, o=o: e.tensor_tensor(
                    accum[:, o * HALF:(o + 1) * HALF], accum[:, o * HALF:(o + 1) * HALF], y1f_t[:, :], ALU.add),
                       reads=[r_y1ft], writes=[r_acc])
                for tg in range(4):
                    sl = (o * 4 + tg) % 2
                    P.emit("vector", lambda e, o=o, tg=tg, sl=sl: e.tensor_copy(
                        zb_t[sl][:, :], accum[:, o * HALF + tg * TG:o * HALF + (tg + 1) * TG]),
                           reads=[r_acc], writes=[r_zb[sl]])
                    P.emit("scalar", lambda e, o=o, tg=tg, sl=sl: e.activation(
                        sqb_t[sl][:, :], accum[:, o * HALF + tg * TG:o * HALF + (tg + 1) * TG], AF.Square),
                           reads=[r_acc], writes=[r_sqb[sl]])
                    P.emit("tensor", lambda e, tg=tg, o=o, sl=sl: e.matmul(
                        pb_gu[tg][0:1, :], onesc_bf[:, :], zb_t[sl][:, :], start=(o == 0), stop=(o == HC - 1)),
                           reads=[r_const, r_zb[sl]],
                           writes=[r_pgu[tg]] if o in (0, HC - 1) else [], tag=f"m2_{o}_{tg}")
                    bsq = pb_dn[tg] if tg < 3 else pb_mc
                    rsq = r_pdn[tg] if tg < 3 else r_pmc
                    P.emit("tensor", lambda e, tg=tg, o=o, sl=sl, bsq=bsq: e.matmul(
                        bsq[0:1, :], onesc_bf[:, :], sqb_t[sl][:, :], start=(o == 0), stop=(o == HC - 1)),
                           reads=[r_const, r_sqb[sl]],
                           writes=[rsq] if o in (0, HC - 1) else [], tag=f"s2_{o}_{tg}")
            for tg in range(4):
                P.emit("vector", lambda e, tg=tg: e.tensor_copy(m2row[:, tg * TG:(tg + 1) * TG], pb_gu[tg][0:1, :]),
                       reads=[r_pgu[tg]], writes=[r_rows2], psum=True)
                bsq = pb_dn[tg] if tg < 3 else pb_mc
                rsq = r_pdn[tg] if tg < 3 else r_pmc
                P.emit("vector", lambda e, tg=tg, bsq=bsq: e.tensor_copy(s2row[:, tg * TG:(tg + 1) * TG], bsq[0:1, :]),
                       reads=[rsq], writes=[r_rows2], psum=True)
            P.emit("vector", lambda e: e.tensor_scalar_mul(m2row[:, :], m2row[:, :], 1.0 / D), reads=[r_rows2], writes=[r_rows2])
            P.emit("vector", lambda e: e.tensor_mul(a2row[:, :], m2row[:, :], m2row[:, :]), reads=[r_rows2], writes=[r_rows2])
            P.emit("vector", lambda e: e.scalar_tensor_tensor(s2row[:, :], s2row[:, :], 1.0 / D, a2row[:, :], ALU.mult, ALU.subtract),
                   reads=[r_rows2], writes=[r_rows2])
            P.emit("vector", lambda e: e.tensor_scalar_add(s2row[:, :], s2row[:, :], 1e-5), reads=[r_rows2], writes=[r_rows2])
            P.emit("scalar", lambda e: e.activation(a2row[:, :], s2row[:, :], AF.Ln),
                   reads=[r_rows2], writes=[r_rows2])
            P.emit("scalar", lambda e: e.activation(a2row[:, :], a2row[:, :], AF.Exp, scale=-0.5),
                   reads=[r_rows2], writes=[r_rows2])
            P.emit("vector", lambda e: e.scalar_tensor_tensor(s2row[:, :], m2row[:, :], -1.0, a2row[:, :], ALU.mult, ALU.mult),
                   reads=[r_rows2], writes=[r_rows2])
            a2_bc = sbd("a2_bc", [128, HALF], F32)
            b2_bc = sbd("b2_bc", [128, HALF], F32)
            r_abc2 = Res("abc2")
            out_ts = [out_t, sbd("out_t2", [128, TG], F32)]
            r_outs = [r_outt, Res("outt2")]
            for tg in range(4):
                P.emit("tensor", lambda e, tg=tg: e.matmul(
                    pb_gu[0][:, :], ones1_f[:, :], a2row[:, tg * TG:(tg + 1) * TG], start=True, stop=True),
                       reads=[r_const, r_rows2, r_pgu[0]], writes=[r_pgu[0]], tag=f"ba2{tg}")
                P.emit("tensor", lambda e, tg=tg: e.matmul(
                    pb_gu[1][:, :], ones1_f[:, :], s2row[:, tg * TG:(tg + 1) * TG], start=True, stop=True),
                       reads=[r_const, r_rows2, r_pgu[1]], writes=[r_pgu[1]], tag=f"bb2{tg}")
                P.emit("scalar", lambda e, tg=tg: e.activation(a2_bc[:, tg * TG:(tg + 1) * TG], pb_gu[0][:, :], AF.Copy),
                       reads=[r_pgu[0]], writes=[r_abc2], psum=True)
                P.emit("scalar", lambda e, tg=tg: e.activation(b2_bc[:, tg * TG:(tg + 1) * TG], pb_gu[1][:, :], AF.Copy),
                       reads=[r_pgu[1]], writes=[r_abc2], psum=True)
            for o in range(HC):
                for tg in range(4):
                    sl = (o * 4 + tg) % 2
                    P.emit("vector", lambda e, o=o, tg=tg, sl=sl: e.tensor_tensor(
                        out_ts[sl][:, :], accum[:, o * HALF + tg * TG:o * HALF + (tg + 1) * TG],
                        a2_bc[:, tg * TG:(tg + 1) * TG], ALU.mult),
                           reads=[r_acc, r_abc2], writes=[r_outs[sl]])
                    P.emit("vector", lambda e, tg=tg, sl=sl: e.tensor_tensor(
                        out_ts[sl][:, :], out_ts[sl][:, :], b2_bc[:, tg * TG:(tg + 1) * TG], ALU.add),
                           reads=[r_abc2], writes=[r_outs[sl]])
                    P.emit("sync", lambda e, o=o, tg=tg, hf=hf, sl=sl: e.dma_start(
                        outT[o * 128:(o + 1) * 128, hf * HALF + tg * TG:hf * HALF + (tg + 1) * TG], out_ts[sl][:, :]),
                           reads=[r_outs[sl]], is_dma=True, tag=f"out{o}_{tg}")

    for hf in range(2):
        emit_half(hf)

    if trunc is not None:
        P.ops = P.ops[:trunc]
    P.finalize(nc, stack)
    nc._stack = stack
    return nc


def kernel(L0_attn_x, L0_attn_w_qkv, L0_attn_w_proj, L0_ffn_gate_w, L0_ffn_gate_bias,
           L0_ffn_w_gate, L0_ffn_w_up, L0_ffn_w_down, L0_ffn_sh_gate, L0_ffn_sh_up,
           L0_ffn_sh_down):
    import ml_dtypes
    x = np.asarray(L0_attn_x, dtype=np.float32)
    common = {
        "wqkv": np.ascontiguousarray(np.asarray(L0_attn_w_qkv, dtype=np.float32)),
        "wproj": np.ascontiguousarray(np.asarray(L0_attn_w_proj, dtype=np.float32)),
        "gwT": np.ascontiguousarray(np.asarray(L0_ffn_gate_w, dtype=np.float32).T),
        "gbias": np.ascontiguousarray(np.asarray(L0_ffn_gate_bias, dtype=np.float32).reshape(1, NE)),
        "wgT": np.ascontiguousarray(np.asarray(L0_ffn_w_gate, dtype=np.float32).transpose(0, 2, 1)),
        "wuT": np.ascontiguousarray(np.asarray(L0_ffn_w_up, dtype=np.float32).transpose(0, 2, 1)),
        "wdT": np.ascontiguousarray(np.asarray(L0_ffn_w_down, dtype=np.float32).transpose(0, 2, 1)),
        "shgT": np.ascontiguousarray(np.asarray(L0_ffn_sh_gate, dtype=np.float32).T),
        "shuT": np.ascontiguousarray(np.asarray(L0_ffn_sh_up, dtype=np.float32).T),
        "shdT": np.ascontiguousarray(np.asarray(L0_ffn_sh_down, dtype=np.float32).T),
        "ident_bf": np.eye(128, dtype=ml_dtypes.bfloat16),
        "ident_f32": np.eye(128, dtype=np.float32),
    }
    in_maps = [dict(common, xT=np.ascontiguousarray(x[c].T)) for c in range(8)]
    nc = build_nc()
    res = run_bass_kernel_spmd(nc, in_maps, core_ids=list(range(8)))
    out = np.stack([np.asarray(r["outT"]).T for r in res.results])
    return np.ascontiguousarray(out.astype(np.float32))


# version 11
# speedup vs baseline: 1.0865x; 1.0201x over previous
import contextlib
import numpy as np
import concourse.bass as bass
import concourse.mybir as mybir
from concourse.bass_utils import run_bass_kernel_spmd

F32 = mybir.dt.float32
BF16 = mybir.dt.bfloat16
AF = mybir.ActivationFunctionType
ALU = mybir.AluOpType
AX = mybir.AxisListType

NT = 4096
D = 1024
HC = 8
NH = 16
HD = 64
NLM = 64
SEG = 64
TG = 512
NTG = NT // TG
NE = 32
EI = 512
HALF = 2048
SCALE_MOE = 2.5

ENGINES = ("tensor", "vector", "scalar", "gpsimd", "sync")


class Op:
    __slots__ = ("engine", "build", "deps", "is_dma", "tag", "needs_inc", "inc_val", "waits")

    def __init__(self, engine, build, deps, is_dma, tag):
        self.engine = engine
        self.build = build
        self.deps = deps
        self.is_dma = is_dma
        self.tag = tag
        self.needs_inc = False
        self.inc_val = None
        self.waits = []


class Res:
    __slots__ = ("w", "readers", "name")

    def __init__(self, name=""):
        self.w = None
        self.readers = []
        self.name = name


class Plan:
    def __init__(self):
        self.ops = []
        self.psum_mutex = Res("psum_sv_mutex")

    def barrier(self):
        last = {}
        for o in self.ops:
            last[o.engine] = o
        self._bar_deps = list(last.values())
        self._bar_seen = set()

    def emit(self, engine, build, reads=(), writes=(), extra=(), is_dma=False, tag="",
             psum=False):
        if getattr(self, "_bar_deps", None) is not None and engine not in self._bar_seen:
            extra = list(extra) + self._bar_deps
            self._bar_seen.add(engine)
            if len(self._bar_seen) == len(ENGINES):
                self._bar_deps = None
        if psum:
            writes = list(writes) + [self.psum_mutex]
        deps = []
        for r in reads:
            if r.w is not None:
                deps.append(r.w)
        for wr in writes:
            deps.extend(wr.readers)
            if wr.w is not None:
                deps.append(wr.w)
        for e in extra:
            if e is not None:
                deps.append(e)
        o = Op(engine, build, deps, is_dma, tag)
        self.ops.append(o)
        for r in reads:
            r.readers.append(o)
        for wr in writes:
            wr.w = o
            wr.readers = []
        return o

    def finalize(self, nc, stack):
        pos = {id(o): i for i, o in enumerate(self.ops)}
        for o in self.ops:
            best = {}
            for d in o.deps:
                key = (d.engine, d.is_dma)
                if key not in best or pos[id(best[key])] < pos[id(d)]:
                    best[key] = d
            o.deps = list(best.values())
        for o in self.ops:
            if o.is_dma:
                o.needs_inc = True
            for d in o.deps:
                if d.engine != o.engine or d.is_dma:
                    d.needs_inc = True
        counters = {}
        for o in self.ops:
            key = (o.engine, "dma" if o.is_dma else "c")
            if o.needs_inc:
                step = 16 if o.is_dma else 1
                counters[key] = counters.get(key, 0) + step
                o.inc_val = counters[key]
            else:
                o.inc_val = None
        for o in self.ops:
            wl = {}
            for d in o.deps:
                if d.engine == o.engine and not d.is_dma:
                    continue
                key = (d.engine, "dma" if d.is_dma else "c")
                assert d.inc_val is not None
                if wl.get(key, 0) < d.inc_val:
                    wl[key] = d.inc_val
            o.waits = sorted(wl.items())
        sems = {}
        for key in counters:
            sems[key] = stack.enter_context(nc.semaphore(f"sem_{key[0]}_{key[1]}"))
        per_eng = {e: [] for e in ENGINES}
        for o in self.ops:
            per_eng[o.engine].append(o)

        with nc.Block() as block:
            def mk_section(eng_name):
                def body(eng):
                    waited = {}
                    for o in per_eng[eng_name]:
                        for key, val in o.waits:
                            if waited.get(key, 0) < val:
                                eng.wait_ge(sems[key], val)
                                waited[key] = val
                        instr = o.build(eng)
                        if o.inc_val is not None:
                            key = (o.engine, "dma" if o.is_dma else "c")
                            step = 16 if o.is_dma else 1
                            instr.then_inc(sems[key], step)
                    if eng_name == "sync":
                        for key, final in sorted(counters.items()):
                            if waited.get(key, 0) < final:
                                eng.wait_ge(sems[key], final)
                    return None
                return body

            for ename in ENGINES:
                getattr(block, ename)(mk_section(ename))
        return sems, counters


def build_nc(dbg=False, trunc=None):
    nc = bass.Bass("TRN2", target_bir_lowering=False, detect_race_conditions=False)
    stack = contextlib.ExitStack()
    P = Plan()

    xT = nc.dram_tensor("xT", [D, NT], F32, kind="ExternalInput").ap()
    wqkv = nc.dram_tensor("wqkv", [D, 3 * D], F32, kind="ExternalInput").ap()
    wproj = nc.dram_tensor("wproj", [D, D], F32, kind="ExternalInput").ap()
    gwT = nc.dram_tensor("gwT", [D, NE], F32, kind="ExternalInput").ap()
    gbias = nc.dram_tensor("gbias", [1, NE], F32, kind="ExternalInput").ap()
    wgT = nc.dram_tensor("wgT", [NE, D, EI], F32, kind="ExternalInput").ap()
    wuT = nc.dram_tensor("wuT", [NE, D, EI], F32, kind="ExternalInput").ap()
    wdT = nc.dram_tensor("wdT", [NE, EI, D], F32, kind="ExternalInput").ap()
    shgT = nc.dram_tensor("shgT", [D, D], F32, kind="ExternalInput").ap()
    shuT = nc.dram_tensor("shuT", [D, D], F32, kind="ExternalInput").ap()
    shdT = nc.dram_tensor("shdT", [D, D], F32, kind="ExternalInput").ap()
    ident_bf = nc.dram_tensor("ident_bf", [128, 128], BF16, kind="ExternalInput").ap()
    ident_f32 = nc.dram_tensor("ident_f32", [128, 128], F32, kind="ExternalInput").ap()
    outT = nc.dram_tensor("outT", [D, NT], F32, kind="ExternalOutput").ap()

    def scratch(name, shape, dt):
        kind = "ExternalOutput" if dbg else "Internal"
        return nc.dram_tensor(name, shape, dt, kind=kind).ap()

    qk_sp = scratch("qk_sp", [16, 128, NT], BF16)
    v_sp = scratch("v_sp", [32, 128, D], BF16)
    z1_sp = scratch("z1_sp", [HC, 128, NT], F32)
    y1b_sp = scratch("y1b_sp", [HC, 128, NT], BF16)
    y1f_sp = scratch("y1f_sp", [HC, 128, NT], F32)
    if dbg:
        qlm_dbg = scratch("qlm_dbg", [128, 16 * NLM], F32)
        sv_dbg = scratch("sv_dbg", [HC, 128, NT], BF16)
        comb_dbg = scratch("comb_dbg", [2, NE, HALF], F32)
        acc_dbg = scratch("acc_dbg", [2, HC, 128, HALF], F32)
        e1_dbg = scratch("e1_dbg", [128, NT], BF16)
        e3_dbg = scratch("e3_dbg", [128, NT], BF16)
        k3vn_dbg = scratch("k3vn_dbg", [128, 128], BF16)
        K_dbg = scratch("K_dbg", [128, 128], BF16)
        M_dbg = scratch("M_dbg", [128, 128], BF16)
        V_dbg = scratch("V_dbg", [128, 128], BF16)
        rB_dbg = scratch("rB_dbg", [128, NT], F32)
        rt_dbg = scratch("rt_dbg", [8, 128, 512], F32)

    sb = lambda name, shape, dt: stack.enter_context(nc.sbuf_tensor(name, shape, dt))
    stack_bc = contextlib.ExitStack()
    sbBC = lambda name, shape, dt: stack_bc.enter_context(nc.sbuf_tensor(name, shape, dt, side="right"))
    qlm_all = sbBC("qlm_all", [128, 16 * NLM], F32)
    ones2w = sbBC("ones2w", [128, 128], BF16)
    blockmask = sbBC("blockmask", [128, 128], F32)
    i7 = sbBC("i7", [128, 128], F32)
    i15 = sbBC("i15", [128, 128], F32)
    i325 = sbBC("i325", [128, 128], F32)
    onesc_bf = sb("onesc_bf", [128, 1], BF16)
    onesc_f = sb("onesc_f", [128, 1], F32)
    ones1_f = sb("ones1_f", [1, 128], F32)
    ones1_bf = sb("ones1_bf", [1, 128], BF16)
    id_bf = sb("id_bf", [128, 128], BF16)
    id_f32 = sb("id_f32", [128, 128], F32)
    gb_b = sb("gb_b", [128, NE], F32)

    r_const = Res("const")

    def init_consts():
        def b(eng):
            return eng.memset(ones2w[:, :], 0.0)
        P.emit("vector", b, writes=[r_const], tag="c0")
        P.emit("vector", lambda e: e.memset(ones2w[0:64, 0:64], 1.0), writes=[r_const])
        P.emit("vector", lambda e: e.memset(ones2w[64:128, 64:128], 1.0), writes=[r_const])
        P.emit("vector", lambda e: e.memset(onesc_bf[:, :], 1.0), writes=[r_const])
        P.emit("vector", lambda e: e.memset(onesc_f[:, :], 1.0), writes=[r_const])
        P.emit("vector", lambda e: e.memset(ones1_f[:, :], 1.0), writes=[r_const])
        P.emit("vector", lambda e: e.memset(ones1_bf[:, :], 1.0), writes=[r_const])
        P.emit("vector", lambda e: e.memset(blockmask[:, :], 0.0), writes=[r_const])
        P.emit("vector", lambda e: e.memset(blockmask[0:64, 0:64], 1.0), writes=[r_const])
        P.emit("vector", lambda e: e.memset(blockmask[64:128, 64:128], 1.0), writes=[r_const])
        P.emit("gpsimd", lambda e: e.dma_start(id_bf[:, :], ident_bf), writes=[r_const], is_dma=True)
        P.emit("gpsimd", lambda e: e.dma_start(id_f32[:, :], ident_f32), writes=[r_const], is_dma=True)
        P.emit("vector", lambda e: e.tensor_scalar_mul(i7[:, :], id_f32[:, :], 7.0), reads=[r_const], writes=[r_const])
        P.emit("vector", lambda e: e.tensor_scalar_mul(i15[:, :], id_f32[:, :], 15.0), reads=[r_const], writes=[r_const])
        P.emit("vector", lambda e: e.tensor_scalar_mul(i325[:, :], id_f32[:, :], 3.25), reads=[r_const], writes=[r_const])
        P.emit("gpsimd", lambda e: e.dma_start(gb_row[:, :], gbias), writes=[r_const], is_dma=True)

    gb_row = sb("gb_row", [1, NE], F32)

    with contextlib.ExitStack() as pha:
        x_bf = pha.enter_context(nc.sbuf_tensor("x_bf", [128, HC * NT], BF16))
        wq_bf = pha.enter_context(nc.sbuf_tensor("wq_bf", [128, HC * 3 * D], BF16))
        stage_qk = [pha.enter_context(nc.sbuf_tensor(f"stage_qk{i}", [128, NT], BF16)) for i in range(2)]
        stage_v = [pha.enter_context(nc.sbuf_tensor(f"stage_v{i}", [128, D], BF16)) for i in range(2)]
        qk_banks = [pha.enter_context(nc.psum_tensor(f"qkb{i}", [128, TG], F32)) for i in range(4)]
        v_banks = [pha.enter_context(nc.psum_tensor(f"vb{i}", [128, D], F32)) for i in range(2)]

        r_xbf = Res("xbf")
        r_wqbf = Res("wqbf")
        r_stage_qk = [Res(f"stqk{i}") for i in range(2)]
        r_stage_v = [Res(f"stv{i}") for i in range(2)]
        r_qkbank = [Res(f"qkb{i}") for i in range(4)]
        r_vbank = [Res(f"vb{i}") for i in range(2)]
        r_qlm = Res("qlm")
        r_qk_sp = [Res(f"qksp{m}") for m in range(16)]
        r_v_sp = [Res(f"vsp{t}") for t in range(32)]

        init_consts()

        r_xc = [Res(f"xc{h}") for h in range(HC)]
        r_wc = [Res(f"wc{h}") for h in range(HC)]
        for h in range(HC):
            P.emit("gpsimd",
                   lambda e, h=h: e.dma_start(x_bf[:, h * NT:(h + 1) * NT],
                                              xT[h * 128:(h + 1) * 128, :]),
                   writes=[r_xc[h], r_xbf], is_dma=True, tag=f"dma_x{h}")
            P.emit("gpsimd",
                   lambda e, h=h: e.dma_start(wq_bf[:, h * 3 * D:(h + 1) * 3 * D],
                                              wqkv[h * 128:(h + 1) * 128, :]),
                   writes=[r_wc[h], r_wqbf], is_dma=True, tag=f"dma_wqkv{h}")

        for m in range(16):
            st = r_stage_qk[m % 2]
            stt = stage_qk[m % 2]
            for tg in range(NTG):
                bk = r_qkbank[(m * NTG + tg) % 4]
                bkt = qk_banks[(m * NTG + tg) % 4]
                for h in range(HC):
                    def mm(e, h=h, m=m, tg=tg, bkt=bkt):
                        return e.matmul(bkt[:, :],
                                        wq_bf[:, h * 3 * D + m * 128:h * 3 * D + (m + 1) * 128],
                                        x_bf[:, h * NT + tg * TG:h * NT + (tg + 1) * TG],
                                        start=(h == 0), stop=(h == HC - 1))
                    P.emit("tensor", mm,
                           reads=[r_xc[h], r_wc[h]],
                           writes=[bk] if h in (0, HC - 1) else [], tag=f"qk_mm{m}_{tg}_{h}")
                sc = 0.125 if m < 8 else 1.0
                def ev(e, bkt=bkt, stt=stt, tg=tg, sc=sc):
                    return e.activation(stt[:, tg * TG:(tg + 1) * TG], bkt[:, :], AF.Copy, scale=sc)
                P.emit("scalar", ev, reads=[bk], writes=[st], tag=f"qk_ev{m}_{tg}", psum=True)
                def lm(e, bkt=bkt, m=m, tg=tg):
                    return e.reduce_sum(qlm_all[:, m * NLM + tg * 8:m * NLM + tg * 8 + 8],
                                        bkt[:, :].rearrange("p (s j) -> p s j", j=SEG), axis=AX.X)
                P.emit("vector", lm, reads=[bk], writes=[r_qlm], tag=f"qk_lm{m}_{tg}", psum=True)
            def sp(e, stt=stt, m=m):
                return e.dma_start(qk_sp[m, :, :], stt[:, :])
            P.emit("sync", sp, reads=[st], writes=[r_qk_sp[m]], is_dma=True, tag=f"qk_sp{m}")

        for tc in range(32):
            bk = r_vbank[tc % 2]
            bkt = v_banks[tc % 2]
            for h in range(HC):
                for half in range(2):
                    def mmv(e, h=h, tc=tc, half=half, bkt=bkt):
                        return e.matmul(bkt[:, half * TG:(half + 1) * TG],
                                        x_bf[:, h * NT + tc * 128:h * NT + (tc + 1) * 128],
                                        wq_bf[:, h * 3 * D + 2 * D + half * TG:h * 3 * D + 2 * D + (half + 1) * TG],
                                        start=(h == 0), stop=(h == HC - 1))
                    P.emit("tensor", mmv,
                           reads=[r_xc[h], r_wc[h]],
                           writes=[bk] if ((h == 0 and half == 0) or (h == HC - 1 and half == 1)) else [], tag=f"v_mm{tc}_{h}_{half}")
            def evv(e, bkt=bkt, tc=tc):
                return e.activation(stage_v[tc % 2][:, :], bkt[:, :], AF.Copy)
            P.emit("scalar", evv, reads=[bk], writes=[r_stage_v[tc % 2]], tag=f"v_ev{tc}", psum=True)
            def spv(e, tc=tc):
                return e.dma_start(v_sp[tc, :, :], stage_v[tc % 2][:, :])
            P.emit("sync", spv, reads=[r_stage_v[tc % 2]], writes=[r_v_sp[tc]], is_dma=True, tag=f"v_sp{tc}")

        if dbg:
            P.emit("sync", lambda e: e.dma_start(qlm_dbg, qlm_all[:, :]), reads=[r_qlm], is_dma=True)

    P.barrier()
    sv_bf = stack_bc.enter_context(nc.sbuf_tensor("sv_bf", [128, HC * NT], BF16, side="right"))
    r_sv = [Res(f"sv{p}") for p in range(HC)]
    with contextlib.ExitStack() as phb:
        sbb = lambda name, shape, dt: phb.enter_context(nc.sbuf_tensor(name, shape, dt))
        q_pair = [sbb(f"q_pair{i}", [128, NT], BF16) for i in range(2)]
        k_pair = [sbb("k_pair0", [128, NT], BF16)]
        v_pair = sbb("v_pair", [128, NT], BF16)
        expk3 = sbb("expk3", [128, NT], BF16)
        expk1 = [sbb(f"expk1_{i}", [128, NT], BF16) for i in range(4)]
        rb_tmp = [sbb(f"rb_tmp{i}", [128, TG], F32) for i in range(2)]
        rb_rec = [sbb(f"rb_rec{i}", [128, TG], F32) for i in range(2)]
        recipB = [sbb(f"recipB{i}", [128, NT], BF16) for i in range(4)]
        qlm_bf = [sbb(f"qlm_bf{i}", [128, NLM], BF16) for i in range(2)]
        klm_bf = [sbb(f"klm_bf{i}", [128, NLM], BF16) for i in range(2)]
        qlm_bd4 = [sbb(f"qlm_bd4_{g}", [128, 512], BF16) for g in range(2)]
        klm_bd4 = [sbb(f"klm_bd4_{g}", [128, 512], BF16) for g in range(2)]
        k3vn4 = [sbb(f"k3vn4_{g}", [128, 512], BF16) for g in range(2)]
        recip3 = [sbb(f"recip3_{i}", [128, 1], F32) for i in range(2)]
        kexp4 = sbb("kexp4", [128, 512], F32)
        K4 = sbb("K4", [128, 512], BF16)
        KT4 = sbb("KT4", [128, 512], BF16)
        KV4 = sbb("KV4", [128, 512], BF16)
        KVT4 = sbb("KVT4", [128, 512], BF16)
        C14 = sbb("C14", [128, 512], BF16)
        C1T4 = sbb("C1T4", [128, 512], BF16)
        C24 = sbb("C24", [128, 512], BF16)
        C2T4 = sbb("C2T4", [128, 512], BF16)
        C34 = sbb("C34", [128, 512], BF16)
        C3T4 = sbb("C3T4", [128, 512], BF16)
        Vb4 = [sbb(f"Vb4_{i}", [128, 512], BF16) for i in range(2)]
        VTb4 = [sbb(f"VTb4_{i}", [128, 512], BF16) for i in range(2)]
        M4 = sbb("M4", [128, 512], BF16)
        s2sum4 = sbb("s2sum4", [128, 4], F32)
        s2rec4 = sbb("s2rec4", [128, 4], F32)
        dvec4 = sbb("dvec4", [128, 4], F32)
        drec4 = sbb("drec4", [128, 4], F32)
        vjunk = sbb("vjunk", [128, 4], F32)

        pb_big = [phb.enter_context(nc.psum_tensor(f"pbig{i}", [128, TG], F32)) for i in range(2)]
        pb_s3 = [phb.enter_context(nc.psum_tensor(f"ps3{i}", [128, TG], F32)) for i in range(2)]
        pb_k3v = phb.enter_context(nc.psum_tensor("pk3v", [128, 128], F32))
        pb_ssum = phb.enter_context(nc.psum_tensor("pssum", [128, 1], F32))
        pb_inv4 = phb.enter_context(nc.psum_tensor("pinv4", [128, 512], F32))
        pb_tr4 = phb.enter_context(nc.psum_tensor("ptr4", [128, 1024], BF16))

        r_qp = [Res(f"qp{i}") for i in range(2)]
        r_kp = [Res("kp0")]
        r_vp = Res("vp")
        r_e3 = Res("e3")
        r_e1 = [Res(f"e1_{i}") for i in range(4)]
        r_rbt = [Res(f"rbt{i}") for i in range(2)]
        r_rbr = [Res(f"rbr{i}") for i in range(2)]
        r_rB = [Res(f"rB{i}") for i in range(4)]
        r_lmb = [Res(f"lmb{i}") for i in range(2)]
        r_bd = [Res(f"bd{g}") for g in range(2)]
        r_k3vn = [Res(f"k3vn{g}") for g in range(2)]
        r_r3 = [Res(f"r3_{i}") for i in range(2)]
        r_big = [Res(f"big{i}") for i in range(2)]
        r_s3 = [Res(f"s3{i}") for i in range(2)]
        r_k3vb = Res("k3vb"); r_ssum = Res("ssum"); r_inv4 = Res("inv4"); r_tr4 = Res("tr4")
        r_kexp = Res("kexp"); r_K = Res("K"); r_KT = Res("KT")
        r_KV = Res("KV"); r_KVT = Res("KVT")
        r_C = [Res(f"C{i}") for i in range(6)]
        r_V = [Res("V0"), Res("V1")]; r_VT = [Res("VT0"), Res("VT1")]
        r_M = Res("M")
        r_srow = [Res(f"srow{i}") for i in range(4)]

        for g in range(2):
            P.emit("vector", lambda e, g=g: e.memset(qlm_bd4[g][:, :], 0.0), writes=[r_bd[g]])
            P.emit("vector", lambda e, g=g: e.memset(klm_bd4[g][:, :], 0.0), writes=[r_bd[g]])
        P.emit("vector", lambda e: e.memset(K4[:, :], 0.0), writes=[r_K])

        def front_thunks(p):
            a = p % 2
            s = p % 4
            g = p // 4
            jo = 128 * s
            T = []
            T.append(lambda: P.emit("sync", lambda e: e.dma_start(q_pair[a][:, :], qk_sp[p, :, :]),
                                    reads=[r_qk_sp[p]], writes=[r_qp[a]], is_dma=True))
            T.append(lambda: P.emit("sync", lambda e: e.dma_start(k_pair[0][:, :], qk_sp[8 + p, :, :]),
                                    reads=[r_qk_sp[8 + p]], writes=[r_kp[0]], is_dma=True))
            T.append(lambda: P.emit("sync", lambda e: e.dma_start(
                v_pair[:, :].rearrange("p (c j) -> p c j", c=32),
                v_sp[:, :, p * 128:(p + 1) * 128].rearrange("c p j -> p c j")),
                                    reads=r_v_sp, writes=[r_vp], is_dma=True))

            def lm_thunk():
                P.emit("vector", lambda e: e.tensor_scalar_mul(qlm_bf[a][:, :], qlm_all[:, p * NLM:(p + 1) * NLM], 1.0 / 512.0),
                       reads=[r_qlm], writes=[r_lmb[a]])
                P.emit("vector", lambda e: e.tensor_scalar_mul(klm_bf[a][:, :], qlm_all[:, (8 + p) * NLM:(9 + p) * NLM], 1.0 / 64.0),
                       reads=[r_qlm], writes=[r_lmb[a]])
                for half in range(2):
                    sl = slice(64 * half, 64 * half + 64)
                    P.emit("vector", lambda e, sl=sl: e.tensor_copy(qlm_bd4[g][sl, jo + sl.start:jo + sl.start + 64], qlm_bf[a][sl, :]),
                           reads=[r_lmb[a]], writes=[r_bd[g]])
                    P.emit("vector", lambda e, sl=sl: e.tensor_copy(klm_bd4[g][sl, jo + sl.start:jo + sl.start + 64], klm_bf[a][sl, :]),
                           reads=[r_lmb[a]], writes=[r_bd[g]])
            T.append(lm_thunk)

            def s3_thunk(tcg):
                bk, rbk = pb_s3[tcg % 2], r_s3[tcg % 2]
                for j in range(4):
                    tc = tcg * 4 + j
                    P.emit("tensor", lambda e, tc=tc, j=j, bk=bk: e.matmul(
                        bk[:, j * 128:(j + 1) * 128], k_pair[0][:, tc * 128:(tc + 1) * 128],
                        qlm_bd4[g][:, jo:jo + 128], start=True, stop=True),
                           reads=[r_kp[0], r_bd[g]],
                           writes=[rbk] if j in (0, 3) else [], tag=f"s3_{p}_{tc}")
                P.emit("scalar", lambda e, tcg=tcg, bk=bk: e.activation(
                    expk3[:, tcg * TG:(tcg + 1) * TG], bk[:, :], AF.Exp),
                       reads=[rbk], writes=[r_e3], psum=True, tag=f"s3e{p}_{tcg}")
            for tcg in range(8):
                T.append(lambda tcg=tcg: s3_thunk(tcg))

            def k3v_thunk(gg):
                for tc in range(gg * 8, gg * 8 + 8):
                    P.emit("tensor", lambda e, tc=tc: e.matmul(
                        pb_k3v[:, :], expk3[:, tc * 128:(tc + 1) * 128], v_pair[:, tc * 128:(tc + 1) * 128],
                        start=(tc == 0), stop=(tc == 31)),
                           reads=[r_e3, r_vp],
                           writes=[r_k3vb] if tc in (0, 31) else [], tag=f"k3v{p}_{tc}")
                    P.emit("tensor", lambda e, tc=tc: e.matmul(
                        pb_ssum[:, :], expk3[:, tc * 128:(tc + 1) * 128], onesc_bf[:, :],
                        start=(tc == 0), stop=(tc == 31)),
                           reads=[r_e3, r_const],
                           writes=[r_ssum] if tc in (0, 31) else [], tag=f"s3s{p}_{tc}")
            for gg in range(4):
                T.append(lambda gg=gg: k3v_thunk(gg))

            def k3vn_thunk():
                P.emit("vector", lambda e: e.reciprocal(recip3[a][:, :], pb_ssum[:, :]),
                       reads=[r_ssum], writes=[r_r3[a]], psum=True)
                P.emit("vector", lambda e: e.tensor_copy(vjunk[:, 0:1], recip3[a][:, :]), reads=[r_r3[a]])
                P.emit("vector", lambda e: e.tensor_copy(vjunk[:, 0:1], recip3[a][:, :]), reads=[r_r3[a]])
                P.emit("vector", lambda e: e.tensor_tensor(k3vn4[g][:, jo:jo + 128], pb_k3v[:, :],
                                                           bass.AP(recip3[a], 0, [[1, 128], [0, 128]]), ALU.mult),
                       reads=[r_k3vb, r_r3[a]], writes=[r_k3vn[g]], psum=True)
            T.append(k3vn_thunk)

            def s1_thunk(tg):
                bk, rbk = pb_big[tg % 2], r_big[tg % 2]
                P.emit("tensor", lambda e, tg=tg, bk=bk: e.matmul(
                    bk[:, :], klm_bd4[g][:, jo:jo + 128], q_pair[a][:, tg * TG:(tg + 1) * TG], start=True, stop=True),
                       reads=[r_bd[g], r_qp[a], rbk], writes=[rbk], tag=f"s1_{p}_{tg}")
                P.emit("scalar", lambda e, tg=tg, bk=bk: e.activation(
                    expk1[s][:, tg * TG:(tg + 1) * TG], bk[:, :], AF.Exp),
                       reads=[rbk], writes=[r_e1[s]], psum=True)
            for tg in range(NTG):
                T.append(lambda tg=tg: s1_thunk(tg))

            def rs_thunk(tg):
                bk, rbk = pb_big[tg % 2], r_big[tg % 2]
                P.emit("tensor", lambda e, tg=tg, bk=bk: e.matmul(
                    bk[:, :], ones2w[:, :], expk1[s][:, tg * TG:(tg + 1) * TG], start=True, stop=True),
                       reads=[r_const, r_e1[s], rbk], writes=[rbk], tag=f"rs{p}_{tg}")
                P.emit("scalar", lambda e, tg=tg, bk=bk: e.activation(
                    rb_tmp[tg % 2][:, :], bk[:, :], AF.Copy),
                       reads=[rbk], writes=[r_rbt[tg % 2]], psum=True)
                P.emit("vector", lambda e, tg=tg: e.reciprocal(rb_rec[tg % 2][:, :], rb_tmp[tg % 2][:, :]),
                       reads=[r_rbt[tg % 2]], writes=[r_rbr[tg % 2]])
                if tg > 0:
                    round_rb(tg - 1)
                if tg == NTG - 1:
                    round_rb(tg)

            def round_rb(tg):
                def b(e, tg=tg):
                    with nc.allow_low_precision(reason="recipB is a bf16 matmul-side scale"):
                        return e.tensor_copy(recipB[s][:, tg * TG:(tg + 1) * TG], rb_rec[tg % 2][:, :])
                P.emit("vector", b, reads=[r_rbr[tg % 2]], writes=[r_rB[s]])
            for tg in range(NTG):
                T.append(lambda tg=tg: rs_thunk(tg))
            return T

        def mid_thunks(g):
            T = []

            def s2_thunk():
                for j in range(4):
                    P.emit("tensor", lambda e, j=j: e.matmul(
                        pb_inv4[:, j * 128:(j + 1) * 128], qlm_bd4[g][:, j * 128:(j + 1) * 128],
                        klm_bd4[g][:, j * 128:(j + 1) * 128], start=True, stop=True),
                           reads=[r_bd[g]],
                           writes=[r_inv4] if j in (0, 3) else [], tag=f"s2mm{g}_{j}")
                P.emit("scalar", lambda e: e.activation(kexp4[:, :], pb_inv4[:, :], AF.Exp),
                       reads=[r_inv4], writes=[r_kexp], psum=True)
            T.append(s2_thunk)

            def dview(t_, half, inner=64):
                return bass.AP(t_, 64 * half * (t_.shape[1] if hasattr(t_, "shape") else 512) // 1, [])

            def ksm_thunk():
                for half in range(2):
                    P.emit("vector", lambda e, half=half: e.reduce_sum(
                        s2sum4[64 * half:64 * half + 64, :],
                        bass.AP(kexp4, 64 * half * 512 // 128 * 128 + 64 * half, [[512, 64], [128, 4], [1, 64]]),
                        axis=AX.X),
                           reads=[r_kexp], writes=[r_srow[0]])
                P.emit("vector", lambda e: e.reciprocal(s2rec4[:, :], s2sum4[:, :]),
                       reads=[r_srow[0]], writes=[r_srow[1]])
                P.emit("vector", lambda e: e.tensor_copy(vjunk[:, :], s2rec4[:, :]), reads=[r_srow[1]])
                P.emit("vector", lambda e: e.tensor_copy(vjunk[:, :], s2rec4[:, :]), reads=[r_srow[1]])
                for half in range(2):
                    P.emit("vector", lambda e, half=half: e.tensor_tensor(
                        bass.AP(K4, 64 * half * 512 + 64 * half, [[512, 64], [128, 4], [1, 64]]),
                        bass.AP(kexp4, 64 * half * 512 + 64 * half, [[512, 64], [128, 4], [1, 64]]),
                        bass.AP(s2rec4, 64 * half * 4, [[4, 64], [1, 4], [0, 64]]), ALU.mult),
                           reads=[r_kexp, r_srow[1]], writes=[r_K])
            T.append(ksm_thunk)

            def kt_thunk():
                for j in range(4):
                    P.emit("tensor", lambda e, j=j: e.transpose(
                        pb_tr4[:, j * 128:(j + 1) * 128], K4[:, j * 128:(j + 1) * 128], id_bf[:, :]),
                           reads=[r_K, r_const],
                           writes=[r_tr4] if j in (0, 3) else [], tag=f"ktr{g}_{j}")
                P.emit("vector", lambda e: e.tensor_copy(KT4[:, :], pb_tr4[:, 0:512]),
                       reads=[r_tr4], writes=[r_KT], psum=True)
            T.append(kt_thunk)

            def den_thunk():
                for j in range(4):
                    P.emit("tensor", lambda e, j=j: e.matmul(
                        pb_inv4[:, j * 128:(j + 1) * 128], ones2w[:, :], K4[:, j * 128:(j + 1) * 128],
                        start=True, stop=True),
                           reads=[r_K, r_const],
                           writes=[r_inv4] if j in (0, 3) else [], tag=f"den{g}_{j}")
                for half in range(2):
                    P.emit("vector", lambda e, half=half: e.reduce_max(
                        dvec4[64 * half:64 * half + 64, :],
                        bass.AP(pb_inv4, 64 * half * 512 + 64 * half, [[512, 64], [128, 4], [1, 64]]),
                        axis=AX.X),
                           reads=[r_inv4], writes=[r_srow[2]], psum=True)
                P.emit("vector", lambda e: e.reciprocal(drec4[:, :], dvec4[:, :]),
                       reads=[r_srow[2]], writes=[r_srow[3]])
                P.emit("vector", lambda e: e.tensor_copy(vjunk[:, :], drec4[:, :]), reads=[r_srow[3]])
                P.emit("vector", lambda e: e.tensor_copy(vjunk[:, :], drec4[:, :]), reads=[r_srow[3]])
                P.emit("vector", lambda e: e.tensor_tensor(
                    Vb4[0][:, :].rearrange("p (a b) -> p a b", a=4),
                    KT4[:, :].rearrange("p (a b) -> p a b", a=4),
                    bass.AP(drec4, 0, [[4, 128], [1, 4], [0, 128]]), ALU.mult),
                       reads=[r_KT, r_srow[3]], writes=[r_V[0]])
                P.emit("vector", lambda e: e.tensor_tensor(
                    VTb4[0][:, :].rearrange("p (a b) -> p a b", a=4),
                    K4[:, :].rearrange("p (a b) -> p a b", a=4),
                    bass.AP(drec4, 0, [[4, 128], [1, 4], [0, 128]]), ALU.mult),
                       reads=[r_K, r_srow[3]], writes=[r_VT[0]])
            T.append(den_thunk)

            def mm4_evac(lhs4, rhs4, lhs_r, rhs_r, out_sb, out_r, kind, sc=None, iN=None, tagn=""):
                for j in range(4):
                    P.emit("tensor", lambda e, j=j, lhs4=lhs4, rhs4=rhs4: e.matmul(
                        pb_inv4[:, j * 128:(j + 1) * 128], lhs4[:, j * 128:(j + 1) * 128],
                        rhs4[:, j * 128:(j + 1) * 128], start=True, stop=True),
                           reads=[lhs_r, rhs_r],
                           writes=[r_inv4] if j in (0, 3) else [], tag=f"{tagn}_{j}")
                if kind == "copy":
                    P.emit("vector", lambda e, out_sb=out_sb: e.tensor_copy(out_sb[:, :], pb_inv4[:, :]),
                           reads=[r_inv4], writes=[out_r], psum=True)
                else:
                    P.emit("vector", lambda e, out_sb=out_sb, sc=sc, iN=iN: e.scalar_tensor_tensor(
                        out_sb[:, :].rearrange("p (a b) -> p a b", a=4),
                        pb_inv4[:, :].rearrange("p (a b) -> p a b", a=4), sc,
                        bass.AP(iN, 0, [[128, 128], [0, 4], [1, 128]]), ALU.mult, ALU.add),
                           reads=[r_inv4, r_const], writes=[out_r], psum=True)

            for it in range(6):
                cur, nxt = it % 2, (it + 1) % 2
                def iter_thunkA(cur=cur, it=it):
                    mm4_evac(KT4, Vb4[cur], r_KT, r_V[cur], KV4, r_KV, "copy", tagn=f"kv{g}_{it}")
                    mm4_evac(Vb4[cur], KT4, r_V[cur], r_KT, KVT4, r_KVT, "copy", tagn=f"kvt{g}_{it}")
                    P.emit("vector", lambda e: e.scalar_tensor_tensor(
                        C14[:, :].rearrange("p (a b) -> p a b", a=4),
                        KV4[:, :].rearrange("p (a b) -> p a b", a=4), -1.0,
                        bass.AP(i7, 0, [[128, 128], [0, 4], [1, 128]]), ALU.mult, ALU.add),
                           reads=[r_KV, r_const], writes=[r_C[0]])
                    P.emit("vector", lambda e: e.scalar_tensor_tensor(
                        C1T4[:, :].rearrange("p (a b) -> p a b", a=4),
                        KVT4[:, :].rearrange("p (a b) -> p a b", a=4), -1.0,
                        bass.AP(i7, 0, [[128, 128], [0, 4], [1, 128]]), ALU.mult, ALU.add),
                           reads=[r_KVT, r_const], writes=[r_C[1]])
                def iter_thunkB(it=it):
                    mm4_evac(KVT4, C14, r_KVT, r_C[0], C24, r_C[2], "stt", sc=-1.0, iN=i15, tagn=f"b{g}_{it}")
                    mm4_evac(C14, KVT4, r_C[0], r_KVT, C2T4, r_C[3], "stt", sc=-1.0, iN=i15, tagn=f"bt{g}_{it}")
                def iter_thunkC(it=it):
                    mm4_evac(KVT4, C24, r_KVT, r_C[2], C34, r_C[4], "stt", sc=-0.25, iN=i325, tagn=f"d{g}_{it}")
                    mm4_evac(C2T4, KVT4, r_C[3], r_KVT, C3T4, r_C[5], "stt", sc=-0.25, iN=i325, tagn=f"dt{g}_{it}")
                def iter_thunkD(cur=cur, nxt=nxt, it=it):
                    mm4_evac(VTb4[cur], C34, r_VT[cur], r_C[4], Vb4[nxt], r_V[nxt], "copy", tagn=f"vn{g}_{it}")
                    mm4_evac(C34, VTb4[cur], r_C[4], r_VT[cur], VTb4[nxt], r_VT[nxt], "copy", tagn=f"vnt{g}_{it}")
                T.extend([iter_thunkA, iter_thunkB, iter_thunkC, iter_thunkD])
            fin = 6 % 2

            def m_thunk():
                for j in range(4):
                    P.emit("tensor", lambda e, j=j, fin=fin: e.matmul(
                        pb_inv4[:, j * 128:(j + 1) * 128], VTb4[fin][:, j * 128:(j + 1) * 128],
                        k3vn4[g][:, j * 128:(j + 1) * 128], start=True, stop=True),
                           reads=[r_VT[fin], r_k3vn[g]],
                           writes=[r_inv4] if j in (0, 3) else [], tag=f"Mmm{g}_{j}")
                P.emit("vector", lambda e: e.tensor_tensor(
                    M4[:, :].rearrange("p (a b) -> p a b", a=4),
                    pb_inv4[:, :].rearrange("p (a b) -> p a b", a=4),
                    bass.AP(blockmask, 0, [[128, 128], [0, 4], [1, 128]]), ALU.mult),
                       reads=[r_inv4, r_const], writes=[r_M], psum=True)
            T.append(m_thunk)
            return T

        def back_thunks(p):
            s = p % 4
            jo = 128 * s
            T = []
            def svu_thunk(tg):
                bk, rbk = pb_big[tg % 2], r_big[tg % 2]
                P.emit("tensor", lambda e, tg=tg, bk=bk: e.matmul(
                    bk[:, :], M4[:, jo:jo + 128], expk1[s][:, tg * TG:(tg + 1) * TG], start=True, stop=True),
                       reads=[r_M, r_e1[s], rbk], writes=[rbk], tag=f"svu{p}_{tg}")
                P.emit("vector", lambda e, tg=tg, bk=bk: e.tensor_tensor(
                    sv_bf[:, p * NT + tg * TG:p * NT + (tg + 1) * TG], bk[:, :],
                    recipB[s][:, tg * TG:(tg + 1) * TG], ALU.mult),
                       reads=[rbk, r_rB[s]], writes=[r_sv[p]], psum=True)
            for tg in range(NTG):
                T.append(lambda tg=tg: svu_thunk(tg))
            if dbg and p == 0:
                def dbg_thunk():
                    P.emit("sync", lambda e: e.dma_start(e1_dbg, expk1[0][:, :]), reads=[r_e1[0]], is_dma=True)
                    P.emit("sync", lambda e: e.dma_start(k3vn_dbg, k3vn4[0][:, 0:128]), reads=[r_k3vn[0]], is_dma=True)
                    P.emit("sync", lambda e: e.dma_start(K_dbg, K4[:, 0:128]), reads=[r_K], is_dma=True)
                    P.emit("sync", lambda e: e.dma_start(M_dbg, M4[:, 0:128]), reads=[r_M], is_dma=True)

                T.append(dbg_thunk)
            return T

        def zip_emit(main, filler):
            fi = 0
            for mi, mth in enumerate(main):
                mth()
                while fi * len(main) < (mi + 1) * len(filler):
                    filler[fi]()
                    fi += 1
            while fi < len(filler):
                filler[fi]()
                fi += 1

        for p in range(4):
            for th in front_thunks(p):
                th()
        fl = []
        for p in range(4, 8):
            fl.extend(front_thunks(p))
        zip_emit(mid_thunks(0), [])
        bl = []
        for p in range(4):
            bl.extend(back_thunks(p))
        zip_emit(bl, fl)
        zip_emit(mid_thunks(1), [])
        for p in range(4, 8):
            for th in back_thunks(p):
                th()

        if dbg:
            for p in range(HC):
                P.emit("sync", lambda e, p=p: e.dma_start(sv_dbg[p, :, :], sv_bf[:, p * NT:(p + 1) * NT]),
                       reads=[r_sv[p]], is_dma=True)

    P.barrier()
    with contextlib.ExitStack() as phc:
        sbc = lambda name, shape, dt: phc.enter_context(nc.sbuf_tensor(name, shape, dt))
        wp_bf = sbc("wp_bf", [128, HC * D], BF16)
        x_res = sbc("x_res", [128, NT], F32)
        z1_buf = sbc("z1_buf", [128, NT], F32)
        z1b_buf = sbc("z1b_buf", [128, TG], BF16)
        sq_buf = sbc("sq_buf", [128, TG], BF16)
        y1f_buf = sbc("y1f_buf", [128, HALF], F32)
        y1b_buf = sbc("y1b_buf", [128, HALF], BF16)
        mrow = sbc("mrow", [1, NT], F32)
        srow = sbc("srow", [1, NT], F32)
        arow = sbc("arow", [1, NT], F32)

        pb_proj = [phc.enter_context(nc.psum_tensor(f"pproj{i}", [128, TG], F32)) for i in range(2)]
        pb_mean = phc.enter_context(nc.psum_tensor("pmean", [1, TG], F32))
        pb_sq = phc.enter_context(nc.psum_tensor("psq", [1, TG], F32))
        pb_a = phc.enter_context(nc.psum_tensor("pa", [128, TG], F32))
        pb_b = phc.enter_context(nc.psum_tensor("pb", [128, TG], F32))

        r_wp = Res("wp"); r_xres = Res("xres"); r_z1 = Res("z1"); r_z1b = Res("z1b")
        r_sq = Res("sq"); r_y1f = Res("y1f"); r_y1b = Res("y1b")
        r_proj = [Res(f"proj{i}") for i in range(2)]
        r_pmean = Res("pmean"); r_psq = Res("psq"); r_pa = Res("pa"); r_pb = Res("pb")
        r_rows = Res("rows")
        r_z1_sp = [Res(f"z1sp{o}") for o in range(HC)]
        r_y1f_sp = [Res(f"y1fsp{o}") for o in range(HC)]
        r_y1b_sp = [Res(f"y1bsp{o}") for o in range(HC)]

        P.emit("gpsimd", lambda e: e.dma_start(wp_bf[:, :].rearrange("p (c o) -> p c o", c=HC),
                                               wproj.rearrange("(c p) o -> p c o", p=128)),
               writes=[r_wp], is_dma=True)
        P.emit("vector", lambda e: e.memset(mrow[:, :], 0.0), writes=[r_rows])
        P.emit("vector", lambda e: e.memset(srow[:, :], 0.0), writes=[r_rows])

        for o in range(HC):
            P.emit("sync", lambda e, o=o: e.dma_start(x_res[:, :], xT[o * 128:(o + 1) * 128, :]),
                   writes=[r_xres], is_dma=True, tag=f"xr{o}")
            for tg in range(NTG):
                bk, rbk = pb_proj[tg % 2], r_proj[tg % 2]
                for c in range(HC):
                    P.emit("tensor", lambda e, o=o, tg=tg, c=c, bk=bk: e.matmul(
                        bk[:, :], wp_bf[:, c * D + o * 128:c * D + (o + 1) * 128],
                        sv_bf[:, c * NT + tg * TG:c * NT + (tg + 1) * TG],
                        start=(c == 0), stop=(c == HC - 1)),
                           reads=[r_wp, r_sv[c]],
                           writes=[rbk] if c in (0, HC - 1) else [], tag=f"pj{o}_{tg}_{c}")
                P.emit("vector", lambda e, tg=tg, bk=bk: e.scalar_tensor_tensor(
                    z1_buf[:, tg * TG:(tg + 1) * TG], bk[:, :], 1.0,
                    x_res[:, tg * TG:(tg + 1) * TG], ALU.mult, ALU.add),
                       reads=[rbk, r_xres], writes=[r_z1], psum=True, tag=f"z1_{o}_{tg}")
                P.emit("scalar", lambda e, tg=tg: e.activation(
                    sq_buf[:, :], z1_buf[:, tg * TG:(tg + 1) * TG], AF.Square),
                       reads=[r_z1], writes=[r_sq])
                P.emit("tensor", lambda e, tg=tg: e.matmul(
                    pb_mean[:, :], onesc_f[:, :], z1_buf[:, tg * TG:(tg + 1) * TG], start=True, stop=True),
                       reads=[r_const, r_z1, r_pmean], writes=[r_pmean], tag=f"mn{o}_{tg}")
                P.emit("vector", lambda e, tg=tg: e.tensor_tensor(
                    mrow[:, tg * TG:(tg + 1) * TG], pb_mean[0:1, :], mrow[:, tg * TG:(tg + 1) * TG], ALU.add),
                       reads=[r_pmean, r_rows], writes=[r_rows], psum=True)
                P.emit("tensor", lambda e, tg=tg: e.matmul(
                    pb_sq[:, :], onesc_bf[:, :], sq_buf[:, :], start=True, stop=True),
                       reads=[r_const, r_sq, r_psq], writes=[r_psq], tag=f"sq{o}_{tg}")
                P.emit("vector", lambda e, tg=tg: e.tensor_tensor(
                    srow[:, tg * TG:(tg + 1) * TG], pb_sq[0:1, :], srow[:, tg * TG:(tg + 1) * TG], ALU.add),
                       reads=[r_psq, r_rows], writes=[r_rows], psum=True)
            P.emit("sync", lambda e, o=o: e.dma_start(z1_sp[o, :, :], z1_buf[:, :]),
                   reads=[r_z1], writes=[r_z1_sp[o]], is_dma=True, tag=f"z1sp{o}")

        P.emit("vector", lambda e: e.tensor_scalar_mul(mrow[:, :], mrow[:, :], 1.0 / D), reads=[r_rows], writes=[r_rows])
        P.emit("vector", lambda e: e.tensor_mul(arow[:, :], mrow[:, :], mrow[:, :]), reads=[r_rows], writes=[r_rows])
        P.emit("vector", lambda e: e.scalar_tensor_tensor(srow[:, :], srow[:, :], 1.0 / D, arow[:, :], ALU.mult, ALU.subtract),
               reads=[r_rows], writes=[r_rows])
        P.emit("vector", lambda e: e.tensor_scalar_add(srow[:, :], srow[:, :], 1e-5), reads=[r_rows], writes=[r_rows])
        P.emit("scalar", lambda e: e.activation(arow[:, :], srow[:, :], AF.Ln),
               reads=[r_rows], writes=[r_rows])
        P.emit("scalar", lambda e: e.activation(arow[:, :], arow[:, :], AF.Exp, scale=-0.5),
               reads=[r_rows], writes=[r_rows])
        P.emit("vector", lambda e: e.scalar_tensor_tensor(srow[:, :], mrow[:, :], -1.0, arow[:, :], ALU.mult, ALU.mult),
               reads=[r_rows], writes=[r_rows])

        a_bc = phc.enter_context(nc.sbuf_tensor("a_bc", [128, HALF], F32))
        b_bc = phc.enter_context(nc.sbuf_tensor("b_bc", [128, HALF], F32))
        r_abc = Res("abc")
        for th in range(2):
            for tgl in range(4):
                tg = th * 4 + tgl
                P.emit("tensor", lambda e, tg=tg: e.matmul(
                    pb_a[:, :], ones1_f[:, :], arow[:, tg * TG:(tg + 1) * TG], start=True, stop=True),
                       reads=[r_const, r_rows, r_pa], writes=[r_pa], tag=f"ba{tg}")
                P.emit("tensor", lambda e, tg=tg: e.matmul(
                    pb_b[:, :], ones1_f[:, :], srow[:, tg * TG:(tg + 1) * TG], start=True, stop=True),
                       reads=[r_const, r_rows, r_pb], writes=[r_pb], tag=f"bb{tg}")
                P.emit("scalar", lambda e, tgl=tgl: e.activation(a_bc[:, tgl * TG:(tgl + 1) * TG], pb_a[:, :], AF.Copy),
                       reads=[r_pa], writes=[r_abc], psum=True)
                P.emit("scalar", lambda e, tgl=tgl: e.activation(b_bc[:, tgl * TG:(tgl + 1) * TG], pb_b[:, :], AF.Copy),
                       reads=[r_pb], writes=[r_abc], psum=True)
            for o in range(HC):
                P.emit("sync", lambda e, o=o, th=th: e.dma_start(
                    z1_buf[:, 0:HALF], z1_sp[o, :, th * HALF:(th + 1) * HALF]),
                       reads=[r_z1_sp[o]], writes=[r_z1], extra=[r_z1b.w, r_sq.w], is_dma=True, tag=f"z1rl{o}_{th}")
                for tgl in range(4):
                    P.emit("vector", lambda e, tgl=tgl: e.tensor_tensor(
                        y1f_buf[:, tgl * TG:(tgl + 1) * TG], z1_buf[:, tgl * TG:(tgl + 1) * TG],
                        a_bc[:, tgl * TG:(tgl + 1) * TG], ALU.mult),
                           reads=[r_z1, r_abc], writes=[r_y1f])
                    P.emit("vector", lambda e, tgl=tgl: e.tensor_tensor(
                        y1f_buf[:, tgl * TG:(tgl + 1) * TG], y1f_buf[:, tgl * TG:(tgl + 1) * TG],
                        b_bc[:, tgl * TG:(tgl + 1) * TG], ALU.add),
                           reads=[r_abc], writes=[r_y1f])
                    P.emit("vector", lambda e, tgl=tgl: e.tensor_copy(
                        y1b_buf[:, tgl * TG:(tgl + 1) * TG], y1f_buf[:, tgl * TG:(tgl + 1) * TG]),
                           reads=[r_y1f], writes=[r_y1b])
                P.emit("sync", lambda e, o=o, th=th: e.dma_start(
                    y1f_sp[o, :, th * HALF:(th + 1) * HALF], y1f_buf[:, :]),
                       reads=[r_y1f], writes=[r_y1f_sp[o]], is_dma=True)
                P.emit("sync", lambda e, o=o, th=th: e.dma_start(
                    y1b_sp[o, :, th * HALF:(th + 1) * HALF], y1b_buf[:, :]),
                       reads=[r_y1b], writes=[r_y1b_sp[o]], is_dma=True)

    stack_bc.close()
    P.barrier()
    gb_ready = Res("gb_ready")

    def emit_half(hf):
        P.barrier()
        with contextlib.ExitStack() as phd:
            sbd = lambda name, shape, dt: phd.enter_context(nc.sbuf_tensor(f"{name}_h{hf}", shape, dt))
            y1h = sbd("y1h", [128, HC * HALF], BF16)
            accum = sbd("accum", [128, HC * HALF], F32)
            wg_bf = [sbd(f"wg_bf{i}", [128, HC * 512], BF16) for i in range(2)]
            wu_bf = [sbd(f"wu_bf{i}", [128, HC * 512], BF16) for i in range(2)]
            wd_bf = [sbd(f"wd_bf{i}", [128, 4 * D], BF16) for i in range(2)]
            c_sb = [sbd(f"c_sb{i}", [128, TG], F32) for i in range(2)]

            pb_gu = [phd.enter_context(nc.psum_tensor(f"pgu{i}_h{hf}", [128, TG], F32)) for i in range(4)]
            pb_dn = [phd.enter_context(nc.psum_tensor(f"pdn{i}_h{hf}", [128, TG], F32)) for i in range(3)]
            pb_mc = phd.enter_context(nc.psum_tensor(f"pmc_h{hf}", [128, TG], F32))
            scopeRE = phd.enter_context(contextlib.ExitStack())
            sbr = lambda name, shape, dt: scopeRE.enter_context(nc.sbuf_tensor(f"{name}_h{hf}", shape, dt))
            combine_fm = sbr("combine_fm", [NE, HALF], BF16)
            c_row = [sbr(f"c_row{i}", [1, HALF], BF16) for i in range(2)]
            guc = [sbr(f"guc{i}", [128, 4 * HALF], BF16) for i in range(2)]
            sg_t = [sbr("sg0", [128, TG], F32)]
            uc_t = [sbr("uc0", [128, TG], F32)]
            NSLOT = 1
            scopeR = contextlib.ExitStack()
            sbr2 = lambda name, shape, dt: scopeR.enter_context(nc.sbuf_tensor(f"{name}_h{hf}", shape, dt))
            gw_bf = sbr2("gw_bf", [128, HC * NE], BF16)
            logits_f = sbr2("logits_f", [32, TG], F32)
            scores_all = sbr2("scores_all", [128, 16 * NE], F32)
            scb_all = sbr2("scb_all", [128, 16 * NE], F32)
            gsc_all = sbr2("gsc_all", [128, 128], F32)
            gtmp = [sbr2(f"gtmp{i}", [128, 128], F32) for i in range(3)]
            m8g = gtmp[0]
            m8m = gtmp[1]
            masked_all = scb_all
            sel_all = scb_all
            w_all = scores_all
            comb_all = w_all
            thr8 = sbr2("thr8", [128, 16], F32)
            ws_r = sbr2("ws_r", [128, 16], F32)
            wr_r = sbr2("wr_r", [128, 16], F32)

            r_y1h = Res("y1h"); r_acc = Res("acc")
            r_guc = [Res(f"guc{i}") for i in range(2)]
            r_wg = [Res(f"wg{i}") for i in range(2)]; r_wu = [Res(f"wu{i}") for i in range(2)]
            r_wd = [Res(f"wd{i}") for i in range(2)]
            r_csb = Res("csb"); r_sg = [Res("sg0")]; r_uct = [Res("uct0")]
            r_gw = Res("gw"); r_cfm = Res("cfm"); r_rout = Res("rout")
            r_pgu = [Res(f"pgu{i}") for i in range(4)]; r_pdn = [Res(f"pdn{i}") for i in range(3)]
            r_pmc = Res("pmc")
            r_y1ft = Res("y1ft"); r_z2b = Res("z2b"); r_sq2 = Res("sq2"); r_outt = Res("outt")
            r_rows2 = Res("rows2")

            ts = slice(hf * HALF, (hf + 1) * HALF)
            for o in range(HC):
                P.emit("sync", lambda e, o=o: e.dma_start(
                    y1h[:, o * HALF:(o + 1) * HALF], y1b_sp[o, :, ts]),
                       reads=[r_y1b_sp[o]], writes=[r_y1h], is_dma=True, tag=f"y1h{o}")
            P.emit("gpsimd", lambda e: e.dma_start(gw_bf[:, :].rearrange("p (c n) -> p c n", c=HC),
                                                   gwT.rearrange("(c p) n -> p c n", p=128)),
                   writes=[r_gw], is_dma=True)
            P.emit("vector", lambda e: e.memset(accum[:, :], 0.0), writes=[r_acc])
            if hf == 0:
                P.emit("tensor", lambda e: e.matmul(pb_mc[:, 0:NE], ones1_f[:, :], gb_row[:, :], start=True, stop=True),
                       reads=[r_const, r_pmc], writes=[r_pmc], tag="gbb")
                P.emit("vector", lambda e: e.tensor_copy(gb_b[:, :], pb_mc[:, 0:NE]),
                       reads=[r_pmc], writes=[gb_ready], psum=True)
                P.emit("vector", lambda e: e.tensor_copy(gsc_all[:, 0:NE], gb_b[:, :]), reads=[gb_ready])
                P.emit("vector", lambda e: e.tensor_copy(gsc_all[:, 0:NE], gb_b[:, :]), reads=[gb_ready])

            r_lg = Res("lg")
            for tg in range(4):
                for h in range(HC):
                    P.emit("tensor", lambda e, tg=tg, h=h: e.matmul(
                        pb_mc[0:32, :], gw_bf[:, h * NE:(h + 1) * NE],
                        y1h[:, h * HALF + tg * TG:h * HALF + (tg + 1) * TG],
                        start=(h == 0), stop=(h == HC - 1)),
                           reads=[r_gw, r_y1h],
                           writes=[r_pmc] if h in (0, HC - 1) else [], tag=f"lg{tg}_{h}")
                P.emit("scalar", lambda e: e.activation(
                    logits_f[:, :], pb_mc[0:32, :], AF.Copy),
                       reads=[r_pmc], writes=[r_lg], psum=True)
                for j in range(4):
                    tc = tg * 4 + j
                    P.emit("tensor", lambda e, tc=tc, j=j: e.transpose(
                        pb_dn[2][:, tc * 32:(tc + 1) * 32], logits_f[0:32, j * 128:(j + 1) * 128], id_f32[0:32, 0:32]),
                           reads=[r_lg],
                           writes=[r_pdn[2]] if tc in (0, 15) else [], tag=f"ltr{tc}")
            P.emit("scalar", lambda e: e.activation(scores_all[:, :], pb_dn[2][:, 0:512], AF.Sigmoid),
                   reads=[r_pdn[2]], writes=[r_rout], psum=True)
            if dbg and hf == 0:
                P.emit("sync", lambda e: e.dma_start(rt_dbg[0], scores_all[:, :]), reads=[r_rout], is_dma=True)
            v16e = lambda t_: t_[:, :].rearrange("p (a b) -> p a b", a=16)
            v16g4 = lambda t_: t_[:, :].rearrange("p (a g j) -> p a g j", a=16, j=4)
            v16x8 = lambda t_: t_[:, :].rearrange("p (a g) -> p a g", a=16)
            P.emit("vector", lambda e: e.tensor_tensor(
                v16e(scb_all), v16e(scores_all),
                bass.AP(gb_b, 0, [[NE, 128], [0, 16], [1, NE]]), ALU.add),
                   reads=[r_rout, gb_ready], writes=[r_rout])
            if dbg and hf == 0:
                P.emit("sync", lambda e: e.dma_start(rt_dbg[1], scb_all[:, :]), reads=[r_rout], is_dma=True)
            def Xj(j):
                return bass.AP(scb_all, j, [[512, 128], [NE, 16], [4, 8]])
            g3 = lambda t_: t_[:, :].rearrange("p (a g) -> p a g", a=16)
            P.emit("vector", lambda e: e.tensor_tensor(g3(gtmp[1]), Xj(0), Xj(1), ALU.min), reads=[r_rout], writes=[r_rout])
            P.emit("vector", lambda e: e.tensor_tensor(g3(gtmp[2]), Xj(2), Xj(3), ALU.min), reads=[r_rout], writes=[r_rout])
            P.emit("vector", lambda e: e.tensor_tensor(gtmp[1][:, :], gtmp[1][:, :], gtmp[2][:, :], ALU.max), reads=[r_rout], writes=[r_rout])
            P.emit("vector", lambda e: e.tensor_tensor(g3(gtmp[2]), Xj(0), Xj(1), ALU.max), reads=[r_rout], writes=[r_rout])
            P.emit("vector", lambda e: e.tensor_tensor(g3(gtmp[0]), Xj(2), Xj(3), ALU.max), reads=[r_rout], writes=[r_rout])
            P.emit("vector", lambda e: e.tensor_tensor(gsc_all[:, :], gtmp[2][:, :], gtmp[0][:, :], ALU.max), reads=[r_rout], writes=[r_rout])
            P.emit("vector", lambda e: e.tensor_tensor(gtmp[2][:, :], gtmp[2][:, :], gtmp[0][:, :], ALU.min), reads=[r_rout], writes=[r_rout])
            P.emit("vector", lambda e: e.tensor_tensor(gtmp[2][:, :], gtmp[2][:, :], gtmp[1][:, :], ALU.max), reads=[r_rout], writes=[r_rout])
            P.emit("vector", lambda e: e.tensor_tensor(gsc_all[:, :], gsc_all[:, :], gtmp[2][:, :], ALU.add), reads=[r_rout], writes=[r_rout])
            for tc in range(16):
                P.emit("vector", lambda e, tc=tc: e.max(m8g[:, tc * 8:(tc + 1) * 8], gsc_all[:, tc * 8:(tc + 1) * 8]),
                       reads=[r_rout], writes=[r_rout])
            if dbg and hf == 0:
                P.emit("sync", lambda e: e.dma_start(rt_dbg[2][:, 0:128], gsc_all[:, :]), reads=[r_rout], is_dma=True)
                P.emit("sync", lambda e: e.dma_start(rt_dbg[2][:, 128:256], m8g[:, :]), reads=[r_rout], is_dma=True)
            P.emit("vector", lambda e: e.tensor_copy(gtmp[2][:, 0:1], m8g[:, 0:1]), reads=[r_rout], writes=[r_rout])
            P.emit("vector", lambda e: e.tensor_copy(gtmp[2][:, 0:1], m8g[:, 0:1]), reads=[r_rout], writes=[r_rout])
            P.emit("vector", lambda e: e.tensor_tensor(
                v16x8(gsc_all), v16x8(gsc_all), bass.AP(m8g, 3, [[128, 128], [8, 16], [0, 8]]), ALU.is_ge),
                   reads=[r_rout], writes=[r_rout])
            P.emit("vector", lambda e: e.tensor_tensor(
                v16g4(masked_all), v16g4(scb_all),
                bass.AP(gsc_all, 0, [[128, 128], [8, 16], [1, 8], [0, 4]]), ALU.mult),
                   reads=[r_rout], writes=[r_rout])
            if dbg and hf == 0:
                P.emit("sync", lambda e: e.dma_start(rt_dbg[3], masked_all[:, :]), reads=[r_rout], is_dma=True)
                P.emit("sync", lambda e: e.dma_start(rt_dbg[4][:, 0:128], gsc_all[:, :]), reads=[r_rout], is_dma=True)
            for tc in range(16):
                P.emit("vector", lambda e, tc=tc: e.max(m8m[:, tc * 8:(tc + 1) * 8], masked_all[:, tc * NE:(tc + 1) * NE]),
                       reads=[r_rout], writes=[r_rout])
            P.emit("vector", lambda e: e.tensor_copy(gtmp[2][:, 0:8], m8m[:, 120:128]), reads=[r_rout], writes=[r_rout])
            P.emit("vector", lambda e: e.tensor_copy(gtmp[2][:, 0:8], m8m[:, 120:128]), reads=[r_rout], writes=[r_rout])
            P.emit("vector", lambda e: e.tensor_scalar_max(thr8[:, :], bass.AP(m8m, 7, [[128, 128], [8, 16]]), 1e-30),
                   reads=[r_rout], writes=[r_rout])
            P.emit("vector", lambda e: e.tensor_copy(gtmp[2][:, 0:16], thr8[:, :]), reads=[r_rout], writes=[r_rout])
            P.emit("vector", lambda e: e.tensor_copy(gtmp[2][:, 0:16], thr8[:, :]), reads=[r_rout], writes=[r_rout])
            P.emit("vector", lambda e: e.tensor_tensor(
                v16e(sel_all), v16e(masked_all), bass.AP(thr8, 0, [[16, 128], [1, 16], [0, NE]]), ALU.is_ge),
                   reads=[r_rout], writes=[r_rout])
            if dbg and hf == 0:
                P.emit("sync", lambda e: e.dma_start(rt_dbg[6], sel_all[:, :]), reads=[r_rout], is_dma=True)
                P.emit("sync", lambda e: e.dma_start(rt_dbg[4][:, 128:144], thr8[:, :]), reads=[r_rout], is_dma=True)
            P.emit("vector", lambda e: e.tensor_mul(w_all[:, :], sel_all[:, :], scores_all[:, :]),
                   reads=[r_rout], writes=[r_rout])
            if dbg and hf == 0:
                P.emit("sync", lambda e: e.dma_start(rt_dbg[7], w_all[:, :]), reads=[r_rout], is_dma=True)
            P.emit("vector", lambda e: e.reduce_sum(
                ws_r[:, :], bass.AP(w_all, 0, [[512, 128], [NE, 16], [1, NE]]), axis=AX.X),
                   reads=[r_rout], writes=[r_rout])
            if dbg and hf == 0:
                P.emit("sync", lambda e: e.dma_start(rt_dbg[4][:, 176:192], ws_r[:, :]), reads=[r_rout], is_dma=True)
            P.emit("vector", lambda e: e.tensor_scalar_add(ws_r[:, :], ws_r[:, :], 1e-20), reads=[r_rout], writes=[r_rout])
            if dbg and hf == 0:
                P.emit("sync", lambda e: e.dma_start(rt_dbg[4][:, 192:208], ws_r[:, :]), reads=[r_rout], is_dma=True)
            P.emit("vector", lambda e: e.reciprocal(wr_r[:, :], ws_r[:, :]), reads=[r_rout], writes=[r_rout])
            if dbg and hf == 0:
                P.emit("sync", lambda e: e.dma_start(rt_dbg[4][:, 208:224], wr_r[:, :]), reads=[r_rout], is_dma=True)
            P.emit("vector", lambda e: e.tensor_copy(gtmp[2][:, 0:16], ws_r[:, :]), reads=[r_rout], writes=[r_rout])
            P.emit("vector", lambda e: e.tensor_copy(gtmp[2][:, 0:16], ws_r[:, :]), reads=[r_rout], writes=[r_rout])
            P.emit("vector", lambda e: e.tensor_scalar_mul(wr_r[:, :], wr_r[:, :], SCALE_MOE), reads=[r_rout], writes=[r_rout])
            if dbg and hf == 0:
                P.emit("sync", lambda e: e.dma_start(rt_dbg[4][:, 144:160], ws_r[:, :]), reads=[r_rout], is_dma=True)
                P.emit("sync", lambda e: e.dma_start(rt_dbg[4][:, 160:176], wr_r[:, :]), reads=[r_rout], is_dma=True)
            P.emit("vector", lambda e: e.tensor_copy(gtmp[1][:, 0:16], wr_r[:, :]), reads=[r_rout], writes=[r_rout])
            P.emit("vector", lambda e: e.tensor_copy(gtmp[1][:, 0:16], wr_r[:, :]), reads=[r_rout], writes=[r_rout])
            P.emit("vector", lambda e: e.tensor_tensor(
                v16e(comb_all), v16e(w_all), bass.AP(wr_r, 0, [[16, 128], [1, 16], [0, NE]]), ALU.mult),
                   reads=[r_rout], writes=[r_rout])
            if dbg and hf == 0:
                P.emit("sync", lambda e: e.dma_start(rt_dbg[5], comb_all[:, :]), reads=[r_rout], is_dma=True)
            for tcg in range(4):
                for j in range(4):
                    tc = tcg * 4 + j
                    P.emit("tensor", lambda e, tc=tc, j=j: e.transpose(
                        pb_mc[0:32, j * 128:(j + 1) * 128], comb_all[:, tc * NE:(tc + 1) * NE], id_f32[:, :]),
                           reads=[r_rout],
                           writes=[r_pmc] if j in (0, 3) else [], tag=f"ctr{tc}")
                P.emit("vector", lambda e, tcg=tcg: e.tensor_copy(
                    combine_fm[:, tcg * TG:(tcg + 1) * TG], pb_mc[0:32, :]),
                       reads=[r_pmc], writes=[r_cfm], psum=True)
            if dbg:
                P.emit("gpsimd", lambda e, hf=hf: e.dma_start(comb_dbg[hf, :, :], combine_fm[:, :]),
                       reads=[r_cfm], is_dma=True)
            scopeR.close()
            P.barrier()

            NEXP = NE + 2

            r_crow = [Res(f"crow{i}") for i in range(2)]
            r_cring = [Res(f"cring{i}") for i in range(2)]

            def emit_crow_dma(e_idx):
                if 0 <= e_idx < NE:
                    P.emit("sync", lambda e, e_idx=e_idx: e.dma_start(
                        c_row[e_idx % 2][:, :], combine_fm[e_idx:e_idx + 1, :]),
                           reads=[r_cfm], writes=[r_crow[e_idx % 2]], is_dma=True, tag=f"crow{e_idx}")

            def emit_cb_tg(e_idx, tg):
                P.emit("tensor", lambda e, e_idx=e_idx, tg=tg: e.matmul(
                    pb_mc[:, :], ones1_bf[:, :], c_row[e_idx % 2][0:1, tg * TG:(tg + 1) * TG],
                    start=True, stop=True),
                       reads=[r_const, r_crow[e_idx % 2], r_pmc], writes=[r_pmc], tag=f"cb{e_idx}_{tg}")
                P.emit("scalar", lambda e, tg=tg: e.activation(
                    c_sb[tg % 2][:, :], pb_mc[:, :], AF.Copy),
                       reads=[r_pmc], writes=[r_cring[tg % 2]], psum=True)

            def emit_wgu_dma(e_idx):
                b = e_idx % 2
                if e_idx < NE:
                    srcs = [(wgT[e_idx], wg_bf[b], r_wg[b]), (wuT[e_idx], wu_bf[b], r_wu[b])]
                else:
                    k = e_idx - NE
                    srcs = [(shgT[:, k * 512:(k + 1) * 512], wg_bf[b], r_wg[b]),
                            (shuT[:, k * 512:(k + 1) * 512], wu_bf[b], r_wu[b])]
                for src, dst, rr in srcs:
                    P.emit("gpsimd", lambda e, src=src, dst=dst: e.dma_start(
                        dst[:, :].rearrange("p (c i) -> p c i", c=HC),
                        src.rearrange("(c p) i -> p c i", p=128)),
                           writes=[rr], is_dma=True, tag=f"wgu{e_idx}")

            def emit_wd_dma(e_idx):
                if e_idx < NE:
                    wdsrc = wdT[e_idx]
                else:
                    k = e_idx - NE
                    wdsrc = shdT[k * 512:(k + 1) * 512, :]
                b = e_idx % 2
                P.emit("gpsimd", lambda e, wdsrc=wdsrc, b=b: e.dma_start(
                    wd_bf[b][:, :].rearrange("p (c o) -> p c o", c=4),
                    wdsrc.rearrange("(c p) o -> p c o", p=128)),
                       writes=[r_wd[b]], is_dma=True, tag=f"wd{e_idx}")

            def emit_gu(e_idx):
                b = e_idx % 2
                emit_crow_dma(e_idx + 1)
                shared = e_idx >= NE
                for tg in range(4):
                    if not shared:
                        emit_cb_tg(e_idx, tg)
                    for i in range(4):
                        pslot = (tg * 4 + i) % 2
                        slot = 0
                        bg, bu = pb_gu[2 * pslot], pb_gu[2 * pslot + 1]
                        rg, ru = r_pgu[2 * pslot], r_pgu[2 * pslot + 1]
                        for h in range(HC):
                            P.emit("tensor", lambda e, b=b, tg=tg, i=i, h=h, bg=bg: e.matmul(
                                bg[:, :], wg_bf[b][:, h * 512 + i * 128:h * 512 + (i + 1) * 128],
                                y1h[:, h * HALF + tg * TG:h * HALF + (tg + 1) * TG],
                                start=(h == 0), stop=(h == HC - 1)),
                                   reads=[r_wg[b], r_y1h],
                                   writes=[rg] if h in (0, HC - 1) else [], tag=f"g{e_idx}_{tg}_{i}")
                        for h in range(HC):
                            P.emit("tensor", lambda e, b=b, tg=tg, i=i, h=h, bu=bu: e.matmul(
                                bu[:, :], wu_bf[b][:, h * 512 + i * 128:h * 512 + (i + 1) * 128],
                                y1h[:, h * HALF + tg * TG:h * HALF + (tg + 1) * TG],
                                start=(h == 0), stop=(h == HC - 1)),
                                   reads=[r_wu[b], r_y1h],
                                   writes=[ru] if h in (0, HC - 1) else [], tag=f"u{e_idx}_{tg}_{i}")
                        P.emit("scalar", lambda e, slot=slot, bg=bg: e.activation(
                            sg_t[slot][:, :], bg[:, :], AF.Silu),
                               reads=[rg], writes=[r_sg[slot]], psum=True, tag=f"sg{e_idx}_{tg}_{i}")
                        if shared:
                            P.emit("vector", lambda e, slot=slot, bu=bu: e.tensor_copy(uc_t[slot][:, :], bu[:, :]),
                                   reads=[ru], writes=[r_uct[slot]], psum=True)
                        else:
                            P.emit("vector", lambda e, slot=slot, bu=bu, tg=tg: e.tensor_tensor(
                                uc_t[slot][:, :], bu[:, :], c_sb[tg % 2][:, :], ALU.mult),
                                   reads=[ru, r_cring[tg % 2]], writes=[r_uct[slot]], psum=True)
                        P.emit("vector", lambda e, slot=slot, b=b, i=i, tg=tg: e.tensor_mul(
                            guc[b][:, i * HALF + tg * TG:i * HALF + (tg + 1) * TG],
                            sg_t[slot][:, :], uc_t[slot][:, :]),
                               reads=[r_sg[slot], r_uct[slot]], writes=[r_guc[b]], tag=f"guc{e_idx}_{tg}_{i}")

            def emit_down(e_idx):
                b = e_idx % 2
                for ho in range(HC):
                    for tg in range(4):
                        slot = (ho * 4 + tg) % 3
                        bd, rd = pb_dn[slot], r_pdn[slot]
                        for i in range(4):
                            P.emit("tensor", lambda e, ho=ho, tg=tg, i=i, bd=bd, b=b: e.matmul(
                                bd[:, :], wd_bf[b][:, i * D + ho * 128:i * D + (ho + 1) * 128],
                                guc[b][:, i * HALF + tg * TG:i * HALF + (tg + 1) * TG],
                                start=(i == 0), stop=(i == 3)),
                                   reads=[r_wd[b], r_guc[b]],
                                   writes=[rd] if i in (0, 3) else [], tag=f"d{e_idx}_{ho}_{tg}_{i}")
                        P.emit("vector", lambda e, ho=ho, tg=tg, bd=bd: e.tensor_tensor(
                            accum[:, ho * HALF + tg * TG:ho * HALF + (tg + 1) * TG], bd[:, :],
                            accum[:, ho * HALF + tg * TG:ho * HALF + (tg + 1) * TG], ALU.add),
                               reads=[rd], writes=[r_acc], psum=True, tag=f"da{e_idx}_{ho}_{tg}")

            emit_wgu_dma(0)
            emit_wd_dma(0)
            emit_wgu_dma(1)
            emit_wd_dma(1)
            emit_crow_dma(0)
            emit_gu(0)
            for e_idx in range(1, NEXP):
                if e_idx + 1 < NEXP:
                    emit_wgu_dma(e_idx + 1)
                emit_gu(e_idx)
                emit_down(e_idx - 1)
                if e_idx + 1 < NEXP:
                    emit_wd_dma(e_idx + 1)
            emit_down(NEXP - 1)
            if dbg:
                for o in range(HC):
                    P.emit("sync", lambda e, hf=hf, o=o: e.dma_start(
                        acc_dbg[hf, o, :, :], accum[:, o * HALF:(o + 1) * HALF]),
                           reads=[r_acc], is_dma=True)

            scopeRE.close()
            P.barrier()
            y1f_t = sbd("y1f_t", [128, HALF], F32)
            z2b_t = sbd("z2b_t", [128, TG], BF16)
            sq2_t = sbd("sq2_t", [128, TG], BF16)
            out_t = sbd("out_t", [128, TG], F32)
            m2row = sbd("m2row", [1, HALF], F32)
            s2row = sbd("s2row", [1, HALF], F32)
            a2row = sbd("a2row", [1, HALF], F32)

            zb_t = [z2b_t, sbd("z2b_t2", [128, TG], BF16)]
            sqb_t = [sq2_t, sbd("sq2_t2", [128, TG], BF16)]
            r_zb = [r_z2b, Res("z2b2")]
            r_sqb = [r_sq2, Res("sq22")]
            for o in range(HC):
                P.emit("sync", lambda e, o=o: e.dma_start(y1f_t[:, :], y1f_sp[o, :, ts]),
                       reads=[r_y1f_sp[o]], writes=[r_y1ft], is_dma=True, tag=f"y1f{o}")
                P.emit("vector", lambda e, o=o: e.tensor_tensor(
                    accum[:, o * HALF:(o + 1) * HALF], accum[:, o * HALF:(o + 1) * HALF], y1f_t[:, :], ALU.add),
                       reads=[r_y1ft], writes=[r_acc])
                for tg in range(4):
                    sl = (o * 4 + tg) % 2
                    P.emit("vector", lambda e, o=o, tg=tg, sl=sl: e.tensor_copy(
                        zb_t[sl][:, :], accum[:, o * HALF + tg * TG:o * HALF + (tg + 1) * TG]),
                           reads=[r_acc], writes=[r_zb[sl]])
                    P.emit("scalar", lambda e, o=o, tg=tg, sl=sl: e.activation(
                        sqb_t[sl][:, :], accum[:, o * HALF + tg * TG:o * HALF + (tg + 1) * TG], AF.Square),
                           reads=[r_acc], writes=[r_sqb[sl]])
                    P.emit("tensor", lambda e, tg=tg, o=o, sl=sl: e.matmul(
                        pb_gu[tg][0:1, :], onesc_bf[:, :], zb_t[sl][:, :], start=(o == 0), stop=(o == HC - 1)),
                           reads=[r_const, r_zb[sl]],
                           writes=[r_pgu[tg]] if o in (0, HC - 1) else [], tag=f"m2_{o}_{tg}")
                    bsq = pb_dn[tg] if tg < 3 else pb_mc
                    rsq = r_pdn[tg] if tg < 3 else r_pmc
                    P.emit("tensor", lambda e, tg=tg, o=o, sl=sl, bsq=bsq: e.matmul(
                        bsq[0:1, :], onesc_bf[:, :], sqb_t[sl][:, :], start=(o == 0), stop=(o == HC - 1)),
                           reads=[r_const, r_sqb[sl]],
                           writes=[rsq] if o in (0, HC - 1) else [], tag=f"s2_{o}_{tg}")
            for tg in range(4):
                P.emit("vector", lambda e, tg=tg: e.tensor_copy(m2row[:, tg * TG:(tg + 1) * TG], pb_gu[tg][0:1, :]),
                       reads=[r_pgu[tg]], writes=[r_rows2], psum=True)
                bsq = pb_dn[tg] if tg < 3 else pb_mc
                rsq = r_pdn[tg] if tg < 3 else r_pmc
                P.emit("vector", lambda e, tg=tg, bsq=bsq: e.tensor_copy(s2row[:, tg * TG:(tg + 1) * TG], bsq[0:1, :]),
                       reads=[rsq], writes=[r_rows2], psum=True)
            P.emit("vector", lambda e: e.tensor_scalar_mul(m2row[:, :], m2row[:, :], 1.0 / D), reads=[r_rows2], writes=[r_rows2])
            P.emit("vector", lambda e: e.tensor_mul(a2row[:, :], m2row[:, :], m2row[:, :]), reads=[r_rows2], writes=[r_rows2])
            P.emit("vector", lambda e: e.scalar_tensor_tensor(s2row[:, :], s2row[:, :], 1.0 / D, a2row[:, :], ALU.mult, ALU.subtract),
                   reads=[r_rows2], writes=[r_rows2])
            P.emit("vector", lambda e: e.tensor_scalar_add(s2row[:, :], s2row[:, :], 1e-5), reads=[r_rows2], writes=[r_rows2])
            P.emit("scalar", lambda e: e.activation(a2row[:, :], s2row[:, :], AF.Ln),
                   reads=[r_rows2], writes=[r_rows2])
            P.emit("scalar", lambda e: e.activation(a2row[:, :], a2row[:, :], AF.Exp, scale=-0.5),
                   reads=[r_rows2], writes=[r_rows2])
            P.emit("vector", lambda e: e.scalar_tensor_tensor(s2row[:, :], m2row[:, :], -1.0, a2row[:, :], ALU.mult, ALU.mult),
                   reads=[r_rows2], writes=[r_rows2])
            a2_bc = sbd("a2_bc", [128, HALF], F32)
            b2_bc = sbd("b2_bc", [128, HALF], F32)
            r_abc2 = Res("abc2")
            out_ts = [out_t, sbd("out_t2", [128, TG], F32)]
            r_outs = [r_outt, Res("outt2")]
            for tg in range(4):
                P.emit("tensor", lambda e, tg=tg: e.matmul(
                    pb_gu[0][:, :], ones1_f[:, :], a2row[:, tg * TG:(tg + 1) * TG], start=True, stop=True),
                       reads=[r_const, r_rows2, r_pgu[0]], writes=[r_pgu[0]], tag=f"ba2{tg}")
                P.emit("tensor", lambda e, tg=tg: e.matmul(
                    pb_gu[1][:, :], ones1_f[:, :], s2row[:, tg * TG:(tg + 1) * TG], start=True, stop=True),
                       reads=[r_const, r_rows2, r_pgu[1]], writes=[r_pgu[1]], tag=f"bb2{tg}")
                P.emit("scalar", lambda e, tg=tg: e.activation(a2_bc[:, tg * TG:(tg + 1) * TG], pb_gu[0][:, :], AF.Copy),
                       reads=[r_pgu[0]], writes=[r_abc2], psum=True)
                P.emit("scalar", lambda e, tg=tg: e.activation(b2_bc[:, tg * TG:(tg + 1) * TG], pb_gu[1][:, :], AF.Copy),
                       reads=[r_pgu[1]], writes=[r_abc2], psum=True)
            for o in range(HC):
                for tg in range(4):
                    sl = (o * 4 + tg) % 2
                    P.emit("vector", lambda e, o=o, tg=tg, sl=sl: e.tensor_tensor(
                        out_ts[sl][:, :], accum[:, o * HALF + tg * TG:o * HALF + (tg + 1) * TG],
                        a2_bc[:, tg * TG:(tg + 1) * TG], ALU.mult),
                           reads=[r_acc, r_abc2], writes=[r_outs[sl]])
                    P.emit("vector", lambda e, tg=tg, sl=sl: e.tensor_tensor(
                        out_ts[sl][:, :], out_ts[sl][:, :], b2_bc[:, tg * TG:(tg + 1) * TG], ALU.add),
                           reads=[r_abc2], writes=[r_outs[sl]])
                    P.emit("sync", lambda e, o=o, tg=tg, hf=hf, sl=sl: e.dma_start(
                        outT[o * 128:(o + 1) * 128, hf * HALF + tg * TG:hf * HALF + (tg + 1) * TG], out_ts[sl][:, :]),
                           reads=[r_outs[sl]], is_dma=True, tag=f"out{o}_{tg}")

    for hf in range(2):
        emit_half(hf)

    if trunc is not None:
        P.ops = P.ops[:trunc]
    P.finalize(nc, stack)
    nc._stack = stack
    return nc


def kernel(L0_attn_x, L0_attn_w_qkv, L0_attn_w_proj, L0_ffn_gate_w, L0_ffn_gate_bias,
           L0_ffn_w_gate, L0_ffn_w_up, L0_ffn_w_down, L0_ffn_sh_gate, L0_ffn_sh_up,
           L0_ffn_sh_down):
    import ml_dtypes
    x = np.asarray(L0_attn_x, dtype=np.float32)
    common = {
        "wqkv": np.ascontiguousarray(np.asarray(L0_attn_w_qkv, dtype=np.float32)),
        "wproj": np.ascontiguousarray(np.asarray(L0_attn_w_proj, dtype=np.float32)),
        "gwT": np.ascontiguousarray(np.asarray(L0_ffn_gate_w, dtype=np.float32).T),
        "gbias": np.ascontiguousarray(np.asarray(L0_ffn_gate_bias, dtype=np.float32).reshape(1, NE)),
        "wgT": np.ascontiguousarray(np.asarray(L0_ffn_w_gate, dtype=np.float32).transpose(0, 2, 1)),
        "wuT": np.ascontiguousarray(np.asarray(L0_ffn_w_up, dtype=np.float32).transpose(0, 2, 1)),
        "wdT": np.ascontiguousarray(np.asarray(L0_ffn_w_down, dtype=np.float32).transpose(0, 2, 1)),
        "shgT": np.ascontiguousarray(np.asarray(L0_ffn_sh_gate, dtype=np.float32).T),
        "shuT": np.ascontiguousarray(np.asarray(L0_ffn_sh_up, dtype=np.float32).T),
        "shdT": np.ascontiguousarray(np.asarray(L0_ffn_sh_down, dtype=np.float32).T),
        "ident_bf": np.eye(128, dtype=ml_dtypes.bfloat16),
        "ident_f32": np.eye(128, dtype=np.float32),
    }
    in_maps = [dict(common, xT=np.ascontiguousarray(x[c].T)) for c in range(8)]
    nc = build_nc()
    res = run_bass_kernel_spmd(nc, in_maps, core_ids=list(range(8)))
    out = np.stack([np.asarray(r["outT"]).T for r in res.results])
    return np.ascontiguousarray(out.astype(np.float32))
